# version 11
# baseline (speedup 1.0000x reference)
import math
from contextlib import ExitStack
from types import SimpleNamespace

import numpy as np
import ml_dtypes

import concourse.bass as bass
import concourse.mybir as mybir
from concourse.bass_utils import run_bass_kernel_spmd

F32 = mybir.dt.float32
BF16 = mybir.dt.bfloat16
I32 = mybir.dt.int32
AF = mybir.ActivationFunctionType
ALU = mybir.AluOpType
AX = mybir.AxisListType

S = 2048
D = 4096
NH, NKV, HD = 32, 8, 128
FQKV = 6144
NT = S // 128
ND = D // 128
IFF = 11008
NIB = IFF // 128
SUPS = [8] * 10 + [6]
EXP_SHIFT = -4.0
SM_SCALE = 1.0 / math.sqrt(HD)
BPF32 = 512
C_ONE, C_ZERO, C_EPS, C_NHALF, C_SHIFT, C_SMS = range(6)


class _DryInst:
    def then_inc(self, sem, k):
        return self


class _DryEng:
    def wait_ge(self, *a, **k):
        return None

    def __getattr__(self, name):
        return lambda *a, **k: _DryInst()


class _Dummy:
    def __getitem__(self, *_):
        return self

    def rearrange(self, *a, **k):
        return self


class Marks(dict):
    lenient = True

    def __setitem__(self, k, v):
        if not self.lenient and k in self and self[k] != v:
            raise AssertionError(f"mark mismatch {k}: {self[k]} vs {v}")
        dict.__setitem__(self, k, v)

    def __missing__(self, k):
        if self.lenient:
            return None
        raise KeyError(k)


def build_nc(upto="full"):
    nc = bass.Bass("TRN2", target_bir_lowering=False)

    x_d = nc.dram_tensor("x", [S, D], F32, kind="ExternalInput").ap()
    wqkv_d = nc.dram_tensor("wqkv", [D, FQKV], F32, kind="ExternalInput").ap()
    wo_d = nc.dram_tensor("wo", [D, D], F32, kind="ExternalInput").ap()
    qg_d = nc.dram_tensor("qg", [IFF, D], I32, kind="ExternalInput").ap()
    qu_d = nc.dram_tensor("qu", [IFF, D], I32, kind="ExternalInput").ap()
    qd_d = nc.dram_tensor("qd", [D, IFF], I32, kind="ExternalInput").ap()
    sg_d = nc.dram_tensor("sg", [IFF, 32], F32, kind="ExternalInput").ap()
    zg_d = nc.dram_tensor("zg", [IFF, 32], F32, kind="ExternalInput").ap()
    su_d = nc.dram_tensor("su", [IFF, 32], F32, kind="ExternalInput").ap()
    zu_d = nc.dram_tensor("zu", [IFF, 32], F32, kind="ExternalInput").ap()
    sd_d = nc.dram_tensor("sd", [D, NIB], F32, kind="ExternalInput").ap()
    zd_d = nc.dram_tensor("zd", [D, NIB], F32, kind="ExternalInput").ap()
    cosp_d = nc.dram_tensor("cosp", [128, S], BF16, kind="ExternalInput").ap()
    sinp_d = nc.dram_tensor("sinp", [128, S], BF16, kind="ExternalInput").ap()
    pswap_d = nc.dram_tensor("pswap", [128, 128], BF16, kind="ExternalInput").ap()
    ident_d = nc.dram_tensor("ident", [128, 128], BF16, kind="ExternalInput").ap()
    maskb_d = nc.dram_tensor("maskb", [128, 128], BF16, kind="ExternalInput").ap()
    y_d = nc.dram_tensor("y", [S, D], F32, kind="ExternalOutput").ap()
    qsp_d = nc.dram_tensor("qsp", [NH, 128, S], BF16).ap()
    ksp_d = nc.dram_tensor("ksp", [NKV, 128, S], BF16).ap()
    vsp_d = nc.dram_tensor("vsp", [NT, 128, NKV * 128], BF16).ap()
    x1sp_d = nc.dram_tensor("x1sp", [S, D], F32).ap()
    hsp_d = nc.dram_tensor("hsp", [NIB, 128, S], BF16).ap()
    tap_d = None
    if upto in ("p0", "p2", "p3b"):
        tap_d = nc.dram_tensor("tap", [128, ND * S], BF16, kind="ExternalOutput").ap()

    wq3 = wqkv_d.rearrange("(j p) c -> p j c", p=128)
    wo3 = wo_d.rearrange("(j p) c -> p j c", p=128)
    sg3 = sg_d.rearrange("(a p) g -> p a g", p=128)
    zg3 = zg_d.rearrange("(a p) g -> p a g", p=128)
    su3 = su_d.rearrange("(a p) g -> p a g", p=128)
    zu3 = zu_d.rearrange("(a p) g -> p a g", p=128)
    sd3 = sd_d.rearrange("(a p) g -> p a g", p=128)
    zd3 = zd_d.rearrange("(a p) g -> p a g", p=128)

    st0 = ExitStack()
    E0 = st0.enter_context
    psA = E0(nc.psum_tensor("psA", [128, 8 * BPF32], F32))
    consts = E0(nc.sbuf_tensor("consts", [128, 8], F32))
    onesc = E0(nc.sbuf_tensor("onesc", [128, 1], BF16))
    onesr = E0(nc.sbuf_tensor("onesr", [1, 128], BF16))
    onesr32 = E0(nc.sbuf_tensor("onesr32", [1, 128], F32))
    pswap = E0(nc.sbuf_tensor("pswap_s", [128, 128], BF16))
    ident = E0(nc.sbuf_tensor("ident_s", [128, 128], BF16))
    maskb = E0(nc.sbuf_tensor("maskb_s", [128, 128], BF16))

    REG = []

    class Sem:
        def __init__(self, name):
            self.h = st0.enter_context(nc.semaphore(name))
            self.n = 0
            self.idx = len(REG)
            REG.append(self)

        def tick(self, inst, k=1):
            if not isinstance(inst, _DryInst):
                inst.then_inc(self.h, k)
            self.n += k
            return (self.idx, self.n)

        def cur(self):
            return (self.idx, self.n)

    def bankS(name, n):
        return [Sem(f"{name}{i}") for i in range(n)]

    s_pe = Sem("s_pe")
    s_dve = Sem("s_dve")
    s_act = Sem("s_act")
    sm_tbl = bankS("m_tbl", 3)
    sm_x = bankS("m_x", 2)
    sm_cs = bankS("m_cs", 2)
    sm_wp = bankS("m_wp", 8)
    sm_rout = bankS("m_rout", 2)
    sm_vst = bankS("m_vst", 4)
    sm_qh = bankS("m_qh", 2)
    sm_kg = bankS("m_kg", 2)
    sm_vg = bankS("m_vg", 4)
    sm_wos = bankS("m_wos", 8)
    sm_xst = bankS("m_xst", 4)
    sm_x1w = bankS("m_x1w", 2)
    sm_qgu = bankS("m_qgu", 6)
    sm_scl = bankS("m_scl", 8)
    sm_hw = bankS("m_hw", 4)
    sm_sdz = bankS("m_sdz", 8)
    sm_hb = bankS("m_hb", 4)
    sm_qdb = bankS("m_qdb", 8)
    sm_x1b = bankS("m_x1b", 8)
    sm_out = bankS("m_out", 4)
    sm_tap = bankS("m_tap", 2)

    M = Marks()

    def bank(b, lo=0, hi=BPF32, p0=0, p1=128):
        assert 0 <= b < 8 and 0 <= lo < hi <= BPF32
        return psA[p0:p1, b * BPF32 + lo: b * BPF32 + hi]

    def cst(i):
        return consts[:, i:i + 1]

    def _aslist(mv):
        return mv if isinstance(mv, list) else [mv]

    def WV(eng, mv):
        if mv is None:
            return
        for si, v in _aslist(mv):
            eng.wait_ge(REG[si].h, v)

    def W(eng, key):
        WV(eng, dict.get(M, key))

    def WT(eng, key):
        mv = dict.get(M, key)
        if mv is None:
            assert M.lenient, f"missing required mark {key}"
            return
        WV(eng, mv)

    def phases(dry, ctx):
        for s in REG:
            s.n = 0

        def phase(allocfn, bodies):
            if dry:
                bufs = allocfn(None)
                for _, fn in bodies:
                    fn(_DryEng(), bufs)
            else:
                with ExitStack() as stL:
                    bufs = allocfn(stL)
                    with nc.Block() as blk:
                        for name, fn in bodies:
                            getattr(blk, name)(
                                lambda e, fn=fn, bufs=bufs: fn(e, bufs))

        def mk(stL):
            if stL is None:
                return lambda n, sh, dt: _Dummy()
            return lambda n, sh, dt: stL.enter_context(nc.sbuf_tensor(n, sh, dt))

        bigT = ctx["bigT"]

        def ln_alloc(tag):
            def f(stL):
                m = mk(stL)
                return SimpleNamespace(
                    xrow=[m(f"{tag}_xr{i}", [128, D], F32) for i in range(2)],
                    xn=[m(f"{tag}_xn{i}", [128, D], BF16) for i in range(2)],
                    trash=m(f"{tag}_tr", [128, D], F32),
                    cols=m(f"{tag}_cols", [128, 20], F32),
                )
            return f

        def ln_bodies(tag, src_d, first=False, pre_sy=None, pre_gp=None):
            def b_sync(sy, u):
                if pre_sy:
                    pre_sy(sy)
                for tt in range(NT):
                    W(sy, (tag, "xn", tt - 2))
                    M[(tag, "xld", tt)] = sm_x[tt % 2].tick(
                        sy.dma_start(u.xrow[tt % 2][:],
                                     src_d[tt * 128:(tt + 1) * 128, :]), 16)

            def b_gp(g, u):
                if pre_gp:
                    pre_gp(g)

            def b_vec(v, u):
                cols = u.cols
                if first:
                    for i, val in ((C_ONE, 1.0), (C_ZERO, 0.0), (C_EPS, 1e-5),
                                   (C_NHALF, -0.5), (C_SHIFT, EXP_SHIFT),
                                   (C_SMS, SM_SCALE)):
                        v.memset(consts[:, i:i + 1], val)
                    v.memset(onesc[:], 1.0)
                    v.memset(onesr[:], 1.0)
                    M["consts"] = s_dve.tick(v.memset(onesr32[:], 1.0))
                for tt in range(NT):
                    b = tt % 2
                    o = 10 * b
                    WT(v, (tag, "xld", tt))
                    W(v, (tag, "xn", tt - 2))
                    m1 = s_dve.tick(v.reduce_sum(cols[:, o:o + 1], u.xrow[b][:], axis=AX.X))
                    WV(v, m1)
                    s_dve.tick(v.tensor_scalar_mul(cols[:, o + 2:o + 3], cols[:, o:o + 1], 1.0 / D))
                    WT(v, (tag, "sq", tt))
                    m4 = s_dve.tick(v.tensor_scalar_mul(cols[:, o + 3:o + 4], cols[:, o + 1:o + 2], 1.0 / D))
                    WV(v, m4)
                    m5 = s_dve.tick(v.tensor_mul(cols[:, o + 4:o + 5], cols[:, o + 2:o + 3], cols[:, o + 2:o + 3]))
                    WV(v, m5)
                    M[(tag, "var", tt)] = s_dve.tick(
                        v.tensor_sub(cols[:, o + 5:o + 6], cols[:, o + 3:o + 4], cols[:, o + 4:o + 5]))
                    WT(v, (tag, "rstd", tt))
                    M[(tag, "negmr", tt)] = s_dve.tick(v.scalar_tensor_tensor(
                        cols[:, o + 8:o + 9], cols[:, o + 2:o + 3], -1.0,
                        cols[:, o + 7:o + 8], ALU.mult, ALU.mult))
                    for jb in range(8):
                        WT(v, (tag, "tp", tt, jb))
                        src = psA[:, (6 + jb % 2) * BPF32:(7 + jb % 2) * BPF32]
                        M[(tag, "ev", tt, jb)] = s_dve.tick(v.tensor_copy(
                            bigT[:, 4 * jb:4 * jb + 4, tt * 128:(tt + 1) * 128],
                            src.rearrange("p (a c) -> p a c", a=4)))

            def b_scal(sc, u):
                cols = u.cols
                if first:
                    WT(sc, "consts")
                for tt in range(NT):
                    b = tt % 2
                    o = 10 * b
                    WT(sc, (tag, "xld", tt))
                    W(sc, (tag, "xn", tt - 2))
                    M[(tag, "sq", tt)] = s_act.tick(sc.activation(
                        u.trash[:], u.xrow[b][:], AF.Square,
                        bias=cst(C_ZERO), scale=cst(C_ONE),
                        accum_out=cols[:, o + 1:o + 2]))
                    WT(sc, (tag, "var", tt))
                    a1 = s_act.tick(sc.activation(cols[:, o + 6:o + 7], cols[:, o + 5:o + 6],
                                                  AF.Ln, bias=cst(C_EPS), scale=cst(C_ONE)))
                    WV(sc, a1)
                    M[(tag, "rstd", tt)] = s_act.tick(sc.activation(
                        cols[:, o + 7:o + 8], cols[:, o + 6:o + 7],
                        AF.Exp, bias=cst(C_ZERO), scale=cst(C_NHALF)))
                    WT(sc, (tag, "negmr", tt))
                    for q in range(4):
                        M[(tag, "xnq", tt, q)] = s_act.tick(sc.activation(
                            u.xn[b][:, q * 1024:(q + 1) * 1024],
                            u.xrow[b][:, q * 1024:(q + 1) * 1024], AF.Identity,
                            bias=cols[:, o + 8:o + 9], scale=cols[:, o + 7:o + 8]))
                    M[(tag, "xn", tt)] = s_act.cur()

            def b_tens(t, u):
                if first:
                    WT(t, "consts")
                    WT(t, "tbl")
                for tt in range(NT):
                    b = tt % 2
                    for jb in range(8):
                        WT(t, (tag, "xnq", tt, jb // 2))
                        bk = 6 + jb % 2
                        if jb >= 2:
                            W(t, (tag, "ev", tt, jb - 2))
                        else:
                            W(t, (tag, "ev", tt - 1, jb + 6))
                        ins = None
                        for u4 in range(4):
                            j = 4 * jb + u4
                            ins = t.matmul(bank(bk, u4 * 128, (u4 + 1) * 128),
                                           u.xn[b][:, j * 128:(j + 1) * 128],
                                           ident[:], start=True, stop=True)
                        M[(tag, "tp", tt, jb)] = s_pe.tick(ins)

            return [("sync", b_sync), ("gpsimd", b_gp), ("vector", b_vec),
                    ("scalar", b_scal), ("tensor", b_tens)]

        def p0_gp(g):
            M["tbl"] = [
                sm_tbl[0].tick(g.dma_start(pswap[:], pswap_d), 16),
                sm_tbl[1].tick(g.dma_start(ident[:], ident_d), 16),
                sm_tbl[2].tick(g.dma_start(maskb[:], maskb_d), 16),
            ]

        def emit_tap():
            def tap_body(sy, u):
                fins = []
                for j in range(ND):
                    if j >= 2:
                        WV(sy, fins[j - 2])
                    fins.append(sm_tap[j % 2].tick(
                        sy.dma_start(tap_d[:, j * S:(j + 1) * S], bigT[:, j, :]), 16))
                WV(sy, fins[-2])
                WV(sy, fins[-1])

            phase(lambda stL: None, [("sync", tap_body)])

        def emit_drain():
            def drain_body(sy, u):
                WV(sy, [s.cur() for s in REG])

            phase(lambda stL: None, [("sync", drain_body)])

        phase(ln_alloc("ln1"), ln_bodies("ln1", x_d, first=True, pre_gp=p0_gp))
        if upto == "p0":
            emit_tap()
            emit_drain()
            return

        NPAIR = FQKV // 256

        def p1_alloc(stL):
            m = mk(stL)
            return SimpleNamespace(
                wpair=[m(f"wp{i}", [128, ND, 256], BF16) for i in range(2)],
                qpre=[m(f"qpre{i}", [128, S], BF16) for i in range(2)],
                rtA=[m(f"rtA{i}", [128, S], BF16) for i in range(2)],
                rtB=[m(f"rtB{i}", [128, S], BF16) for i in range(2)],
                rout=[m(f"rout{i}", [128, S], BF16) for i in range(2)],
                vst=[m(f"vst{i}", [128, 256], BF16) for i in range(4)],
                cosp=m("cosp_s", [128, S], BF16),
                sinp=m("sinp_s", [128, S], BF16),
            )

        def p1_sync(sy, u):
            M["cos"] = [
                sm_cs[0].tick(sy.dma_start(u.cosp[:], cosp_d), 16),
                sm_cs[1].tick(sy.dma_start(u.sinp[:], sinp_d), 16),
            ]

        def p1_gp(g, u):
            for fp in range(NPAIR):
                W(g, ("p1pe", fp - 2))
                M[("wp", fp)] = [
                    sm_wp[(fp % 2) * 4 + jg].tick(g.dma_start(
                        u.wpair[fp % 2][:, 8 * jg:8 * jg + 8, :],
                        wq3[:, 8 * jg:8 * jg + 8, 256 * fp:256 * (fp + 1)]), 16)
                    for jg in range(4)]

        def p1_tens(t, u):
            WT(t, "cos")
            for fp in range(NPAIR):
                WT(t, ("wp", fp))
                if fp < 20:
                    for sub in range(2):
                        ft = 2 * fp + sub
                        grp = ft % 2
                        W(t, ("p1ropeA", ft - 2))
                        ins = None
                        for j in range(ND):
                            for ch in range(4):
                                ins = t.matmul(
                                    bank(grp * 4 + ch),
                                    u.wpair[fp % 2][:, j, sub * 128:(sub + 1) * 128],
                                    bigT[:, j, ch * 512:(ch + 1) * 512],
                                    start=(j == 0), stop=(j == ND - 1))
                        M[("p1qpre", ft)] = s_pe.tick(ins)
                        ins = None
                        for ch in range(4):
                            WT(t, ("p1evacC", ft, ch))
                            ins = t.matmul(bank(grp * 4 + ch), pswap[:],
                                           u.qpre[ft % 2][:, ch * 512:(ch + 1) * 512],
                                           start=True, stop=True)
                        M[("p1swap", ft)] = s_pe.tick(ins)
                    M[("p1pe", fp)] = s_pe.cur()
                else:
                    if fp == 20:
                        WT(t, ("p1ropeA", 38))
                    for tt in range(NT):
                        slot = tt % 4
                        if tt >= 4:
                            W(t, ("p1vev", fp, tt - 4))
                        elif fp > 20:
                            W(t, ("p1vev", fp - 1, tt + 12))
                        ins = None
                        for j in range(ND):
                            ins = t.matmul(bank(slot, 0, 256),
                                           bigT[:, j, tt * 128:(tt + 1) * 128],
                                           u.wpair[fp % 2][:, j, :],
                                           start=(j == 0), stop=(j == ND - 1))
                        M[("p1v", fp, tt)] = s_pe.tick(ins)
                    M[("p1pe", fp)] = s_pe.cur()

        def p1_vec(v, u):
            for fp in range(NPAIR):
                if fp < 20:
                    for sub in range(2):
                        ft = 2 * fp + sub
                        grp = ft % 2
                        reg = psA[:, grp * 4 * BPF32:(grp * 4 + 4) * BPF32]
                        WT(v, ("p1qpre", ft))
                        W(v, ("p1ropeB", ft - 2))
                        for ch in range(4):
                            M[("p1evacC", ft, ch)] = s_dve.tick(v.tensor_copy(
                                u.qpre[ft % 2][:, ch * 512:(ch + 1) * 512],
                                psA[:, (grp * 4 + ch) * BPF32:(grp * 4 + ch + 1) * BPF32]))
                        M[("p1evac", ft)] = s_dve.cur()
                        WT(v, ("p1swap", ft))
                        W(v, ("p1spill", ft - 2))
                        M[("p1ropeA", ft)] = s_dve.tick(
                            v.tensor_mul(u.rtA[ft % 2][:], reg, u.sinp[:]))
                        mB = s_dve.tick(v.tensor_mul(u.rtB[ft % 2][:], u.qpre[ft % 2][:], u.cosp[:]))
                        M[("p1ropeB", ft)] = mB
                        WV(v, mB)
                        M[("p1rope", ft)] = s_dve.tick(
                            v.tensor_add(u.rout[ft % 2][:], u.rtA[ft % 2][:], u.rtB[ft % 2][:]))
                else:
                    for tt in range(NT):
                        slot = tt % 4
                        WT(v, ("p1v", fp, tt))
                        if tt >= 4:
                            W(v, ("p1vsp", fp, tt - 4))
                        elif fp > 20:
                            W(v, ("p1vsp", fp - 1, tt + 12))
                        M[("p1vev", fp, tt)] = s_dve.tick(
                            v.tensor_copy(u.vst[slot][:], bank(slot, 0, 256)))

        def p1_scal(sc, u):
            for fp in range(NPAIR):
                if fp < 20:
                    for sub in range(2):
                        ft = 2 * fp + sub
                        WT(sc, ("p1rope", ft))
                        dst = qsp_d[ft] if ft < NH else ksp_d[ft - NH]
                        M[("p1spill", ft)] = sm_rout[ft % 2].tick(
                            sc.dma_start(dst, u.rout[ft % 2][:]), 16)
                else:
                    for tt in range(NT):
                        WT(sc, ("p1vev", fp, tt))
                        M[("p1vsp", fp, tt)] = sm_vst[tt % 4].tick(sc.dma_start(
                            vsp_d[tt][:, (fp - 20) * 256:(fp - 19) * 256],
                            u.vst[tt % 4][:]), 16)
            M["p1_spills"] = ([s.cur() for s in sm_rout] + [s.cur() for s in sm_vst])
            WV(sc, M["p1_spills"])

        phase(p1_alloc, [("sync", p1_sync), ("gpsimd", p1_gp), ("tensor", p1_tens),
                         ("vector", p1_vec), ("scalar", p1_scal)])
        if upto == "p1":
            emit_drain()
            return

        def a_alloc(stL):
            m = mk(stL)
            return SimpleNamespace(
                qh=[m(f"qh{i}", [128, S], BF16) for i in range(2)],
                kg=[m(f"kg{i}", [128, S], BF16) for i in range(2)],
                vg=[m(f"vg{i}", [128, NT, 128], BF16) for i in range(2)],
                expS=[m(f"expS{i}", [128, NT, 512], BF16) for i in range(2)],
                btmp=[m(f"btmp{i}", [128, 512], F32) for i in range(2)],
                rcp=[m(f"rcp{i}", [1, 512], F32) for i in range(2)],
                mtmp=[m(f"mtmp{i}", [128, 128], BF16) for i in range(4)],
            )

        def tiles_of(c):
            return list(range(4 * c + 4))

        def a_sync(sy, u):
            for h in range(NH):
                WT(sy, ("p1spill", h))
                W(sy, ("a_hdone", h - 2))
                M[("a_ql", h)] = sm_qh[h % 2].tick(
                    sy.dma_start(u.qh[h % 2][:], qsp_d[h]), 16)

        def a_gp_ld(g_, u):
            for g in range(NKV):
                WT(g_, ("p1spill", NH + g))
                fp = 20 + g // 2
                for tt in range(NT):
                    WT(g_, ("p1vsp", fp, tt))
                W(g_, ("a_gdone", g - 2))
                mv = [sm_kg[g % 2].tick(g_.dma_start(u.kg[g % 2][:], ksp_d[g]), 16)]
                for half in range(2):
                    mv.append(sm_vg[(g % 2) * 2 + half].tick(g_.dma_start(
                        u.vg[g % 2][:, 8 * half:8 * half + 8, :],
                        vsp_d[8 * half:8 * half + 8, :, g * 128:(g + 1) * 128]
                        .rearrange("t p c -> p t c")), 16))
                M[("a_kvl", g)] = mv

        gsidx = [0]

        def em_scores(t, u, h, c):
            g = h // 4
            for k in tiles_of(c):
                d = k - 4 * c
                n0 = 128 * max(d, 0)
                gi = gsidx[0]
                sbank = 4 + (gi % 2)
                W(t, ("a_edone_gi", gi - 2))
                ins = t.matmul(bank(sbank, n0, 512),
                               u.kg[g % 2][:, k * 128:(k + 1) * 128],
                               u.qh[h % 2][:, c * 512 + n0:(c + 1) * 512],
                               start=True, stop=True)
                M[("a_s", h, c, k)] = s_pe.tick(ins)
                M[("a_gi_of", h, c, k)] = gi
                gsidx[0] += 1

        def em_pv(t, u, h, c):
            g = h // 4
            cb = c % 2
            ks = tiles_of(c)
            gc = 4 * h + c
            ins = None
            for k in ks:
                d = k - 4 * c
                n0 = 128 * max(d, 0)
                WT(t, ("a_e", h, c, k))
                if d >= 0:
                    WT(t, ("a_mk", h, c, k))
                if k == 0:
                    W(t, ("a_onG", gc - 2))
                ins = t.matmul(bank(cb, n0, 512),
                               u.vg[g % 2][:, k, :],
                               u.expS[cb][:, k, n0:512],
                               start=(k == 0), stop=(k == ks[-1]))
            M[("a_pv", h, c)] = s_pe.tick(ins)
            ins = None
            for k in ks:
                d = k - 4 * c
                n0 = 128 * max(d, 0)
                if k == 0:
                    W(t, ("a_rcpG", gc - 2))
                ins = t.matmul(bank(2 + cb, n0, 512, p0=0, p1=1),
                               onesc[:], u.expS[cb][:, k, n0:512],
                               start=(k == 0), stop=(k == ks[-1]))
            M[("a_dnG", gc)] = s_pe.tick(ins)

        def em_bc(t, u, h, c):
            cb = c % 2
            gc = 4 * h + c
            WT(t, ("a_rcpG", gc))
            W(t, ("a_onG", gc - 2))
            M[("a_bcG", gc)] = s_pe.tick(t.matmul(
                bank(6 + cb), onesr32[:], u.rcp[cb][0:1, :], start=True, stop=True))

        def a_tens(t, u):
            for h in range(NH):
                g = h // 4
                WT(t, ("a_ql", h))
                if h % 4 == 0:
                    WT(t, ("a_kvl", g))
                if h > 0:
                    em_bc(t, u, h - 1, 2)
                em_scores(t, u, h, 0)
                em_scores(t, u, h, 1)
                if h > 0:
                    em_bc(t, u, h - 1, 3)
                em_pv(t, u, h, 0)
                em_scores(t, u, h, 2)
                em_bc(t, u, h, 0)
                em_pv(t, u, h, 1)
                em_scores(t, u, h, 3)
                em_bc(t, u, h, 1)
                em_pv(t, u, h, 2)
                em_pv(t, u, h, 3)
                M[("a_hdone", h)] = s_pe.cur()
                if h % 4 == 3:
                    M[("a_gdone", g)] = s_pe.cur()
            em_bc(t, u, NH - 1, 2)
            em_bc(t, u, NH - 1, 3)

        def a_act(sc, u):
            for h in range(NH):
                for c in range(4):
                    cb = c % 2
                    gc = 4 * h + c
                    W(sc, ("a_dnG", gc - 2))
                    for k in tiles_of(c):
                        d = k - 4 * c
                        n0 = 128 * max(d, 0)
                        gi = M[("a_gi_of", h, c, k)]
                        sbank = 4 + ((gi or 0) % 2)
                        WT(sc, ("a_s", h, c, k))
                        if d >= 0:
                            if gc > 0:
                                ph, pc = divmod(gc - 1, 4)
                                W(sc, ("a_mk", ph, pc, 4 * pc + d))
                            M[("a_em", h, c, k)] = s_act.tick(sc.activation(
                                u.mtmp[d][:], bank(sbank, n0, n0 + 128),
                                AF.Exp, bias=cst(C_SHIFT), scale=cst(C_SMS)))
                            if n0 + 128 < 512:
                                M[("a_e", h, c, k)] = s_act.tick(sc.activation(
                                    u.expS[cb][:, k, n0 + 128:512],
                                    bank(sbank, n0 + 128, 512),
                                    AF.Exp, bias=cst(C_SHIFT), scale=cst(C_SMS)))
                            else:
                                M[("a_e", h, c, k)] = s_act.cur()
                        else:
                            M[("a_e", h, c, k)] = s_act.tick(sc.activation(
                                u.expS[cb][:, k, :], bank(sbank),
                                AF.Exp, bias=cst(C_SHIFT), scale=cst(C_SMS)))
                        M[("a_edone_gi", M[("a_gi_of", h, c, k)])] = s_act.cur()

        def a_vec(v, u):
            def masks(h, c):
                cb = c % 2
                for k in tiles_of(c):
                    d = k - 4 * c
                    if d >= 0:
                        n0 = 128 * d
                        WT(v, ("a_em", h, c, k))
                        M[("a_mk", h, c, k)] = s_dve.tick(v.tensor_mul(
                            u.expS[cb][:, k, n0:n0 + 128], u.mtmp[d][:], maskb[:]))

            def norm(h, c):
                cb = c % 2
                gc = 4 * h + c
                WT(v, ("a_dnG", gc))
                W(v, ("a_bcG", gc - 2))
                M[("a_rcpG", gc)] = s_dve.tick(
                    v.reciprocal(u.rcp[cb][0:1, :], bank(2 + cb, p0=0, p1=1)))
                WT(v, ("a_bcG", gc))
                W(v, ("a_onG", gc - 2))
                mcp = s_dve.tick(v.tensor_copy(u.btmp[cb][:], bank(6 + cb)))
                WV(v, mcp)
                M[("a_onG", gc)] = s_dve.tick(v.tensor_mul(
                    bigT[:, h, c * 512:(c + 1) * 512], bank(cb), u.btmp[cb][:]))

            for h in range(NH):
                masks(h, 0)
                if h > 0:
                    norm(h - 1, 3)
                masks(h, 1)
                norm(h, 0)
                masks(h, 2)
                norm(h, 1)
                masks(h, 3)
                norm(h, 2)
            norm(NH - 1, 3)

        phase(a_alloc, [("sync", a_sync), ("gpsimd", a_gp_ld),
                        ("tensor", a_tens), ("scalar", a_act), ("vector", a_vec)])
        if upto == "p2":
            emit_tap()
            emit_drain()
            return

        NDS = 16

        def wo_alloc(stL):
            m = mk(stL)
            return SimpleNamespace(
                wos=[m(f"wos{i}", [128, ND, 256], BF16) for i in range(2)],
                xst=[m(f"xst{i}", [128, 256], F32) for i in range(4)],
                x1st=[m(f"x1st{i}", [128, 256], F32) for i in range(2)],
            )

        def wo_gp(g, u):
            for ds in range(NDS):
                W(g, ("wo_pe", ds - 2))
                M[("wo_w", ds)] = [
                    sm_wos[(ds % 2) * 4 + jg].tick(g.dma_start(
                        u.wos[ds % 2][:, 8 * jg:8 * jg + 8, :],
                        wo3[:, 8 * jg:8 * jg + 8, 256 * ds:256 * (ds + 1)]), 16)
                    for jg in range(4)]

        def wo_sync(sy, u):
            for ds in range(NDS):
                for tt in range(NT):
                    i = ds * NT + tt
                    W(sy, ("wo_add", i - 4))
                    M[("wo_x", i)] = sm_xst[i % 4].tick(sy.dma_start(
                        u.xst[i % 4][:],
                        x_d[tt * 128:(tt + 1) * 128, 256 * ds:256 * (ds + 1)]), 16)

        def wo_tens(t, u):
            for ds in range(NDS):
                WT(t, ("wo_w", ds))
                for tt in range(NT):
                    i = ds * NT + tt
                    bk = tt % 2
                    W(t, ("wo_add", i - 2))
                    ins = None
                    for hh in range(NH):
                        ins = t.matmul(bank(bk, 0, 256),
                                       bigT[:, hh, tt * 128:(tt + 1) * 128],
                                       u.wos[ds % 2][:, hh, :],
                                       start=(hh == 0), stop=(hh == NH - 1))
                    M[("wo_mm", i)] = s_pe.tick(ins)
                M[("wo_pe", ds)] = s_pe.cur()

        def wo_vec(v, u):
            for ds in range(NDS):
                for tt in range(NT):
                    i = ds * NT + tt
                    bk = tt % 2
                    WT(v, ("wo_mm", i))
                    WT(v, ("wo_x", i))
                    W(v, ("wo_sp", i - 2))
                    M[("wo_add", i)] = s_dve.tick(v.tensor_add(
                        u.x1st[tt % 2][:], bank(bk, 0, 256), u.xst[i % 4][:]))

        def wo_scal(sc, u):
            for ds in range(NDS):
                for tt in range(NT):
                    i = ds * NT + tt
                    WT(sc, ("wo_add", i))
                    M[("wo_sp", i)] = sm_x1w[tt % 2].tick(sc.dma_start(
                        x1sp_d[tt * 128:(tt + 1) * 128, 256 * ds:256 * (ds + 1)],
                        u.x1st[tt % 2][:]), 16)
            M["x1_done"] = [s.cur() for s in sm_x1w]
            WV(sc, M["x1_done"])

        phase(wo_alloc, [("gpsimd", wo_gp), ("sync", wo_sync), ("tensor", wo_tens),
                         ("vector", wo_vec), ("scalar", wo_scal)])
        if upto == "p3a":
            emit_drain()
            return

        def p3b_sy(sy):
            WT(sy, "x1_done")

        phase(ln_alloc("ln2"), ln_bodies("ln2", x1sp_d, pre_sy=p3b_sy))
        if upto == "p3b":
            emit_tap()
            emit_drain()
            return

        NSC = (NIB + 3) // 4

        def p4_alloc(stL):
            m = mk(stL)
            return SimpleNamespace(
                qbg=[m(f"qbg{i}", [128, D], BF16) for i in range(3)],
                qbu=[m(f"qbu{i}", [128, D], BF16) for i in range(3)],
                wgT=m("wgT", [128, ND, 128], BF16),
                wuT=m("wuT", [128, ND, 128], BF16),
                sgb=[m(f"sgb{i}", [128, 4, 32], F32) for i in range(2)],
                zgb=[m(f"zgb{i}", [128, 4, 32], F32) for i in range(2)],
                sub=[m(f"sub{i}", [128, 4, 32], F32) for i in range(2)],
                zub=[m(f"zub{i}", [128, 4, 32], F32) for i in range(2)],
                nzg=[m(f"nzg{i}", [128, 4, 32], F32) for i in range(2)],
                nzu=[m(f"nzu{i}", [128, 4, 32], F32) for i in range(2)],
                silu=[m(f"silu{i}", [128, 512], BF16) for i in range(2)],
                hac=[m(f"hac{i}", [128, 1024], BF16) for i in range(2)],
            )

        def p4_sync(sy, u):
            def scl_load(sy, ac):
                if ac >= 2:
                    W(sy, ("p4dqa", 4 * (ac - 1) - 1))
                a0 = 4 * ac
                an = min(4, NIB - a0)
                M[("p4scl", ac)] = [
                    sm_scl[ai * 2 + ac % 2].tick(sy.dma_start(
                        dst[ac % 2][:, 0:an, :], src[:, a0:a0 + an, :]), 16)
                    for ai, (src, dst) in enumerate(
                        ((sg3, u.sgb), (zg3, u.zgb), (su3, u.sub), (zu3, u.zub)))]

            for a in range(NIB):
                if a == 0:
                    scl_load(sy, 0)
                    scl_load(sy, 1)
                elif (a + 2) % 4 == 0 and 2 <= (a + 2) // 4 < (NIB + 3) // 4:
                    scl_load(sy, (a + 2) // 4)
                for hf in range(2):
                    WT(sy, ("p4hh", a, hf))
                    M[("p4hspH", a, hf)] = sm_hw[(a % 2) * 2 + hf].tick(
                        sy.dma_start(hsp_d[a][:, hf * 1024:(hf + 1) * 1024],
                                     u.hac[a % 2][:]), 16)
            M["p4_spills"] = [s.cur() for s in sm_hw]
            WV(sy, M["p4_spills"])

        def p4_gp(g, u):
            for a in range(NIB):
                W(g, ("p4T", a - 3))
                M[("p4w", a)] = [
                    sm_qgu[0 + a % 3].tick(g.dma_start(
                        u.qbg[a % 3][:], qg_d[a * 128:(a + 1) * 128, :]), 16),
                    sm_qgu[3 + a % 3].tick(g.dma_start(
                        u.qbu[a % 3][:], qu_d[a * 128:(a + 1) * 128, :]), 16)]

        def p4_nz(v, u, a):
            ac = a // 4
            if a % 4 == 0:
                WT(v, ("p4scl", ac))
                if ac >= 2:
                    W(v, ("p4dqa", 4 * (ac - 1) - 1))
                s_dve.tick(v.scalar_tensor_tensor(
                    u.nzg[ac % 2][:], u.zgb[ac % 2][:], -1.0, u.sgb[ac % 2][:],
                    ALU.mult, ALU.mult))
                mz = s_dve.tick(v.scalar_tensor_tensor(
                    u.nzu[ac % 2][:], u.zub[ac % 2][:], -1.0, u.sub[ac % 2][:],
                    ALU.mult, ALU.mult))
                WV(v, mz)
                M[("p4nz", ac)] = mz

        def p4_dqa(sc, u, a, part):
            ac = a // 4
            js = list(range(ND))
            quarter = js[part * len(js) // 4:(part + 1) * len(js) // 4]
            if part == 0:
                WT(sc, ("p4w", a))
                WT(sc, ("p4nz", ac))
                W(sc, ("p4dqa", a - 1))
            ins = None
            for qb, sb, nzb in ((u.qbg, u.sgb, u.nzg), (u.qbu, u.sub, u.nzu)):
                for j in quarter:
                    ins = sc.activation(
                        qb[a % 3][:, j * 128:(j + 1) * 128],
                        qb[a % 3][:, j * 128:(j + 1) * 128],
                        AF.Identity, bias=nzb[ac % 2][:, a % 4, j:j + 1],
                        scale=sb[ac % 2][:, a % 4, j:j + 1])
            if part == 3:
                M[("p4dqa", a)] = s_act.tick(ins)
            else:
                s_act.tick(ins)

        def p4_vec(v, u):
            p4_nz(v, u, 0)
            for a in range(NIB):
                for tv in range(8):
                    WT(v, ("p4tp", a, 2 * tv))
                    WT(v, ("p4tp", a, 2 * tv + 1))
                    dstT = u.wgT if tv < 4 else u.wuT
                    j0 = (tv % 4) * 8
                    pb = 4 + 2 * (tv % 2)
                    src = psA[:, pb * BPF32:(pb + 2) * BPF32]
                    M[("p4tev2", a, tv)] = s_dve.tick(v.tensor_copy(
                        dstT[:, j0:j0 + 8, :], src.rearrange("p (q c) -> p q c", q=8)))
                M[("p4tevall", a)] = s_dve.cur()
                if a + 1 < NIB:
                    p4_nz(v, u, a + 1)
                for c in range(4):
                    WT(v, ("p4silu", a, c))
                    WT(v, ("p4u", a, c))
                    if c == 0:
                        W(v, ("p4hspH", a - 2, 1))
                    elif c == 2:
                        WT(v, ("p4hspH", a, 0))
                    M[("p4mul", a, c)] = s_dve.tick(v.tensor_mul(
                        u.hac[a % 2][:, (c % 2) * 512:(c % 2 + 1) * 512],
                        bank(2 + c % 2), u.silu[c % 2][:]))
                    if c % 2 == 1:
                        M[("p4hh", a, c // 2)] = s_dve.cur()

        def p4_act(sc, u):
            for part in range(4):
                p4_dqa(sc, u, 0, part)
            for a in range(NIB):
                for c in range(4):
                    if a + 1 < NIB:
                        p4_dqa(sc, u, a + 1, c)
                    WT(sc, ("p4g", a, c))
                    if c >= 2:
                        W(sc, ("p4mul", a, c - 2))
                    elif a > 0:
                        W(sc, ("p4mul", a - 1, c + 2))
                    M[("p4silu", a, c)] = s_act.tick(sc.activation(
                        u.silu[c % 2][:], bank(c % 2), AF.Silu,
                        bias=cst(C_ZERO), scale=cst(C_ONE)))

        def p4_tens(t, u):
            for a in range(NIB):
                WT(t, ("p4dqa", a))
                for gb in range(16):
                    if gb >= 4:
                        W(t, ("p4tev2", a, (gb - 4) // 2))
                    else:
                        W(t, ("p4tev2", a - 1, (gb + 12) // 2))
                    src = u.qbg if gb < 8 else u.qbu
                    j0 = (gb % 8) * 4
                    pb = 4 + 2 * ((gb // 2) % 2)
                    ins = None
                    for q4 in range(4):
                        j = j0 + q4
                        ins = t.matmul(bank(pb + gb % 2, q4 * 128, (q4 + 1) * 128),
                                       src[a % 3][:, j * 128:(j + 1) * 128],
                                       ident[:], start=True, stop=True)
                    M[("p4tp", a, gb)] = s_pe.tick(ins)
                M[("p4T", a)] = s_pe.cur()
                WT(t, ("p4tevall", a))
                for c in range(4):
                    if c >= 2:
                        W(t, ("p4silu", a, c - 2))
                    elif a > 0:
                        W(t, ("p4silu", a - 1, c + 2))
                    ins = None
                    for j in range(ND):
                        ins = t.matmul(bank(c % 2), u.wgT[:, j, :],
                                       bigT[:, j, c * 512:(c + 1) * 512],
                                       start=(j == 0), stop=(j == ND - 1))
                    M[("p4g", a, c)] = s_pe.tick(ins)
                    if c >= 2:
                        W(t, ("p4mul", a, c - 2))
                    elif a > 0:
                        W(t, ("p4mul", a - 1, c + 2))
                    ins = None
                    for j in range(ND):
                        ins = t.matmul(bank(2 + c % 2), u.wuT[:, j, :],
                                       bigT[:, j, c * 512:(c + 1) * 512],
                                       start=(j == 0), stop=(j == ND - 1))
                    M[("p4u", a, c)] = s_pe.tick(ins)

        phase(p4_alloc, [("sync", p4_sync), ("gpsimd", p4_gp), ("vector", p4_vec),
                         ("scalar", p4_act), ("tensor", p4_tens)])
        if upto == "p4":
            emit_drain()
            return

        ctx["close_big"]()

        NS = len(SUPS)
        SL = [(q4, s) for q4 in range(4) for s in range(NS)]
        NGB = [(ss + 3) // 4 for ss in SUPS]

        def p5_li(idx, hh):
            return idx * 8 + hh

        def p5_gidx(idx, hh, gb):
            q4, s = SL[idx]
            per = [8 * NGB[SL[i][1]] for i in range(len(SL))]
            return sum(per[:idx]) + hh * NGB[s] + gb

        _gcum = [0]
        for _i in range(len(SL)):
            _gcum.append(_gcum[-1] + 8 * NGB[SL[_i][1]])

        def p5_gidx(idx, hh, gb):
            return _gcum[idx] + hh * NGB[SL[idx][1]] + gb

        def p5_lin(idx, tt, ch):
            return (idx * NT + tt) * 2 + ch

        def p5_alloc(stL):
            m = mk(stL)
            return SimpleNamespace(
                y2=m("y2", [128, NT * 1024], F32),
                wdT=[m(f"wdT{i}", [128, 8, 1024], BF16) for i in range(2)],
                hbuf=[m(f"hbuf{i}", [128, 8, 512], BF16) for i in range(2)],
                qdb=[m(f"qdb{i}", [128, 1024], BF16) for i in range(8)],
                sdb=m("sdb", [128, 32, NIB], F32),
                zdb=m("zdb", [128, 32, NIB], F32),
                nzd=[m(f"nzd{i}", [128, 8, 8], F32) for i in range(2)],
                outst=[m(f"outst{i}", [128, 1024], F32) for i in range(4)],
                x1b=[m(f"x1b{i}", [128, 1024], F32) for i in range(8)],
            )

        def p5_sync(sy, u):
            WT(sy, "x1_done")
            WT(sy, "p4_spills")
            M["p5sdz"] = (
                [sm_sdz[sl].tick(sy.dma_start(
                    u.sdb[:, 8 * sl:8 * sl + 8, :], sd3[:, 8 * sl:8 * sl + 8, :]), 16)
                 for sl in range(4)] +
                [sm_sdz[4 + sl].tick(sy.dma_start(
                    u.zdb[:, 8 * sl:8 * sl + 8, :], zd3[:, 8 * sl:8 * sl + 8, :]), 16)
                 for sl in range(4)])
            for idx, (q4, s) in enumerate(SL):
                ss = SUPS[s]
                for tg in range(4):
                    li4 = idx * 4 + tg
                    W(sy, ("p5mmG", li4 - 2))
                    nh = (ss + 1) // 2
                    M[("p5h", idx, tg)] = [
                        sm_hb[(tg % 2) * 2 + 0].tick(sy.dma_start(
                            u.hbuf[tg % 2][:, 0:nh, :],
                            hsp_d[8 * s:8 * s + nh, :, tg * 512:(tg + 1) * 512]
                            .rearrange("a p c -> p a c")), 16),
                        sm_hb[(tg % 2) * 2 + 1].tick(sy.dma_start(
                            u.hbuf[tg % 2][:, nh:ss, :],
                            hsp_d[8 * s + nh:8 * s + ss, :, tg * 512:(tg + 1) * 512]
                            .rearrange("a p c -> p a c")), 16)]
                    if s == NS - 2 and tg < 2:
                        for tt in (4 * tg, 4 * tg + 1, 4 * tg + 2, 4 * tg + 3):
                            if q4 > 0:
                                W(sy, ("p5fin", q4 - 1, tt + 8))
                            M[("p5x1", q4, tt)] = sm_x1b[tt % 8].tick(sy.dma_start(
                                u.x1b[tt % 8][:],
                                x1sp_d[tt * 128:(tt + 1) * 128,
                                       q4 * 1024:(q4 + 1) * 1024]), 16)
                    if s == NS - 1 and tg < 2:
                        for tt in (8 + 4 * tg, 9 + 4 * tg, 10 + 4 * tg, 11 + 4 * tg):
                            W(sy, ("p5fin", q4, tt - 8))
                            M[("p5x1", q4, tt)] = sm_x1b[tt % 8].tick(sy.dma_start(
                                u.x1b[tt % 8][:],
                                x1sp_d[tt * 128:(tt + 1) * 128,
                                       q4 * 1024:(q4 + 1) * 1024]), 16)

        def p5_gp(g, u):
            for idx, (q4, s) in enumerate(SL):
                ss = SUPS[s]
                cw = ss * 128
                for hh in range(8):
                    li = p5_li(idx, hh)
                    W(g, ("p5tpA", li - 8))
                    r0 = q4 * 1024 + hh * 128
                    M[("p5w", li)] = sm_qdb[hh].tick(g.dma_start(
                        u.qdb[hh][:, 0:cw],
                        qd_d[r0:r0 + 128, 1024 * s:1024 * s + cw]), 16)

        def p5_dq(v, u, idx):
            q4, s = SL[idx]
            ss = SUPS[s]
            if idx >= 2:
                W(v, ("p5dqaA", idx - 2))
            mz = s_dve.tick(v.scalar_tensor_tensor(
                u.nzd[idx % 2][:, :, 0:ss],
                u.zdb[:, q4 * 8:q4 * 8 + 8, 8 * s:8 * s + ss], -1.0,
                u.sdb[:, q4 * 8:q4 * 8 + 8, 8 * s:8 * s + ss],
                ALU.mult, ALU.mult))
            WV(v, mz)
            M[("p5nz", idx)] = mz

        def p5_dqa(sc, u, idx):
            q4, s = SL[idx]
            ss = SUPS[s]
            WT(sc, ("p5nz", idx))
            for hh in range(8):
                li = p5_li(idx, hh)
                WT(sc, ("p5w", li))
                ai = q4 * 8 + hh
                ins = None
                for gg in range(ss):
                    ins = sc.activation(
                        u.qdb[hh][:, gg * 128:(gg + 1) * 128],
                        u.qdb[hh][:, gg * 128:(gg + 1) * 128],
                        AF.Identity,
                        bias=u.nzd[idx % 2][:, hh, gg:gg + 1],
                        scale=u.sdb[:, ai, 8 * s + gg:8 * s + gg + 1])
                M[("p5dqa", li)] = s_act.tick(ins)
            M[("p5dqaA", idx)] = s_act.cur()

        def p5_tev(v, u, idx):
            q4, s = SL[idx]
            ss = SUPS[s]
            for hh in range(8):
                for gb in range(NGB[s]):
                    WT(v, ("p5tp", idx, hh, gb))
                pb = 4 + 2 * (hh % 2)
                M[("p5tevHL", idx * 8 + hh)] = s_dve.tick(v.tensor_copy(
                    u.wdT[idx % 2][:, 0:ss, hh * 128:(hh + 1) * 128],
                    psA[:, pb * BPF32:pb * BPF32 + ss * 128]
                    .rearrange("p (q c) -> p q c", q=ss)))
            M[("p5tev", idx)] = s_dve.cur()

        def p5_vec(v, u):
            WT(v, "p5sdz")
            p5_dq(v, u, 0)
            p5_tev(v, u, 0)
            p5_dq(v, u, 1)
            for idx, (q4, s) in enumerate(SL):
                for tg in range(4):
                    for ti in range(4):
                        tt = 4 * tg + ti
                        pb = 2 * (tt % 2)
                        WT(v, ("p5mm", idx, tt, 0))
                        WT(v, ("p5mm", idx, tt, 1))
                        if s > 0:
                            W(v, ("p5y2T", idx - 1, tt))
                        elif q4 > 0:
                            W(v, ("p5fin", q4 - 1, tt))
                        dst = u.y2[:, tt * 1024:(tt + 1) * 1024]
                        src_ = psA[:, pb * BPF32:(pb + 2) * BPF32]
                        if s == 0:
                            ins = v.tensor_copy(dst, src_)
                        else:
                            ins = v.tensor_add(dst, src_, dst)
                        my = s_dve.tick(ins)
                        M[("p5y2T", idx, tt)] = my
                        M[("p5y2L", idx * NT + tt)] = my
                        if s == NS - 1:
                            WV(v, my)
                            WT(v, ("p5x1", q4, tt))
                            W(v, ("p5out", q4, tt - 4))
                            if tt < 4 and q4 > 0:
                                W(v, ("p5out", q4 - 1, tt + 12))
                            M[("p5fin", q4, tt)] = s_dve.tick(v.tensor_add(
                                u.outst[tt % 4][:], dst, u.x1b[tt % 8][:]))
                if idx + 1 < len(SL):
                    p5_tev(v, u, idx + 1)
                if idx + 2 < len(SL):
                    p5_dq(v, u, idx + 2)

        def p5_act(sc, u):
            p5_dqa(sc, u, 0)
            p5_dqa(sc, u, 1)
            for idx, (q4, s) in enumerate(SL):
                if s == NS - 1:
                    for tt in range(NT):
                        WT(sc, ("p5fin", q4, tt))
                        M[("p5out", q4, tt)] = sm_out[tt % 4].tick(sc.dma_start(
                            y_d[tt * 128:(tt + 1) * 128, q4 * 1024:(q4 + 1) * 1024],
                            u.outst[tt % 4][:]), 16)
                if idx + 2 < len(SL):
                    p5_dqa(sc, u, idx + 2)
            WV(sc, [s.cur() for s in sm_out])

        def p5_T(t, u, idx):
            q4, s = SL[idx]
            ss = SUPS[s]
            for hh in range(8):
                li = p5_li(idx, hh)
                WT(t, ("p5dqa", li))
                pb = 4 + 2 * (hh % 2)
                W(t, ("p5tevHL", idx * 8 + hh - 2))
                for gb in range(NGB[s]):
                    it0 = 4 * gb
                    itn = min(4, ss - it0)
                    ins = None
                    for q in range(itn):
                        it = it0 + q
                        ins = t.matmul(
                            bank(pb + gb, q * 128, (q + 1) * 128),
                            u.qdb[hh][:, it * 128:(it + 1) * 128],
                            ident[:], start=True, stop=True)
                    M[("p5tp", idx, hh, gb)] = s_pe.tick(ins)
                M[("p5tpA", li)] = s_pe.cur()

        def p5_tens(t, u):
            p5_T(t, u, 0)
            for idx, (q4, s) in enumerate(SL):
                ss = SUPS[s]
                WT(t, ("p5tev", idx))
                for tg in range(4):
                    li4 = idx * 4 + tg
                    WT(t, ("p5h", idx, tg))
                    for ti in range(4):
                        tt = 4 * tg + ti
                        for ch in range(2):
                            bk = 2 * (tt % 2) + ch
                            if ch == 0:
                                W(t, ("p5y2L", idx * NT + tt - 2))
                            ins = None
                            for it in range(ss):
                                ins = t.matmul(
                                    bank(bk),
                                    u.hbuf[tg % 2][:, it, ti * 128:(ti + 1) * 128],
                                    u.wdT[idx % 2][:, it, ch * 512:(ch + 1) * 512],
                                    start=(it == 0), stop=(it == ss - 1))
                            M[("p5mm", idx, tt, ch)] = s_pe.tick(ins)
                    M[("p5mmG", li4)] = s_pe.cur()
                if idx + 1 < len(SL):
                    p5_T(t, u, idx + 1)

        phase(p5_alloc, [("sync", p5_sync), ("gpsimd", p5_gp), ("vector", p5_vec),
                         ("scalar", p5_act), ("tensor", p5_tens)])
        emit_drain()

    ctx = {"bigT": _Dummy(), "close_big": lambda: None}
    M.lenient = True
    phases(True, ctx)
    M.lenient = False

    stBig = ExitStack()
    bigT_real = stBig.enter_context(nc.sbuf_tensor("bigT", [128, ND, S], BF16))
    ctx = {"bigT": bigT_real, "close_big": stBig.close}
    phases(False, ctx)
    try:
        stBig.close()
    except Exception:
        pass
    st0.close()
    return nc


def _tables():
    inv = 10000.0 ** (-(np.arange(64, dtype=np.float64) * 2.0 / 128.0))
    ang = np.arange(S, dtype=np.float64)[None, :] * inv[:, None]
    cosp = np.repeat(np.cos(ang), 2, axis=0)
    sinp2 = np.empty((128, S), dtype=np.float64)
    sinp2[0::2] = -np.sin(ang)
    sinp2[1::2] = np.sin(ang)
    pswap = np.zeros((128, 128), dtype=np.float32)
    for k in range(64):
        pswap[2 * k, 2 * k + 1] = 1.0
        pswap[2 * k + 1, 2 * k] = 1.0
    ident = np.eye(128, dtype=np.float32)
    mask = np.triu(np.ones((128, 128), dtype=np.float32))
    bf = ml_dtypes.bfloat16
    return (cosp.astype(bf), sinp2.astype(bf), pswap.astype(bf),
            ident.astype(bf), mask.astype(bf))


def make_inmaps(inputs):
    x = np.asarray(inputs["L0_attn_x"])
    cosp, sinp, pswap, ident, maskb = _tables()
    shared = {
        "wqkv": np.ascontiguousarray(np.asarray(inputs["L0_attn_w_qkv"])),
        "wo": np.ascontiguousarray(np.asarray(inputs["L0_attn_w_o"])),
        "qg": np.ascontiguousarray(np.asarray(inputs["L0_ffn_qw_gate"])),
        "qu": np.ascontiguousarray(np.asarray(inputs["L0_ffn_qw_up"])),
        "qd": np.ascontiguousarray(np.asarray(inputs["L0_ffn_qw_down"])),
        "sg": np.ascontiguousarray(np.asarray(inputs["L0_ffn_s_gate"])),
        "zg": np.ascontiguousarray(np.asarray(inputs["L0_ffn_z_gate"])),
        "su": np.ascontiguousarray(np.asarray(inputs["L0_ffn_s_up"])),
        "zu": np.ascontiguousarray(np.asarray(inputs["L0_ffn_z_up"])),
        "sd": np.ascontiguousarray(np.asarray(inputs["L0_ffn_s_down"])),
        "zd": np.ascontiguousarray(np.asarray(inputs["L0_ffn_z_down"])),
        "cosp": cosp, "sinp": sinp, "pswap": pswap,
        "ident": ident, "maskb": maskb,
    }
    return [dict(shared, x=np.ascontiguousarray(x[c])) for c in range(8)]


def kernel(**inputs):
    nc = build_nc("full")
    in_maps = make_inmaps(inputs)
    res = run_bass_kernel_spmd(nc, in_maps, core_ids=list(range(8)))
    return np.stack([np.asarray(res.results[c]["y"]) for c in range(8)], axis=0)


# version 12
# speedup vs baseline: 1.0061x; 1.0061x over previous
import math
from contextlib import ExitStack
from types import SimpleNamespace

import numpy as np
import ml_dtypes

import concourse.bass as bass
import concourse.mybir as mybir
from concourse.bass_utils import run_bass_kernel_spmd

F32 = mybir.dt.float32
BF16 = mybir.dt.bfloat16
I32 = mybir.dt.int32
AF = mybir.ActivationFunctionType
ALU = mybir.AluOpType
AX = mybir.AxisListType

S = 2048
D = 4096
NH, NKV, HD = 32, 8, 128
FQKV = 6144
NT = S // 128
ND = D // 128
IFF = 11008
NIB = IFF // 128
SUPS = [8] * 10 + [6]
EXP_SHIFT = -4.0
SM_SCALE = 1.0 / math.sqrt(HD)
BPF32 = 512
C_ONE, C_ZERO, C_EPS, C_NHALF, C_SHIFT, C_SMS = range(6)


class _DryInst:
    def then_inc(self, sem, k):
        return self


class _DryEng:
    def wait_ge(self, *a, **k):
        return None

    def __getattr__(self, name):
        return lambda *a, **k: _DryInst()


class _Dummy:
    def __getitem__(self, *_):
        return self

    def rearrange(self, *a, **k):
        return self


class Marks(dict):
    lenient = True

    def __setitem__(self, k, v):
        if not self.lenient and k in self and self[k] != v:
            raise AssertionError(f"mark mismatch {k}: {self[k]} vs {v}")
        dict.__setitem__(self, k, v)

    def __missing__(self, k):
        if self.lenient:
            return None
        raise KeyError(k)


def build_nc(upto="full"):
    nc = bass.Bass("TRN2", target_bir_lowering=False)

    x_d = nc.dram_tensor("x", [S, D], F32, kind="ExternalInput").ap()
    wqkv_d = nc.dram_tensor("wqkv", [D, FQKV], F32, kind="ExternalInput").ap()
    wo_d = nc.dram_tensor("wo", [D, D], F32, kind="ExternalInput").ap()
    qg_d = nc.dram_tensor("qg", [IFF, D], I32, kind="ExternalInput").ap()
    qu_d = nc.dram_tensor("qu", [IFF, D], I32, kind="ExternalInput").ap()
    qd_d = nc.dram_tensor("qd", [D, IFF], I32, kind="ExternalInput").ap()
    sg_d = nc.dram_tensor("sg", [IFF, 32], F32, kind="ExternalInput").ap()
    zg_d = nc.dram_tensor("zg", [IFF, 32], F32, kind="ExternalInput").ap()
    su_d = nc.dram_tensor("su", [IFF, 32], F32, kind="ExternalInput").ap()
    zu_d = nc.dram_tensor("zu", [IFF, 32], F32, kind="ExternalInput").ap()
    sd_d = nc.dram_tensor("sd", [D, NIB], F32, kind="ExternalInput").ap()
    zd_d = nc.dram_tensor("zd", [D, NIB], F32, kind="ExternalInput").ap()
    cosp_d = nc.dram_tensor("cosp", [128, S], BF16, kind="ExternalInput").ap()
    sinp_d = nc.dram_tensor("sinp", [128, S], BF16, kind="ExternalInput").ap()
    pswap_d = nc.dram_tensor("pswap", [128, 128], BF16, kind="ExternalInput").ap()
    ident_d = nc.dram_tensor("ident", [128, 128], BF16, kind="ExternalInput").ap()
    maskb_d = nc.dram_tensor("maskb", [128, 128], BF16, kind="ExternalInput").ap()
    y_d = nc.dram_tensor("y", [S, D], F32, kind="ExternalOutput").ap()
    qsp_d = nc.dram_tensor("qsp", [NH, 128, S], BF16).ap()
    ksp_d = nc.dram_tensor("ksp", [NKV, 128, S], BF16).ap()
    vsp_d = nc.dram_tensor("vsp", [NT, 128, NKV * 128], BF16).ap()
    x1sp_d = nc.dram_tensor("x1sp", [S, D], F32).ap()
    hsp_d = nc.dram_tensor("hsp", [NIB, 128, S], BF16).ap()
    tap_d = None
    if upto in ("p0", "p2", "p3b"):
        tap_d = nc.dram_tensor("tap", [128, ND * S], BF16, kind="ExternalOutput").ap()

    wq3 = wqkv_d.rearrange("(j p) c -> p j c", p=128)
    wo3 = wo_d.rearrange("(j p) c -> p j c", p=128)
    sg3 = sg_d.rearrange("(a p) g -> p a g", p=128)
    zg3 = zg_d.rearrange("(a p) g -> p a g", p=128)
    su3 = su_d.rearrange("(a p) g -> p a g", p=128)
    zu3 = zu_d.rearrange("(a p) g -> p a g", p=128)
    sd3 = sd_d.rearrange("(a p) g -> p a g", p=128)
    zd3 = zd_d.rearrange("(a p) g -> p a g", p=128)

    st0 = ExitStack()
    E0 = st0.enter_context
    psA = E0(nc.psum_tensor("psA", [128, 8 * BPF32], F32))
    consts = E0(nc.sbuf_tensor("consts", [128, 8], F32))
    onesc = E0(nc.sbuf_tensor("onesc", [128, 1], BF16))
    onesr = E0(nc.sbuf_tensor("onesr", [1, 128], BF16))
    onesr32 = E0(nc.sbuf_tensor("onesr32", [1, 128], F32))
    pswap = E0(nc.sbuf_tensor("pswap_s", [128, 128], BF16))
    ident = E0(nc.sbuf_tensor("ident_s", [128, 128], BF16))
    maskb = E0(nc.sbuf_tensor("maskb_s", [128, 128], BF16))

    REG = []

    class Sem:
        def __init__(self, name):
            self.h = st0.enter_context(nc.semaphore(name))
            self.n = 0
            self.idx = len(REG)
            REG.append(self)

        def tick(self, inst, k=1):
            if not isinstance(inst, _DryInst):
                inst.then_inc(self.h, k)
            self.n += k
            return (self.idx, self.n)

        def cur(self):
            return (self.idx, self.n)

    def bankS(name, n):
        return [Sem(f"{name}{i}") for i in range(n)]

    s_pe = Sem("s_pe")
    s_dve = Sem("s_dve")
    s_act = Sem("s_act")
    sm_tbl = bankS("m_tbl", 3)
    sm_x = bankS("m_x", 2)
    sm_cs = bankS("m_cs", 2)
    sm_wp = bankS("m_wp", 8)
    sm_rout = bankS("m_rout", 2)
    sm_vst = bankS("m_vst", 4)
    sm_qh = bankS("m_qh", 2)
    sm_kg = bankS("m_kg", 2)
    sm_vg = bankS("m_vg", 4)
    sm_wos = bankS("m_wos", 8)
    sm_xst = bankS("m_xst", 4)
    sm_x1w = bankS("m_x1w", 2)
    sm_qgu = bankS("m_qgu", 6)
    sm_scl = bankS("m_scl", 8)
    sm_hw = bankS("m_hw", 4)
    sm_sdz = bankS("m_sdz", 8)
    sm_hb = bankS("m_hb", 4)
    sm_qdb = bankS("m_qdb", 8)
    sm_x1b = bankS("m_x1b", 8)
    sm_out = bankS("m_out", 4)
    sm_tap = bankS("m_tap", 2)

    M = Marks()

    def bank(b, lo=0, hi=BPF32, p0=0, p1=128):
        assert 0 <= b < 8 and 0 <= lo < hi <= BPF32
        return psA[p0:p1, b * BPF32 + lo: b * BPF32 + hi]

    def cst(i):
        return consts[:, i:i + 1]

    def _aslist(mv):
        return mv if isinstance(mv, list) else [mv]

    def WV(eng, mv):
        if mv is None:
            return
        for si, v in _aslist(mv):
            eng.wait_ge(REG[si].h, v)

    def W(eng, key):
        WV(eng, dict.get(M, key))

    def WT(eng, key):
        mv = dict.get(M, key)
        if mv is None:
            assert M.lenient, f"missing required mark {key}"
            return
        WV(eng, mv)

    def phases(dry, ctx):
        for s in REG:
            s.n = 0

        def phase(allocfn, bodies):
            if dry:
                bufs = allocfn(None)
                for _, fn in bodies:
                    fn(_DryEng(), bufs)
            else:
                with ExitStack() as stL:
                    bufs = allocfn(stL)
                    with nc.Block() as blk:
                        for name, fn in bodies:
                            getattr(blk, name)(
                                lambda e, fn=fn, bufs=bufs: fn(e, bufs))

        def mk(stL):
            if stL is None:
                return lambda n, sh, dt: _Dummy()
            return lambda n, sh, dt: stL.enter_context(nc.sbuf_tensor(n, sh, dt))

        bigT = ctx["bigT"]

        def ln_alloc(tag):
            def f(stL):
                m = mk(stL)
                return SimpleNamespace(
                    xrow=[m(f"{tag}_xr{i}", [128, D], F32) for i in range(2)],
                    xn=[m(f"{tag}_xn{i}", [128, D], BF16) for i in range(2)],
                    trash=m(f"{tag}_tr", [128, D], F32),
                    cols=m(f"{tag}_cols", [128, 20], F32),
                )
            return f

        def ln_bodies(tag, src_d, first=False, pre_sy=None, pre_gp=None):
            def b_sync(sy, u):
                if pre_sy:
                    pre_sy(sy)
                for tt in range(NT):
                    W(sy, (tag, "xn", tt - 2))
                    M[(tag, "xld", tt)] = sm_x[tt % 2].tick(
                        sy.dma_start(u.xrow[tt % 2][:],
                                     src_d[tt * 128:(tt + 1) * 128, :]), 16)

            def b_gp(g, u):
                if pre_gp:
                    pre_gp(g)

            def b_vec(v, u):
                cols = u.cols
                if first:
                    for i, val in ((C_ONE, 1.0), (C_ZERO, 0.0), (C_EPS, 1e-5),
                                   (C_NHALF, -0.5), (C_SHIFT, EXP_SHIFT),
                                   (C_SMS, SM_SCALE)):
                        v.memset(consts[:, i:i + 1], val)
                    v.memset(onesc[:], 1.0)
                    v.memset(onesr[:], 1.0)
                    M["consts"] = s_dve.tick(v.memset(onesr32[:], 1.0))
                for tt in range(NT):
                    b = tt % 2
                    o = 10 * b
                    WT(v, (tag, "xld", tt))
                    W(v, (tag, "xn", tt - 2))
                    m1 = s_dve.tick(v.reduce_sum(cols[:, o:o + 1], u.xrow[b][:], axis=AX.X))
                    WV(v, m1)
                    s_dve.tick(v.tensor_scalar_mul(cols[:, o + 2:o + 3], cols[:, o:o + 1], 1.0 / D))
                    WT(v, (tag, "sq", tt))
                    m4 = s_dve.tick(v.tensor_scalar_mul(cols[:, o + 3:o + 4], cols[:, o + 1:o + 2], 1.0 / D))
                    WV(v, m4)
                    m5 = s_dve.tick(v.tensor_mul(cols[:, o + 4:o + 5], cols[:, o + 2:o + 3], cols[:, o + 2:o + 3]))
                    WV(v, m5)
                    M[(tag, "var", tt)] = s_dve.tick(
                        v.tensor_sub(cols[:, o + 5:o + 6], cols[:, o + 3:o + 4], cols[:, o + 4:o + 5]))
                    WT(v, (tag, "rstd", tt))
                    M[(tag, "negmr", tt)] = s_dve.tick(v.scalar_tensor_tensor(
                        cols[:, o + 8:o + 9], cols[:, o + 2:o + 3], -1.0,
                        cols[:, o + 7:o + 8], ALU.mult, ALU.mult))
                    for jb in range(8):
                        WT(v, (tag, "tp", tt, jb))
                        src = psA[:, (6 + jb % 2) * BPF32:(7 + jb % 2) * BPF32]
                        M[(tag, "ev", tt, jb)] = s_dve.tick(v.tensor_copy(
                            bigT[:, 4 * jb:4 * jb + 4, tt * 128:(tt + 1) * 128],
                            src.rearrange("p (a c) -> p a c", a=4)))

            def b_scal(sc, u):
                cols = u.cols
                if first:
                    WT(sc, "consts")
                for tt in range(NT):
                    b = tt % 2
                    o = 10 * b
                    WT(sc, (tag, "xld", tt))
                    W(sc, (tag, "xn", tt - 2))
                    M[(tag, "sq", tt)] = s_act.tick(sc.activation(
                        u.trash[:], u.xrow[b][:], AF.Square,
                        bias=cst(C_ZERO), scale=cst(C_ONE),
                        accum_out=cols[:, o + 1:o + 2]))
                    WT(sc, (tag, "var", tt))
                    a1 = s_act.tick(sc.activation(cols[:, o + 6:o + 7], cols[:, o + 5:o + 6],
                                                  AF.Ln, bias=cst(C_EPS), scale=cst(C_ONE)))
                    WV(sc, a1)
                    M[(tag, "rstd", tt)] = s_act.tick(sc.activation(
                        cols[:, o + 7:o + 8], cols[:, o + 6:o + 7],
                        AF.Exp, bias=cst(C_ZERO), scale=cst(C_NHALF)))
                    WT(sc, (tag, "negmr", tt))
                    for q in range(4):
                        M[(tag, "xnq", tt, q)] = s_act.tick(sc.activation(
                            u.xn[b][:, q * 1024:(q + 1) * 1024],
                            u.xrow[b][:, q * 1024:(q + 1) * 1024], AF.Identity,
                            bias=cols[:, o + 8:o + 9], scale=cols[:, o + 7:o + 8]))
                    M[(tag, "xn", tt)] = s_act.cur()

            def b_tens(t, u):
                if first:
                    WT(t, "consts")
                    WT(t, "tbl")
                for tt in range(NT):
                    b = tt % 2
                    for jb in range(8):
                        WT(t, (tag, "xnq", tt, jb // 2))
                        bk = 6 + jb % 2
                        if jb >= 2:
                            W(t, (tag, "ev", tt, jb - 2))
                        else:
                            W(t, (tag, "ev", tt - 1, jb + 6))
                        ins = None
                        for u4 in range(4):
                            j = 4 * jb + u4
                            ins = t.matmul(bank(bk, u4 * 128, (u4 + 1) * 128),
                                           u.xn[b][:, j * 128:(j + 1) * 128],
                                           ident[:], start=True, stop=True)
                        M[(tag, "tp", tt, jb)] = s_pe.tick(ins)

            return [("sync", b_sync), ("gpsimd", b_gp), ("vector", b_vec),
                    ("scalar", b_scal), ("tensor", b_tens)]

        def p0_gp(g):
            M["tbl"] = [
                sm_tbl[0].tick(g.dma_start(pswap[:], pswap_d), 16),
                sm_tbl[1].tick(g.dma_start(ident[:], ident_d), 16),
                sm_tbl[2].tick(g.dma_start(maskb[:], maskb_d), 16),
            ]

        def emit_tap():
            def tap_body(sy, u):
                fins = []
                for j in range(ND):
                    if j >= 2:
                        WV(sy, fins[j - 2])
                    fins.append(sm_tap[j % 2].tick(
                        sy.dma_start(tap_d[:, j * S:(j + 1) * S], bigT[:, j, :]), 16))
                WV(sy, fins[-2])
                WV(sy, fins[-1])

            phase(lambda stL: None, [("sync", tap_body)])

        def emit_drain():
            def drain_body(sy, u):
                WV(sy, [s.cur() for s in REG])

            phase(lambda stL: None, [("sync", drain_body)])

        phase(ln_alloc("ln1"), ln_bodies("ln1", x_d, first=True, pre_gp=p0_gp))
        if upto == "p0":
            emit_tap()
            emit_drain()
            return

        NPAIR = FQKV // 256

        def p1_alloc(stL):
            m = mk(stL)
            return SimpleNamespace(
                wpair=[m(f"wp{i}", [128, ND, 256], BF16) for i in range(2)],
                qpre=[m(f"qpre{i}", [128, S], BF16) for i in range(2)],
                rtA=[m(f"rtA{i}", [128, S], BF16) for i in range(2)],
                rtB=[m(f"rtB{i}", [128, S], BF16) for i in range(2)],
                rout=[m(f"rout{i}", [128, S], BF16) for i in range(2)],
                vst=[m(f"vst{i}", [128, 256], BF16) for i in range(4)],
                cosp=m("cosp_s", [128, S], BF16),
                sinp=m("sinp_s", [128, S], BF16),
            )

        def p1_sync(sy, u):
            M["cos"] = [
                sm_cs[0].tick(sy.dma_start(u.cosp[:], cosp_d), 16),
                sm_cs[1].tick(sy.dma_start(u.sinp[:], sinp_d), 16),
            ]

        def p1_gp(g, u):
            for fp in range(NPAIR):
                W(g, ("p1pe", fp - 2))
                M[("wp", fp)] = [
                    sm_wp[(fp % 2) * 4 + jg].tick(g.dma_start(
                        u.wpair[fp % 2][:, 8 * jg:8 * jg + 8, :],
                        wq3[:, 8 * jg:8 * jg + 8, 256 * fp:256 * (fp + 1)]), 16)
                    for jg in range(4)]

        def p1_tens(t, u):
            WT(t, "cos")
            for fp in range(NPAIR):
                WT(t, ("wp", fp))
                if fp < 20:
                    for sub in range(2):
                        ft = 2 * fp + sub
                        grp = ft % 2
                        W(t, ("p1ropeA", ft - 2))
                        ins = None
                        for j in range(ND):
                            for ch in range(4):
                                ins = t.matmul(
                                    bank(grp * 4 + ch),
                                    u.wpair[fp % 2][:, j, sub * 128:(sub + 1) * 128],
                                    bigT[:, j, ch * 512:(ch + 1) * 512],
                                    start=(j == 0), stop=(j == ND - 1))
                        M[("p1qpre", ft)] = s_pe.tick(ins)
                        ins = None
                        for ch in range(4):
                            WT(t, ("p1evacC", ft, ch))
                            ins = t.matmul(bank(grp * 4 + ch), pswap[:],
                                           u.qpre[ft % 2][:, ch * 512:(ch + 1) * 512],
                                           start=True, stop=True)
                        M[("p1swap", ft)] = s_pe.tick(ins)
                    M[("p1pe", fp)] = s_pe.cur()
                else:
                    if fp == 20:
                        WT(t, ("p1ropeA", 38))
                    for tt in range(NT):
                        slot = tt % 4
                        if tt >= 4:
                            W(t, ("p1vev", fp, tt - 4))
                        elif fp > 20:
                            W(t, ("p1vev", fp - 1, tt + 12))
                        ins = None
                        for j in range(ND):
                            ins = t.matmul(bank(slot, 0, 256),
                                           bigT[:, j, tt * 128:(tt + 1) * 128],
                                           u.wpair[fp % 2][:, j, :],
                                           start=(j == 0), stop=(j == ND - 1))
                        M[("p1v", fp, tt)] = s_pe.tick(ins)
                    M[("p1pe", fp)] = s_pe.cur()

        def p1_vec(v, u):
            for fp in range(NPAIR):
                if fp < 20:
                    for sub in range(2):
                        ft = 2 * fp + sub
                        grp = ft % 2
                        reg = psA[:, grp * 4 * BPF32:(grp * 4 + 4) * BPF32]
                        WT(v, ("p1qpre", ft))
                        W(v, ("p1ropeB", ft - 2))
                        for ch in range(4):
                            M[("p1evacC", ft, ch)] = s_dve.tick(v.tensor_copy(
                                u.qpre[ft % 2][:, ch * 512:(ch + 1) * 512],
                                psA[:, (grp * 4 + ch) * BPF32:(grp * 4 + ch + 1) * BPF32]))
                        M[("p1evac", ft)] = s_dve.cur()
                        WT(v, ("p1swap", ft))
                        W(v, ("p1spill", ft - 2))
                        M[("p1ropeA", ft)] = s_dve.tick(
                            v.tensor_mul(u.rtA[ft % 2][:], reg, u.sinp[:]))
                        mB = s_dve.tick(v.tensor_mul(u.rtB[ft % 2][:], u.qpre[ft % 2][:], u.cosp[:]))
                        M[("p1ropeB", ft)] = mB
                        WV(v, mB)
                        M[("p1rope", ft)] = s_dve.tick(
                            v.tensor_add(u.rout[ft % 2][:], u.rtA[ft % 2][:], u.rtB[ft % 2][:]))
                else:
                    for tt in range(NT):
                        slot = tt % 4
                        WT(v, ("p1v", fp, tt))
                        if tt >= 4:
                            W(v, ("p1vsp", fp, tt - 4))
                        elif fp > 20:
                            W(v, ("p1vsp", fp - 1, tt + 12))
                        M[("p1vev", fp, tt)] = s_dve.tick(
                            v.tensor_copy(u.vst[slot][:], bank(slot, 0, 256)))

        def p1_scal(sc, u):
            for fp in range(NPAIR):
                if fp < 20:
                    for sub in range(2):
                        ft = 2 * fp + sub
                        WT(sc, ("p1rope", ft))
                        dst = qsp_d[ft] if ft < NH else ksp_d[ft - NH]
                        M[("p1spill", ft)] = sm_rout[ft % 2].tick(
                            sc.dma_start(dst, u.rout[ft % 2][:]), 16)
                else:
                    for tt in range(NT):
                        WT(sc, ("p1vev", fp, tt))
                        M[("p1vsp", fp, tt)] = sm_vst[tt % 4].tick(sc.dma_start(
                            vsp_d[tt][:, (fp - 20) * 256:(fp - 19) * 256],
                            u.vst[tt % 4][:]), 16)
            M["p1_spills"] = ([s.cur() for s in sm_rout] + [s.cur() for s in sm_vst])
            WV(sc, M["p1_spills"])

        phase(p1_alloc, [("sync", p1_sync), ("gpsimd", p1_gp), ("tensor", p1_tens),
                         ("vector", p1_vec), ("scalar", p1_scal)])
        if upto == "p1":
            emit_drain()
            return

        def a_alloc(stL):
            m = mk(stL)
            return SimpleNamespace(
                qh=[m(f"qh{i}", [128, S], BF16) for i in range(2)],
                kg=[m(f"kg{i}", [128, S], BF16) for i in range(2)],
                vg=[m(f"vg{i}", [128, NT, 128], BF16) for i in range(2)],
                expS=[m(f"expS{i}", [128, NT, 512], BF16) for i in range(2)],
                btmp=[m(f"btmp{i}", [128, 512], F32) for i in range(2)],
                rcp=[m(f"rcp{i}", [1, 512], F32) for i in range(2)],
                mtmp=[m(f"mtmp{i}", [128, 128], BF16) for i in range(4)],
            )

        def tiles_of(c):
            return list(range(4 * c + 4))

        def a_sync(sy, u):
            for h in range(NH):
                WT(sy, ("p1spill", h))
                W(sy, ("a_hdone", h - 2))
                M[("a_ql", h)] = sm_qh[h % 2].tick(
                    sy.dma_start(u.qh[h % 2][:], qsp_d[h]), 16)

        def a_gp_ld(g_, u):
            for g in range(NKV):
                WT(g_, ("p1spill", NH + g))
                fp = 20 + g // 2
                for tt in range(NT):
                    WT(g_, ("p1vsp", fp, tt))
                W(g_, ("a_gdone", g - 2))
                mv = [sm_kg[g % 2].tick(g_.dma_start(u.kg[g % 2][:], ksp_d[g]), 16)]
                for half in range(2):
                    mv.append(sm_vg[(g % 2) * 2 + half].tick(g_.dma_start(
                        u.vg[g % 2][:, 8 * half:8 * half + 8, :],
                        vsp_d[8 * half:8 * half + 8, :, g * 128:(g + 1) * 128]
                        .rearrange("t p c -> p t c")), 16))
                M[("a_kvl", g)] = mv

        gsidx = [0]

        def em_scores(t, u, h, c):
            g = h // 4
            for k in tiles_of(c):
                d = k - 4 * c
                n0 = 128 * max(d, 0)
                gi = gsidx[0]
                sbank = 4 + (gi % 2)
                W(t, ("a_edone_gi", gi - 2))
                ins = t.matmul(bank(sbank, n0, 512),
                               u.kg[g % 2][:, k * 128:(k + 1) * 128],
                               u.qh[h % 2][:, c * 512 + n0:(c + 1) * 512],
                               start=True, stop=True)
                M[("a_s", h, c, k)] = s_pe.tick(ins)
                M[("a_gi_of", h, c, k)] = gi
                gsidx[0] += 1

        def em_pv(t, u, h, c):
            g = h // 4
            cb = c % 2
            ks = tiles_of(c)
            gc = 4 * h + c
            ins = None
            for k in ks:
                d = k - 4 * c
                n0 = 128 * max(d, 0)
                WT(t, ("a_e", h, c, k))
                if d >= 0:
                    WT(t, ("a_mk", h, c, k))
                if k == 0:
                    W(t, ("a_onG", gc - 2))
                ins = t.matmul(bank(cb, n0, 512),
                               u.vg[g % 2][:, k, :],
                               u.expS[cb][:, k, n0:512],
                               start=(k == 0), stop=(k == ks[-1]))
            M[("a_pv", h, c)] = s_pe.tick(ins)
            ins = None
            for k in ks:
                d = k - 4 * c
                n0 = 128 * max(d, 0)
                if k == 0:
                    W(t, ("a_rcpG", gc - 2))
                ins = t.matmul(bank(2 + cb, n0, 512, p0=0, p1=1),
                               onesc[:], u.expS[cb][:, k, n0:512],
                               start=(k == 0), stop=(k == ks[-1]))
            M[("a_dnG", gc)] = s_pe.tick(ins)

        def em_bc(t, u, h, c):
            cb = c % 2
            gc = 4 * h + c
            WT(t, ("a_rcpG", gc))
            W(t, ("a_onG", gc - 2))
            M[("a_bcG", gc)] = s_pe.tick(t.matmul(
                bank(6 + cb), onesr32[:], u.rcp[cb][0:1, :], start=True, stop=True))

        def a_tens(t, u):
            for h in range(NH):
                g = h // 4
                WT(t, ("a_ql", h))
                if h % 4 == 0:
                    WT(t, ("a_kvl", g))
                if h > 0:
                    em_bc(t, u, h - 1, 2)
                em_scores(t, u, h, 0)
                em_scores(t, u, h, 1)
                if h > 0:
                    em_bc(t, u, h - 1, 3)
                em_pv(t, u, h, 0)
                em_scores(t, u, h, 2)
                em_bc(t, u, h, 0)
                em_pv(t, u, h, 1)
                em_scores(t, u, h, 3)
                em_bc(t, u, h, 1)
                em_pv(t, u, h, 2)
                em_pv(t, u, h, 3)
                M[("a_hdone", h)] = s_pe.cur()
                if h % 4 == 3:
                    M[("a_gdone", g)] = s_pe.cur()
            em_bc(t, u, NH - 1, 2)
            em_bc(t, u, NH - 1, 3)

        def a_act(sc, u):
            for h in range(NH):
                for c in range(4):
                    cb = c % 2
                    gc = 4 * h + c
                    W(sc, ("a_dnG", gc - 2))
                    for k in tiles_of(c):
                        d = k - 4 * c
                        n0 = 128 * max(d, 0)
                        gi = M[("a_gi_of", h, c, k)]
                        sbank = 4 + ((gi or 0) % 2)
                        WT(sc, ("a_s", h, c, k))
                        if d >= 0:
                            if gc > 0:
                                ph, pc = divmod(gc - 1, 4)
                                W(sc, ("a_mk", ph, pc, 4 * pc + d))
                            M[("a_em", h, c, k)] = s_act.tick(sc.activation(
                                u.mtmp[d][:], bank(sbank, n0, n0 + 128),
                                AF.Exp, bias=cst(C_SHIFT), scale=cst(C_SMS)))
                            if n0 + 128 < 512:
                                M[("a_e", h, c, k)] = s_act.tick(sc.activation(
                                    u.expS[cb][:, k, n0 + 128:512],
                                    bank(sbank, n0 + 128, 512),
                                    AF.Exp, bias=cst(C_SHIFT), scale=cst(C_SMS)))
                            else:
                                M[("a_e", h, c, k)] = s_act.cur()
                        else:
                            M[("a_e", h, c, k)] = s_act.tick(sc.activation(
                                u.expS[cb][:, k, :], bank(sbank),
                                AF.Exp, bias=cst(C_SHIFT), scale=cst(C_SMS)))
                        M[("a_edone_gi", M[("a_gi_of", h, c, k)])] = s_act.cur()

        def a_vec(v, u):
            def masks(h, c):
                cb = c % 2
                for k in tiles_of(c):
                    d = k - 4 * c
                    if d >= 0:
                        n0 = 128 * d
                        WT(v, ("a_em", h, c, k))
                        M[("a_mk", h, c, k)] = s_dve.tick(v.tensor_mul(
                            u.expS[cb][:, k, n0:n0 + 128], u.mtmp[d][:], maskb[:]))

            def norm(h, c):
                cb = c % 2
                gc = 4 * h + c
                WT(v, ("a_dnG", gc))
                W(v, ("a_bcG", gc - 2))
                M[("a_rcpG", gc)] = s_dve.tick(
                    v.reciprocal(u.rcp[cb][0:1, :], bank(2 + cb, p0=0, p1=1)))
                WT(v, ("a_bcG", gc))
                W(v, ("a_onG", gc - 2))
                mcp = s_dve.tick(v.tensor_copy(u.btmp[cb][:], bank(6 + cb)))
                WV(v, mcp)
                M[("a_onG", gc)] = s_dve.tick(v.tensor_mul(
                    bigT[:, h, c * 512:(c + 1) * 512], bank(cb), u.btmp[cb][:]))

            for h in range(NH):
                masks(h, 0)
                if h > 0:
                    norm(h - 1, 3)
                masks(h, 1)
                norm(h, 0)
                masks(h, 2)
                norm(h, 1)
                masks(h, 3)
                norm(h, 2)
            norm(NH - 1, 3)

        phase(a_alloc, [("sync", a_sync), ("gpsimd", a_gp_ld),
                        ("tensor", a_tens), ("scalar", a_act), ("vector", a_vec)])
        if upto == "p2":
            emit_tap()
            emit_drain()
            return

        NDS = 16

        def wo_alloc(stL):
            m = mk(stL)
            return SimpleNamespace(
                wos=[m(f"wos{i}", [128, ND, 256], BF16) for i in range(2)],
                xst=[m(f"xst{i}", [128, 256], F32) for i in range(4)],
                x1st=[m(f"x1st{i}", [128, 256], F32) for i in range(2)],
            )

        def wo_gp(g, u):
            for ds in range(NDS):
                W(g, ("wo_pe", ds - 2))
                M[("wo_w", ds)] = [
                    sm_wos[(ds % 2) * 4 + jg].tick(g.dma_start(
                        u.wos[ds % 2][:, 8 * jg:8 * jg + 8, :],
                        wo3[:, 8 * jg:8 * jg + 8, 256 * ds:256 * (ds + 1)]), 16)
                    for jg in range(4)]

        def wo_sync(sy, u):
            for ds in range(NDS):
                for tt in range(NT):
                    i = ds * NT + tt
                    W(sy, ("wo_add", i - 4))
                    M[("wo_x", i)] = sm_xst[i % 4].tick(sy.dma_start(
                        u.xst[i % 4][:],
                        x_d[tt * 128:(tt + 1) * 128, 256 * ds:256 * (ds + 1)]), 16)

        def wo_tens(t, u):
            for ds in range(NDS):
                WT(t, ("wo_w", ds))
                for tt in range(NT):
                    i = ds * NT + tt
                    bk = tt % 2
                    W(t, ("wo_add", i - 2))
                    ins = None
                    for hh in range(NH):
                        ins = t.matmul(bank(bk, 0, 256),
                                       bigT[:, hh, tt * 128:(tt + 1) * 128],
                                       u.wos[ds % 2][:, hh, :],
                                       start=(hh == 0), stop=(hh == NH - 1))
                    M[("wo_mm", i)] = s_pe.tick(ins)
                M[("wo_pe", ds)] = s_pe.cur()

        def wo_vec(v, u):
            for ds in range(NDS):
                for tt in range(NT):
                    i = ds * NT + tt
                    bk = tt % 2
                    WT(v, ("wo_mm", i))
                    WT(v, ("wo_x", i))
                    W(v, ("wo_sp", i - 2))
                    M[("wo_add", i)] = s_dve.tick(v.tensor_add(
                        u.x1st[tt % 2][:], bank(bk, 0, 256), u.xst[i % 4][:]))

        def wo_scal(sc, u):
            for ds in range(NDS):
                for tt in range(NT):
                    i = ds * NT + tt
                    WT(sc, ("wo_add", i))
                    M[("wo_sp", i)] = sm_x1w[tt % 2].tick(sc.dma_start(
                        x1sp_d[tt * 128:(tt + 1) * 128, 256 * ds:256 * (ds + 1)],
                        u.x1st[tt % 2][:]), 16)
            M["x1_done"] = [s.cur() for s in sm_x1w]
            WV(sc, M["x1_done"])

        phase(wo_alloc, [("gpsimd", wo_gp), ("sync", wo_sync), ("tensor", wo_tens),
                         ("vector", wo_vec), ("scalar", wo_scal)])
        if upto == "p3a":
            emit_drain()
            return

        def p3b_sy(sy):
            WT(sy, "x1_done")

        phase(ln_alloc("ln2"), ln_bodies("ln2", x1sp_d, pre_sy=p3b_sy))
        if upto == "p3b":
            emit_tap()
            emit_drain()
            return

        NSC = (NIB + 3) // 4

        def p4_alloc(stL):
            m = mk(stL)
            return SimpleNamespace(
                qbg=[m(f"qbg{i}", [128, D], BF16) for i in range(3)],
                qbu=[m(f"qbu{i}", [128, D], BF16) for i in range(3)],
                wgT=m("wgT", [128, ND, 128], BF16),
                wuT=m("wuT", [128, ND, 128], BF16),
                sgb=[m(f"sgb{i}", [128, 4, 32], F32) for i in range(2)],
                zgb=[m(f"zgb{i}", [128, 4, 32], F32) for i in range(2)],
                sub=[m(f"sub{i}", [128, 4, 32], F32) for i in range(2)],
                zub=[m(f"zub{i}", [128, 4, 32], F32) for i in range(2)],
                nzg=[m(f"nzg{i}", [128, 4, 32], F32) for i in range(2)],
                nzu=[m(f"nzu{i}", [128, 4, 32], F32) for i in range(2)],
                silu=[m(f"silu{i}", [128, 512], BF16) for i in range(2)],
                hac=[m(f"hac{i}", [128, 1024], BF16) for i in range(2)],
            )

        def p4_sync(sy, u):
            def scl_load(sy, ac):
                if ac >= 2:
                    W(sy, ("p4dqa", 4 * (ac - 1) - 1))
                a0 = 4 * ac
                an = min(4, NIB - a0)
                M[("p4scl", ac)] = [
                    sm_scl[ai * 2 + ac % 2].tick(sy.dma_start(
                        dst[ac % 2][:, 0:an, :], src[:, a0:a0 + an, :]), 16)
                    for ai, (src, dst) in enumerate(
                        ((sg3, u.sgb), (zg3, u.zgb), (su3, u.sub), (zu3, u.zub)))]

            for a in range(NIB):
                if a == 0:
                    scl_load(sy, 0)
                    scl_load(sy, 1)
                elif (a + 2) % 4 == 0 and 2 <= (a + 2) // 4 < (NIB + 3) // 4:
                    scl_load(sy, (a + 2) // 4)
                for hf in range(2):
                    WT(sy, ("p4hh", a, hf))
                    M[("p4hspH", a, hf)] = sm_hw[(a % 2) * 2 + hf].tick(
                        sy.dma_start(hsp_d[a][:, hf * 1024:(hf + 1) * 1024],
                                     u.hac[a % 2][:]), 16)
            M["p4_spills"] = [s.cur() for s in sm_hw]
            WV(sy, M["p4_spills"])

        def p4_gp(g, u):
            for a in range(NIB):
                W(g, ("p4T", a - 3))
                M[("p4w", a)] = [
                    sm_qgu[0 + a % 3].tick(g.dma_start(
                        u.qbg[a % 3][:], qg_d[a * 128:(a + 1) * 128, :]), 16),
                    sm_qgu[3 + a % 3].tick(g.dma_start(
                        u.qbu[a % 3][:], qu_d[a * 128:(a + 1) * 128, :]), 16)]

        def p4_nz(v, u, a):
            ac = a // 4
            if a % 4 == 0:
                WT(v, ("p4scl", ac))
                if ac >= 2:
                    W(v, ("p4dqa", 4 * (ac - 1) - 1))
                s_dve.tick(v.scalar_tensor_tensor(
                    u.nzg[ac % 2][:], u.zgb[ac % 2][:], -1.0, u.sgb[ac % 2][:],
                    ALU.mult, ALU.mult))
                mz = s_dve.tick(v.scalar_tensor_tensor(
                    u.nzu[ac % 2][:], u.zub[ac % 2][:], -1.0, u.sub[ac % 2][:],
                    ALU.mult, ALU.mult))
                WV(v, mz)
                M[("p4nz", ac)] = mz

        def p4_dqa(sc, u, a, part):
            ac = a // 4
            js = list(range(ND))
            quarter = js[part * len(js) // 4:(part + 1) * len(js) // 4]
            if part == 0:
                WT(sc, ("p4w", a))
                WT(sc, ("p4nz", ac))
                W(sc, ("p4dqa", a - 1))
            ins = None
            for qb, sb, nzb in ((u.qbg, u.sgb, u.nzg), (u.qbu, u.sub, u.nzu)):
                for j in quarter:
                    ins = sc.activation(
                        qb[a % 3][:, j * 128:(j + 1) * 128],
                        qb[a % 3][:, j * 128:(j + 1) * 128],
                        AF.Identity, bias=nzb[ac % 2][:, a % 4, j:j + 1],
                        scale=sb[ac % 2][:, a % 4, j:j + 1])
            if part == 3:
                M[("p4dqa", a)] = s_act.tick(ins)
            else:
                s_act.tick(ins)

        def p4_vec(v, u):
            p4_nz(v, u, 0)
            for a in range(NIB):
                for tv in range(8):
                    WT(v, ("p4tp", a, 2 * tv))
                    WT(v, ("p4tp", a, 2 * tv + 1))
                    dstT = u.wgT if tv < 4 else u.wuT
                    j0 = (tv % 4) * 8
                    pb = 4 + 2 * (tv % 2)
                    src = psA[:, pb * BPF32:(pb + 2) * BPF32]
                    M[("p4tev2", a, tv)] = s_dve.tick(v.tensor_copy(
                        dstT[:, j0:j0 + 8, :], src.rearrange("p (q c) -> p q c", q=8)))
                M[("p4tevall", a)] = s_dve.cur()
                if a + 1 < NIB:
                    p4_nz(v, u, a + 1)
                for c in range(4):
                    WT(v, ("p4silu", a, c))
                    WT(v, ("p4u", a, c))
                    if c == 0:
                        W(v, ("p4hspH", a - 2, 1))
                    elif c == 2:
                        WT(v, ("p4hspH", a, 0))
                    M[("p4mul", a, c)] = s_dve.tick(v.tensor_mul(
                        u.hac[a % 2][:, (c % 2) * 512:(c % 2 + 1) * 512],
                        bank(2 + c % 2), u.silu[c % 2][:]))
                    if c % 2 == 1:
                        M[("p4hh", a, c // 2)] = s_dve.cur()

        def p4_act(sc, u):
            for part in range(4):
                p4_dqa(sc, u, 0, part)
            for a in range(NIB):
                for c in range(4):
                    if a + 1 < NIB:
                        p4_dqa(sc, u, a + 1, c)
                    WT(sc, ("p4g", a, c))
                    if c >= 2:
                        W(sc, ("p4mul", a, c - 2))
                    elif a > 0:
                        W(sc, ("p4mul", a - 1, c + 2))
                    M[("p4silu", a, c)] = s_act.tick(sc.activation(
                        u.silu[c % 2][:], bank(c % 2), AF.Silu,
                        bias=cst(C_ZERO), scale=cst(C_ONE)))

        def p4_tens(t, u):
            for a in range(NIB):
                WT(t, ("p4dqa", a))
                for gb in range(16):
                    if gb >= 4:
                        W(t, ("p4tev2", a, (gb - 4) // 2))
                    else:
                        W(t, ("p4tev2", a - 1, (gb + 12) // 2))
                    src = u.qbg if gb < 8 else u.qbu
                    j0 = (gb % 8) * 4
                    pb = 4 + 2 * ((gb // 2) % 2)
                    ins = None
                    for q4 in range(4):
                        j = j0 + q4
                        ins = t.matmul(bank(pb + gb % 2, q4 * 128, (q4 + 1) * 128),
                                       src[a % 3][:, j * 128:(j + 1) * 128],
                                       ident[:], start=True, stop=True)
                    M[("p4tp", a, gb)] = s_pe.tick(ins)
                M[("p4T", a)] = s_pe.cur()
                WT(t, ("p4tevall", a))
                for c in range(4):
                    if c >= 2:
                        W(t, ("p4silu", a, c - 2))
                    elif a > 0:
                        W(t, ("p4silu", a - 1, c + 2))
                    ins = None
                    for j in range(ND):
                        ins = t.matmul(bank(c % 2), u.wgT[:, j, :],
                                       bigT[:, j, c * 512:(c + 1) * 512],
                                       start=(j == 0), stop=(j == ND - 1))
                    M[("p4g", a, c)] = s_pe.tick(ins)
                    if c >= 2:
                        W(t, ("p4mul", a, c - 2))
                    elif a > 0:
                        W(t, ("p4mul", a - 1, c + 2))
                    ins = None
                    for j in range(ND):
                        ins = t.matmul(bank(2 + c % 2), u.wuT[:, j, :],
                                       bigT[:, j, c * 512:(c + 1) * 512],
                                       start=(j == 0), stop=(j == ND - 1))
                    M[("p4u", a, c)] = s_pe.tick(ins)

        phase(p4_alloc, [("sync", p4_sync), ("gpsimd", p4_gp), ("vector", p4_vec),
                         ("scalar", p4_act), ("tensor", p4_tens)])
        if upto == "p4":
            emit_drain()
            return

        ctx["close_big"]()

        NS = len(SUPS)
        SL = [(q4, s) for q4 in range(4) for s in range(NS)]
        NGB = [(ss + 3) // 4 for ss in SUPS]

        def p5_li(idx, hh):
            return idx * 8 + hh

        def p5_gidx(idx, hh, gb):
            q4, s = SL[idx]
            per = [8 * NGB[SL[i][1]] for i in range(len(SL))]
            return sum(per[:idx]) + hh * NGB[s] + gb

        _gcum = [0]
        for _i in range(len(SL)):
            _gcum.append(_gcum[-1] + 8 * NGB[SL[_i][1]])

        def p5_gidx(idx, hh, gb):
            return _gcum[idx] + hh * NGB[SL[idx][1]] + gb

        def p5_lin(idx, tt, ch):
            return (idx * NT + tt) * 2 + ch

        def p5_alloc(stL):
            m = mk(stL)
            return SimpleNamespace(
                y2=m("y2", [128, NT * 1024], F32),
                wdT=[m(f"wdT{i}", [128, 8, 1024], BF16) for i in range(2)],
                hbuf=[m(f"hbuf{i}", [128, 8, 512], BF16) for i in range(2)],
                qdb=[m(f"qdb{i}", [128, 1024], BF16) for i in range(8)],
                sdb=m("sdb", [128, 32, NIB], F32),
                zdb=m("zdb", [128, 32, NIB], F32),
                nzd=[m(f"nzd{i}", [128, 8, 8], F32) for i in range(2)],
                outst=[m(f"outst{i}", [128, 1024], F32) for i in range(4)],
                x1b=[m(f"x1b{i}", [128, 1024], F32) for i in range(8)],
            )

        def p5_sync(sy, u):
            WT(sy, "x1_done")
            WT(sy, "p4_spills")
            M["p5sdz"] = (
                [sm_sdz[sl].tick(sy.dma_start(
                    u.sdb[:, 8 * sl:8 * sl + 8, :], sd3[:, 8 * sl:8 * sl + 8, :]), 16)
                 for sl in range(4)] +
                [sm_sdz[4 + sl].tick(sy.dma_start(
                    u.zdb[:, 8 * sl:8 * sl + 8, :], zd3[:, 8 * sl:8 * sl + 8, :]), 16)
                 for sl in range(4)])
            for idx, (q4, s) in enumerate(SL):
                ss = SUPS[s]
                for tg in range(4):
                    li4 = idx * 4 + tg
                    W(sy, ("p5mmG", li4 - 2))
                    nh = (ss + 1) // 2
                    M[("p5h", idx, tg)] = [
                        sm_hb[(tg % 2) * 2 + 0].tick(sy.dma_start(
                            u.hbuf[tg % 2][:, 0:nh, :],
                            hsp_d[8 * s:8 * s + nh, :, tg * 512:(tg + 1) * 512]
                            .rearrange("a p c -> p a c")), 16),
                        sm_hb[(tg % 2) * 2 + 1].tick(sy.dma_start(
                            u.hbuf[tg % 2][:, nh:ss, :],
                            hsp_d[8 * s + nh:8 * s + ss, :, tg * 512:(tg + 1) * 512]
                            .rearrange("a p c -> p a c")), 16)]
                    if idx == 0 and tg < 2:
                        for tt in (4 * tg, 4 * tg + 1, 4 * tg + 2, 4 * tg + 3):
                            M[("p5x1", 0, tt)] = sm_x1b[tt % 8].tick(sy.dma_start(
                                u.x1b[tt % 8][:],
                                x1sp_d[tt * 128:(tt + 1) * 128, 0:1024]), 16)

        def p5_gp(g, u):
            for idx, (q4, s) in enumerate(SL):
                ss = SUPS[s]
                cw = ss * 128
                for hh in range(8):
                    li = p5_li(idx, hh)
                    W(g, ("p5tpA", li - 8))
                    r0 = q4 * 1024 + hh * 128
                    M[("p5w", li)] = sm_qdb[hh].tick(g.dma_start(
                        u.qdb[hh][:, 0:cw],
                        qd_d[r0:r0 + 128, 1024 * s:1024 * s + cw]), 16)

        def p5_dq(v, u, idx):
            q4, s = SL[idx]
            ss = SUPS[s]
            if idx >= 2:
                W(v, ("p5dqaA", idx - 2))
            mz = s_dve.tick(v.scalar_tensor_tensor(
                u.nzd[idx % 2][:, :, 0:ss],
                u.zdb[:, q4 * 8:q4 * 8 + 8, 8 * s:8 * s + ss], -1.0,
                u.sdb[:, q4 * 8:q4 * 8 + 8, 8 * s:8 * s + ss],
                ALU.mult, ALU.mult))
            WV(v, mz)
            M[("p5nz", idx)] = mz

        def p5_dqa(sc, u, idx):
            q4, s = SL[idx]
            ss = SUPS[s]
            WT(sc, ("p5nz", idx))
            for hh in range(8):
                li = p5_li(idx, hh)
                WT(sc, ("p5w", li))
                ai = q4 * 8 + hh
                ins = None
                for gg in range(ss):
                    ins = sc.activation(
                        u.qdb[hh][:, gg * 128:(gg + 1) * 128],
                        u.qdb[hh][:, gg * 128:(gg + 1) * 128],
                        AF.Identity,
                        bias=u.nzd[idx % 2][:, hh, gg:gg + 1],
                        scale=u.sdb[:, ai, 8 * s + gg:8 * s + gg + 1])
                M[("p5dqa", li)] = s_act.tick(ins)
            M[("p5dqaA", idx)] = s_act.cur()

        def p5_tev(v, u, idx):
            q4, s = SL[idx]
            ss = SUPS[s]
            for hh in range(8):
                for gb in range(NGB[s]):
                    WT(v, ("p5tp", idx, hh, gb))
                pb = 4 + 2 * (hh % 2)
                M[("p5tevHL", idx * 8 + hh)] = s_dve.tick(v.tensor_copy(
                    u.wdT[idx % 2][:, 0:ss, hh * 128:(hh + 1) * 128],
                    psA[:, pb * BPF32:pb * BPF32 + ss * 128]
                    .rearrange("p (q c) -> p q c", q=ss)))
            M[("p5tev", idx)] = s_dve.cur()

        def p5_vec(v, u):
            WT(v, "p5sdz")
            p5_dq(v, u, 0)
            p5_tev(v, u, 0)
            p5_dq(v, u, 1)
            for idx, (q4, s) in enumerate(SL):
                for tg in range(4):
                    for ti in range(4):
                        tt = 4 * tg + ti
                        pb = 2 * (tt % 2)
                        WT(v, ("p5mm", idx, tt, 0))
                        WT(v, ("p5mm", idx, tt, 1))
                        if s > 0:
                            W(v, ("p5y2T", idx - 1, tt))
                        elif q4 > 0:
                            W(v, ("p5fin", q4 - 1, tt))
                        dst = u.y2[:, tt * 1024:(tt + 1) * 1024]
                        src_ = psA[:, pb * BPF32:(pb + 2) * BPF32]
                        if s == 0:
                            ins = v.tensor_copy(dst, src_)
                        else:
                            ins = v.tensor_add(dst, src_, dst)
                        my = s_dve.tick(ins)
                        M[("p5y2T", idx, tt)] = my
                        M[("p5y2L", idx * NT + tt)] = my
                        if s == NS - 1:
                            WV(v, my)
                            WT(v, ("p5x1", q4, tt))
                            W(v, ("p5out", q4, tt - 4))
                            if tt < 4 and q4 > 0:
                                W(v, ("p5out", q4 - 1, tt + 12))
                            M[("p5fin", q4, tt)] = s_dve.tick(v.tensor_add(
                                u.outst[tt % 4][:], dst, u.x1b[tt % 8][:]))
                if idx + 1 < len(SL):
                    p5_tev(v, u, idx + 1)
                if idx + 2 < len(SL):
                    p5_dq(v, u, idx + 2)

        def p5_act(sc, u):
            p5_dqa(sc, u, 0)
            p5_dqa(sc, u, 1)
            for idx, (q4, s) in enumerate(SL):
                if s == NS - 1:
                    for tt in range(NT):
                        WT(sc, ("p5fin", q4, tt))
                        M[("p5out", q4, tt)] = sm_out[tt % 4].tick(sc.dma_start(
                            y_d[tt * 128:(tt + 1) * 128, q4 * 1024:(q4 + 1) * 1024],
                            u.outst[tt % 4][:]), 16)
                        nq, ntt = (q4, tt + 8) if tt < 8 else (q4 + 1, tt - 8)
                        if nq < 4:
                            M[("p5x1", nq, ntt)] = sm_x1b[ntt % 8].tick(
                                sc.dma_start(
                                    u.x1b[ntt % 8][:],
                                    x1sp_d[ntt * 128:(ntt + 1) * 128,
                                           nq * 1024:(nq + 1) * 1024]), 16)
                if idx + 2 < len(SL):
                    p5_dqa(sc, u, idx + 2)
            WV(sc, [s.cur() for s in sm_out])

        def p5_T(t, u, idx):
            q4, s = SL[idx]
            ss = SUPS[s]
            for hh in range(8):
                li = p5_li(idx, hh)
                WT(t, ("p5dqa", li))
                pb = 4 + 2 * (hh % 2)
                W(t, ("p5tevHL", idx * 8 + hh - 2))
                for gb in range(NGB[s]):
                    it0 = 4 * gb
                    itn = min(4, ss - it0)
                    ins = None
                    for q in range(itn):
                        it = it0 + q
                        ins = t.matmul(
                            bank(pb + gb, q * 128, (q + 1) * 128),
                            u.qdb[hh][:, it * 128:(it + 1) * 128],
                            ident[:], start=True, stop=True)
                    M[("p5tp", idx, hh, gb)] = s_pe.tick(ins)
                M[("p5tpA", li)] = s_pe.cur()

        def p5_tens(t, u):
            p5_T(t, u, 0)
            for idx, (q4, s) in enumerate(SL):
                ss = SUPS[s]
                WT(t, ("p5tev", idx))
                for tg in range(4):
                    li4 = idx * 4 + tg
                    WT(t, ("p5h", idx, tg))
                    for ti in range(4):
                        tt = 4 * tg + ti
                        for ch in range(2):
                            bk = 2 * (tt % 2) + ch
                            if ch == 0:
                                W(t, ("p5y2L", idx * NT + tt - 2))
                            ins = None
                            for it in range(ss):
                                ins = t.matmul(
                                    bank(bk),
                                    u.hbuf[tg % 2][:, it, ti * 128:(ti + 1) * 128],
                                    u.wdT[idx % 2][:, it, ch * 512:(ch + 1) * 512],
                                    start=(it == 0), stop=(it == ss - 1))
                            M[("p5mm", idx, tt, ch)] = s_pe.tick(ins)
                    M[("p5mmG", li4)] = s_pe.cur()
                if idx + 1 < len(SL):
                    p5_T(t, u, idx + 1)

        phase(p5_alloc, [("sync", p5_sync), ("gpsimd", p5_gp), ("vector", p5_vec),
                         ("scalar", p5_act), ("tensor", p5_tens)])
        emit_drain()

    ctx = {"bigT": _Dummy(), "close_big": lambda: None}
    M.lenient = True
    phases(True, ctx)
    M.lenient = False

    stBig = ExitStack()
    bigT_real = stBig.enter_context(nc.sbuf_tensor("bigT", [128, ND, S], BF16))
    ctx = {"bigT": bigT_real, "close_big": stBig.close}
    phases(False, ctx)
    try:
        stBig.close()
    except Exception:
        pass
    st0.close()
    return nc


def _tables():
    inv = 10000.0 ** (-(np.arange(64, dtype=np.float64) * 2.0 / 128.0))
    ang = np.arange(S, dtype=np.float64)[None, :] * inv[:, None]
    cosp = np.repeat(np.cos(ang), 2, axis=0)
    sinp2 = np.empty((128, S), dtype=np.float64)
    sinp2[0::2] = -np.sin(ang)
    sinp2[1::2] = np.sin(ang)
    pswap = np.zeros((128, 128), dtype=np.float32)
    for k in range(64):
        pswap[2 * k, 2 * k + 1] = 1.0
        pswap[2 * k + 1, 2 * k] = 1.0
    ident = np.eye(128, dtype=np.float32)
    mask = np.triu(np.ones((128, 128), dtype=np.float32))
    bf = ml_dtypes.bfloat16
    return (cosp.astype(bf), sinp2.astype(bf), pswap.astype(bf),
            ident.astype(bf), mask.astype(bf))


def make_inmaps(inputs):
    x = np.asarray(inputs["L0_attn_x"])
    cosp, sinp, pswap, ident, maskb = _tables()
    shared = {
        "wqkv": np.ascontiguousarray(np.asarray(inputs["L0_attn_w_qkv"])),
        "wo": np.ascontiguousarray(np.asarray(inputs["L0_attn_w_o"])),
        "qg": np.ascontiguousarray(np.asarray(inputs["L0_ffn_qw_gate"])),
        "qu": np.ascontiguousarray(np.asarray(inputs["L0_ffn_qw_up"])),
        "qd": np.ascontiguousarray(np.asarray(inputs["L0_ffn_qw_down"])),
        "sg": np.ascontiguousarray(np.asarray(inputs["L0_ffn_s_gate"])),
        "zg": np.ascontiguousarray(np.asarray(inputs["L0_ffn_z_gate"])),
        "su": np.ascontiguousarray(np.asarray(inputs["L0_ffn_s_up"])),
        "zu": np.ascontiguousarray(np.asarray(inputs["L0_ffn_z_up"])),
        "sd": np.ascontiguousarray(np.asarray(inputs["L0_ffn_s_down"])),
        "zd": np.ascontiguousarray(np.asarray(inputs["L0_ffn_z_down"])),
        "cosp": cosp, "sinp": sinp, "pswap": pswap,
        "ident": ident, "maskb": maskb,
    }
    return [dict(shared, x=np.ascontiguousarray(x[c])) for c in range(8)]


def kernel(**inputs):
    nc = build_nc("full")
    in_maps = make_inmaps(inputs)
    res = run_bass_kernel_spmd(nc, in_maps, core_ids=list(range(8)))
    return np.stack([np.asarray(res.results[c]["y"]) for c in range(8)], axis=0)


# version 13
# speedup vs baseline: 1.0368x; 1.0306x over previous
import math
from contextlib import ExitStack
from types import SimpleNamespace

import numpy as np
import ml_dtypes

import concourse.bass as bass
import concourse.mybir as mybir
from concourse.bass_utils import run_bass_kernel_spmd

F32 = mybir.dt.float32
BF16 = mybir.dt.bfloat16
I32 = mybir.dt.int32
AF = mybir.ActivationFunctionType
ALU = mybir.AluOpType
AX = mybir.AxisListType

S = 2048
D = 4096
NH, NKV, HD = 32, 8, 128
FQKV = 6144
NT = S // 128
ND = D // 128
IFF = 11008
NIB = IFF // 128
SUPS = [8] * 10 + [6]
EXP_SHIFT = -4.0
SM_SCALE = 1.0 / math.sqrt(HD)
BPF32 = 512
C_ONE, C_ZERO, C_EPS, C_NHALF, C_SHIFT, C_SMS = range(6)


class _DryInst:
    def then_inc(self, sem, k):
        return self


class _DryEng:
    def wait_ge(self, *a, **k):
        return None

    def __getattr__(self, name):
        return lambda *a, **k: _DryInst()


class _Dummy:
    def __getitem__(self, *_):
        return self

    def rearrange(self, *a, **k):
        return self


class Marks(dict):
    lenient = True

    def __setitem__(self, k, v):
        if not self.lenient and k in self and self[k] != v:
            raise AssertionError(f"mark mismatch {k}: {self[k]} vs {v}")
        dict.__setitem__(self, k, v)

    def __missing__(self, k):
        if self.lenient:
            return None
        raise KeyError(k)


def build_nc(upto="full"):
    nc = bass.Bass("TRN2", target_bir_lowering=False)

    x_d = nc.dram_tensor("x", [S, D], F32, kind="ExternalInput").ap()
    wqkv_d = nc.dram_tensor("wqkv", [D, FQKV], F32, kind="ExternalInput").ap()
    wo_d = nc.dram_tensor("wo", [D, D], F32, kind="ExternalInput").ap()
    qg_d = nc.dram_tensor("qg", [IFF, D], I32, kind="ExternalInput").ap()
    qu_d = nc.dram_tensor("qu", [IFF, D], I32, kind="ExternalInput").ap()
    qd_d = nc.dram_tensor("qd", [D, IFF], I32, kind="ExternalInput").ap()
    sg_d = nc.dram_tensor("sg", [IFF, 32], F32, kind="ExternalInput").ap()
    zg_d = nc.dram_tensor("zg", [IFF, 32], F32, kind="ExternalInput").ap()
    su_d = nc.dram_tensor("su", [IFF, 32], F32, kind="ExternalInput").ap()
    zu_d = nc.dram_tensor("zu", [IFF, 32], F32, kind="ExternalInput").ap()
    sd_d = nc.dram_tensor("sd", [D, NIB], F32, kind="ExternalInput").ap()
    zd_d = nc.dram_tensor("zd", [D, NIB], F32, kind="ExternalInput").ap()
    cosp_d = nc.dram_tensor("cosp", [128, S], BF16, kind="ExternalInput").ap()
    sinp_d = nc.dram_tensor("sinp", [128, S], BF16, kind="ExternalInput").ap()
    pswap_d = nc.dram_tensor("pswap", [128, 128], BF16, kind="ExternalInput").ap()
    ident_d = nc.dram_tensor("ident", [128, 128], BF16, kind="ExternalInput").ap()
    maskb_d = nc.dram_tensor("maskb", [128, 128], BF16, kind="ExternalInput").ap()
    y_d = nc.dram_tensor("y", [S, D], F32, kind="ExternalOutput").ap()
    qsp_d = nc.dram_tensor("qsp", [NH, 128, S], BF16).ap()
    ksp_d = nc.dram_tensor("ksp", [NKV, 128, S], BF16).ap()
    vsp_d = nc.dram_tensor("vsp", [NT, 128, NKV * 128], BF16).ap()
    x1sp_d = nc.dram_tensor("x1sp", [S, D], F32).ap()
    hsp_d = nc.dram_tensor("hsp", [NIB, 128, S], BF16).ap()
    tap_d = None
    if upto in ("p0", "p2", "p3b"):
        tap_d = nc.dram_tensor("tap", [128, ND * S], BF16, kind="ExternalOutput").ap()

    wq3 = wqkv_d.rearrange("(j p) c -> p j c", p=128)
    wo3 = wo_d.rearrange("(j p) c -> p j c", p=128)
    sg3 = sg_d.rearrange("(a p) g -> p a g", p=128)
    zg3 = zg_d.rearrange("(a p) g -> p a g", p=128)
    su3 = su_d.rearrange("(a p) g -> p a g", p=128)
    zu3 = zu_d.rearrange("(a p) g -> p a g", p=128)
    sd3 = sd_d.rearrange("(a p) g -> p a g", p=128)
    zd3 = zd_d.rearrange("(a p) g -> p a g", p=128)

    st0 = ExitStack()
    E0 = st0.enter_context
    psA = E0(nc.psum_tensor("psA", [128, 8 * BPF32], F32))
    consts = E0(nc.sbuf_tensor("consts", [128, 8], F32))
    onesc = E0(nc.sbuf_tensor("onesc", [128, 1], BF16))
    onesr = E0(nc.sbuf_tensor("onesr", [1, 128], BF16))
    onesr32 = E0(nc.sbuf_tensor("onesr32", [1, 128], F32))
    pswap = E0(nc.sbuf_tensor("pswap_s", [128, 128], BF16))
    ident = E0(nc.sbuf_tensor("ident_s", [128, 128], BF16))
    maskb = E0(nc.sbuf_tensor("maskb_s", [128, 128], BF16))

    REG = []

    class Sem:
        def __init__(self, name):
            self.h = st0.enter_context(nc.semaphore(name))
            self.n = 0
            self.idx = len(REG)
            REG.append(self)

        def tick(self, inst, k=1):
            if not isinstance(inst, _DryInst):
                inst.then_inc(self.h, k)
            self.n += k
            return (self.idx, self.n)

        def cur(self):
            return (self.idx, self.n)

    def bankS(name, n):
        return [Sem(f"{name}{i}") for i in range(n)]

    s_pe = Sem("s_pe")
    s_dve = Sem("s_dve")
    s_act = Sem("s_act")
    sm_tbl = bankS("m_tbl", 3)
    sm_x = bankS("m_x", 2)
    sm_cs = bankS("m_cs", 2)
    sm_wp = bankS("m_wp", 8)
    sm_rout = bankS("m_rout", 2)
    sm_vst = bankS("m_vst", 4)
    sm_qh = bankS("m_qh", 2)
    sm_kg = bankS("m_kg", 2)
    sm_vg = bankS("m_vg", 4)
    sm_wos = bankS("m_wos", 8)
    sm_xst = bankS("m_xst", 4)
    sm_x1w = bankS("m_x1w", 2)
    sm_qgu = bankS("m_qgu", 6)
    sm_scl = bankS("m_scl", 8)
    sm_hw = bankS("m_hw", 4)
    sm_sdz = bankS("m_sdz", 8)
    sm_hb = bankS("m_hb", 4)
    sm_qdb = bankS("m_qdb", 8)
    sm_x1b = bankS("m_x1b", 8)
    sm_out = bankS("m_out", 4)
    sm_tap = bankS("m_tap", 2)

    M = Marks()

    def bank(b, lo=0, hi=BPF32, p0=0, p1=128):
        assert 0 <= b < 8 and 0 <= lo < hi <= BPF32
        return psA[p0:p1, b * BPF32 + lo: b * BPF32 + hi]

    def cst(i):
        return consts[:, i:i + 1]

    def _aslist(mv):
        return mv if isinstance(mv, list) else [mv]

    def WV(eng, mv):
        if mv is None:
            return
        for si, v in _aslist(mv):
            eng.wait_ge(REG[si].h, v)

    def W(eng, key):
        WV(eng, dict.get(M, key))

    def WT(eng, key):
        mv = dict.get(M, key)
        if mv is None:
            assert M.lenient, f"missing required mark {key}"
            return
        WV(eng, mv)

    def phases(dry, ctx):
        for s in REG:
            s.n = 0

        def phase(allocfn, bodies):
            if dry:
                bufs = allocfn(None)
                for _, fn in bodies:
                    fn(_DryEng(), bufs)
            else:
                with ExitStack() as stL:
                    bufs = allocfn(stL)
                    with nc.Block() as blk:
                        for name, fn in bodies:
                            getattr(blk, name)(
                                lambda e, fn=fn, bufs=bufs: fn(e, bufs))

        def mk(stL):
            if stL is None:
                return lambda n, sh, dt: _Dummy()
            return lambda n, sh, dt: stL.enter_context(nc.sbuf_tensor(n, sh, dt))

        bigT = ctx["bigT"]

        def ln_alloc(tag):
            def f(stL):
                m = mk(stL)
                return SimpleNamespace(
                    xrow=[m(f"{tag}_xr{i}", [128, D], F32) for i in range(2)],
                    xn=[m(f"{tag}_xn{i}", [128, D], BF16) for i in range(2)],
                    trash=m(f"{tag}_tr", [128, D], F32),
                    cols=m(f"{tag}_cols", [128, 20], F32),
                )
            return f

        def ln_bodies(tag, src_d, first=False, pre_sy=None, pre_gp=None):
            def b_sync(sy, u):
                if pre_sy:
                    pre_sy(sy)
                for tt in range(NT):
                    W(sy, (tag, "xn", tt - 2))
                    M[(tag, "xld", tt)] = sm_x[tt % 2].tick(
                        sy.dma_start(u.xrow[tt % 2][:],
                                     src_d[tt * 128:(tt + 1) * 128, :]), 16)

            def b_gp(g, u):
                if pre_gp:
                    pre_gp(g)

            def b_vec(v, u):
                cols = u.cols
                if first:
                    for i, val in ((C_ONE, 1.0), (C_ZERO, 0.0), (C_EPS, 1e-5),
                                   (C_NHALF, -0.5), (C_SHIFT, EXP_SHIFT),
                                   (C_SMS, SM_SCALE)):
                        v.memset(consts[:, i:i + 1], val)
                    v.memset(onesc[:], 1.0)
                    v.memset(onesr[:], 1.0)
                    M["consts"] = s_dve.tick(v.memset(onesr32[:], 1.0))
                for tt in range(NT):
                    b = tt % 2
                    o = 10 * b
                    WT(v, (tag, "xld", tt))
                    W(v, (tag, "xn", tt - 2))
                    m1 = s_dve.tick(v.reduce_sum(cols[:, o:o + 1], u.xrow[b][:], axis=AX.X))
                    WV(v, m1)
                    s_dve.tick(v.tensor_scalar_mul(cols[:, o + 2:o + 3], cols[:, o:o + 1], 1.0 / D))
                    WT(v, (tag, "sq", tt))
                    m4 = s_dve.tick(v.tensor_scalar_mul(cols[:, o + 3:o + 4], cols[:, o + 1:o + 2], 1.0 / D))
                    WV(v, m4)
                    m5 = s_dve.tick(v.tensor_mul(cols[:, o + 4:o + 5], cols[:, o + 2:o + 3], cols[:, o + 2:o + 3]))
                    WV(v, m5)
                    M[(tag, "var", tt)] = s_dve.tick(
                        v.tensor_sub(cols[:, o + 5:o + 6], cols[:, o + 3:o + 4], cols[:, o + 4:o + 5]))
                    WT(v, (tag, "rstd", tt))
                    M[(tag, "negmr", tt)] = s_dve.tick(v.scalar_tensor_tensor(
                        cols[:, o + 8:o + 9], cols[:, o + 2:o + 3], -1.0,
                        cols[:, o + 7:o + 8], ALU.mult, ALU.mult))
                    for jb in range(8):
                        WT(v, (tag, "tp", tt, jb))
                        src = psA[:, (6 + jb % 2) * BPF32:(7 + jb % 2) * BPF32]
                        M[(tag, "ev", tt, jb)] = s_dve.tick(v.tensor_copy(
                            bigT[:, 4 * jb:4 * jb + 4, tt * 128:(tt + 1) * 128],
                            src.rearrange("p (a c) -> p a c", a=4)))

            def b_scal(sc, u):
                cols = u.cols
                if first:
                    WT(sc, "consts")
                for tt in range(NT):
                    b = tt % 2
                    o = 10 * b
                    WT(sc, (tag, "xld", tt))
                    W(sc, (tag, "xn", tt - 2))
                    M[(tag, "sq", tt)] = s_act.tick(sc.activation(
                        u.trash[:], u.xrow[b][:], AF.Square,
                        bias=cst(C_ZERO), scale=cst(C_ONE),
                        accum_out=cols[:, o + 1:o + 2]))
                    WT(sc, (tag, "var", tt))
                    a1 = s_act.tick(sc.activation(cols[:, o + 6:o + 7], cols[:, o + 5:o + 6],
                                                  AF.Ln, bias=cst(C_EPS), scale=cst(C_ONE)))
                    WV(sc, a1)
                    M[(tag, "rstd", tt)] = s_act.tick(sc.activation(
                        cols[:, o + 7:o + 8], cols[:, o + 6:o + 7],
                        AF.Exp, bias=cst(C_ZERO), scale=cst(C_NHALF)))
                    WT(sc, (tag, "negmr", tt))
                    for q in range(4):
                        M[(tag, "xnq", tt, q)] = s_act.tick(sc.activation(
                            u.xn[b][:, q * 1024:(q + 1) * 1024],
                            u.xrow[b][:, q * 1024:(q + 1) * 1024], AF.Identity,
                            bias=cols[:, o + 8:o + 9], scale=cols[:, o + 7:o + 8]))
                    M[(tag, "xn", tt)] = s_act.cur()

            def b_tens(t, u):
                if first:
                    WT(t, "consts")
                    WT(t, "tbl")
                for tt in range(NT):
                    b = tt % 2
                    for jb in range(8):
                        WT(t, (tag, "xnq", tt, jb // 2))
                        bk = 6 + jb % 2
                        if jb >= 2:
                            W(t, (tag, "ev", tt, jb - 2))
                        else:
                            W(t, (tag, "ev", tt - 1, jb + 6))
                        ins = None
                        for u4 in range(4):
                            j = 4 * jb + u4
                            ins = t.matmul(bank(bk, u4 * 128, (u4 + 1) * 128),
                                           u.xn[b][:, j * 128:(j + 1) * 128],
                                           ident[:], start=True, stop=True)
                        M[(tag, "tp", tt, jb)] = s_pe.tick(ins)

            return [("sync", b_sync), ("gpsimd", b_gp), ("vector", b_vec),
                    ("scalar", b_scal), ("tensor", b_tens)]

        def p0_gp(g):
            M["tbl"] = [
                sm_tbl[0].tick(g.dma_start(pswap[:], pswap_d), 16),
                sm_tbl[1].tick(g.dma_start(ident[:], ident_d), 16),
                sm_tbl[2].tick(g.dma_start(maskb[:], maskb_d), 16),
            ]

        def emit_tap():
            def tap_body(sy, u):
                fins = []
                for j in range(ND):
                    if j >= 2:
                        WV(sy, fins[j - 2])
                    fins.append(sm_tap[j % 2].tick(
                        sy.dma_start(tap_d[:, j * S:(j + 1) * S], bigT[:, j, :]), 16))
                WV(sy, fins[-2])
                WV(sy, fins[-1])

            phase(lambda stL: None, [("sync", tap_body)])

        def emit_drain():
            def drain_body(sy, u):
                WV(sy, [s.cur() for s in REG])

            phase(lambda stL: None, [("sync", drain_body)])

        phase(ln_alloc("ln1"), ln_bodies("ln1", x_d, first=True, pre_gp=p0_gp))
        if upto == "p0":
            emit_tap()
            emit_drain()
            return

        NPAIR = FQKV // 256

        def p1_alloc(stL):
            m = mk(stL)
            return SimpleNamespace(
                wpair=[m(f"wp{i}", [128, ND, 256], BF16) for i in range(2)],
                qpre=[m(f"qpre{i}", [128, S], BF16) for i in range(2)],
                rtA=[m(f"rtA{i}", [128, S], BF16) for i in range(2)],
                rtB=[m(f"rtB{i}", [128, S], BF16) for i in range(2)],
                rout=[m(f"rout{i}", [128, S], BF16) for i in range(2)],
                vst=[m(f"vst{i}", [128, 256], BF16) for i in range(4)],
                cosp=m("cosp_s", [128, S], BF16),
                sinp=m("sinp_s", [128, S], BF16),
            )

        def p1_sync(sy, u):
            M["cos"] = [
                sm_cs[0].tick(sy.dma_start(u.cosp[:], cosp_d), 16),
                sm_cs[1].tick(sy.dma_start(u.sinp[:], sinp_d), 16),
            ]

        def p1_gp(g, u):
            for fp in range(NPAIR):
                W(g, ("p1pe", fp - 2))
                M[("wp", fp)] = [
                    sm_wp[(fp % 2) * 4 + jg].tick(g.dma_start(
                        u.wpair[fp % 2][:, 8 * jg:8 * jg + 8, :],
                        wq3[:, 8 * jg:8 * jg + 8, 256 * fp:256 * (fp + 1)]), 16)
                    for jg in range(4)]

        def p1_tens(t, u):
            WT(t, "cos")
            for fp in range(NPAIR):
                WT(t, ("wp", fp))
                if fp < 20:
                    for sub in range(2):
                        ft = 2 * fp + sub
                        grp = ft % 2
                        W(t, ("p1ropeA", ft - 2))
                        ins = None
                        for j in range(ND):
                            for ch in range(4):
                                ins = t.matmul(
                                    bank(grp * 4 + ch),
                                    u.wpair[fp % 2][:, j, sub * 128:(sub + 1) * 128],
                                    bigT[:, j, ch * 512:(ch + 1) * 512],
                                    start=(j == 0), stop=(j == ND - 1))
                        M[("p1qpre", ft)] = s_pe.tick(ins)
                        ins = None
                        for ch in range(4):
                            WT(t, ("p1evacC", ft, ch))
                            ins = t.matmul(bank(grp * 4 + ch), pswap[:],
                                           u.qpre[ft % 2][:, ch * 512:(ch + 1) * 512],
                                           start=True, stop=True)
                        M[("p1swap", ft)] = s_pe.tick(ins)
                    M[("p1pe", fp)] = s_pe.cur()
                else:
                    if fp == 20:
                        WT(t, ("p1ropeA", 38))
                    for tt in range(NT):
                        slot = tt % 4
                        if tt >= 4:
                            W(t, ("p1vev", fp, tt - 4))
                        elif fp > 20:
                            W(t, ("p1vev", fp - 1, tt + 12))
                        ins = None
                        for j in range(ND):
                            ins = t.matmul(bank(slot, 0, 256),
                                           bigT[:, j, tt * 128:(tt + 1) * 128],
                                           u.wpair[fp % 2][:, j, :],
                                           start=(j == 0), stop=(j == ND - 1))
                        M[("p1v", fp, tt)] = s_pe.tick(ins)
                    M[("p1pe", fp)] = s_pe.cur()

        def p1_vec(v, u):
            for fp in range(NPAIR):
                if fp < 20:
                    for sub in range(2):
                        ft = 2 * fp + sub
                        grp = ft % 2
                        reg = psA[:, grp * 4 * BPF32:(grp * 4 + 4) * BPF32]
                        WT(v, ("p1qpre", ft))
                        W(v, ("p1ropeB", ft - 2))
                        for ch in range(4):
                            M[("p1evacC", ft, ch)] = s_dve.tick(v.tensor_copy(
                                u.qpre[ft % 2][:, ch * 512:(ch + 1) * 512],
                                psA[:, (grp * 4 + ch) * BPF32:(grp * 4 + ch + 1) * BPF32]))
                        M[("p1evac", ft)] = s_dve.cur()
                        WT(v, ("p1swap", ft))
                        W(v, ("p1spill", ft - 2))
                        M[("p1ropeA", ft)] = s_dve.tick(
                            v.tensor_mul(u.rtA[ft % 2][:], reg, u.sinp[:]))
                        mB = s_dve.tick(v.tensor_mul(u.rtB[ft % 2][:], u.qpre[ft % 2][:], u.cosp[:]))
                        M[("p1ropeB", ft)] = mB
                        WV(v, mB)
                        M[("p1rope", ft)] = s_dve.tick(
                            v.tensor_add(u.rout[ft % 2][:], u.rtA[ft % 2][:], u.rtB[ft % 2][:]))
                else:
                    for tt in range(NT):
                        slot = tt % 4
                        WT(v, ("p1v", fp, tt))
                        if tt >= 4:
                            W(v, ("p1vsp", fp, tt - 4))
                        elif fp > 20:
                            W(v, ("p1vsp", fp - 1, tt + 12))
                        M[("p1vev", fp, tt)] = s_dve.tick(
                            v.tensor_copy(u.vst[slot][:], bank(slot, 0, 256)))

        def p1_scal(sc, u):
            for fp in range(NPAIR):
                if fp < 20:
                    for sub in range(2):
                        ft = 2 * fp + sub
                        WT(sc, ("p1rope", ft))
                        dst = qsp_d[ft] if ft < NH else ksp_d[ft - NH]
                        M[("p1spill", ft)] = sm_rout[ft % 2].tick(
                            sc.dma_start(dst, u.rout[ft % 2][:]), 16)
                else:
                    for tt in range(NT):
                        WT(sc, ("p1vev", fp, tt))
                        M[("p1vsp", fp, tt)] = sm_vst[tt % 4].tick(sc.dma_start(
                            vsp_d[tt][:, (fp - 20) * 256:(fp - 19) * 256],
                            u.vst[tt % 4][:]), 16)
            M["p1_spills"] = ([s.cur() for s in sm_rout] + [s.cur() for s in sm_vst])
            WV(sc, M["p1_spills"])

        phase(p1_alloc, [("sync", p1_sync), ("gpsimd", p1_gp), ("tensor", p1_tens),
                         ("vector", p1_vec), ("scalar", p1_scal)])
        if upto == "p1":
            emit_drain()
            return

        def a_alloc(stL):
            m = mk(stL)
            return SimpleNamespace(
                qh=[m(f"qh{i}", [128, S], BF16) for i in range(2)],
                kg=[m(f"kg{i}", [128, S], BF16) for i in range(2)],
                vg=[m(f"vg{i}", [128, NT, 128], BF16) for i in range(2)],
                expS=[m(f"expS{i}", [128, NT, 512], BF16) for i in range(2)],
                btmp=[m(f"btmp{i}", [128, 512], F32) for i in range(2)],
                rcp=[m(f"rcp{i}", [1, 512], F32) for i in range(2)],
                mtmp=[m(f"mtmp{i}", [128, 128], BF16) for i in range(4)],
            )

        def tiles_of(c):
            return list(range(4 * c + 4))

        def a_sync(sy, u):
            for h in range(NH):
                WT(sy, ("p1spill", h))
                W(sy, ("a_hdone", h - 2))
                M[("a_ql", h)] = sm_qh[h % 2].tick(
                    sy.dma_start(u.qh[h % 2][:], qsp_d[h]), 16)

        def a_gp_ld(g_, u):
            for g in range(NKV):
                WT(g_, ("p1spill", NH + g))
                fp = 20 + g // 2
                for tt in range(NT):
                    WT(g_, ("p1vsp", fp, tt))
                W(g_, ("a_gdone", g - 2))
                mv = [sm_kg[g % 2].tick(g_.dma_start(u.kg[g % 2][:], ksp_d[g]), 16)]
                for half in range(2):
                    mv.append(sm_vg[(g % 2) * 2 + half].tick(g_.dma_start(
                        u.vg[g % 2][:, 8 * half:8 * half + 8, :],
                        vsp_d[8 * half:8 * half + 8, :, g * 128:(g + 1) * 128]
                        .rearrange("t p c -> p t c")), 16))
                M[("a_kvl", g)] = mv

        gsidx = [0]

        def em_scores(t, u, h, c):
            g = h // 4
            for k in tiles_of(c):
                d = k - 4 * c
                n0 = 128 * max(d, 0)
                gi = gsidx[0]
                sbank = 4 + (gi % 2)
                W(t, ("a_edone_gi", gi - 2))
                ins = t.matmul(bank(sbank, n0, 512),
                               u.kg[g % 2][:, k * 128:(k + 1) * 128],
                               u.qh[h % 2][:, c * 512 + n0:(c + 1) * 512],
                               start=True, stop=True)
                M[("a_s", h, c, k)] = s_pe.tick(ins)
                M[("a_gi_of", h, c, k)] = gi
                gsidx[0] += 1

        def em_pv(t, u, h, c):
            g = h // 4
            cb = c % 2
            ks = tiles_of(c)
            gc = 4 * h + c
            ins = None
            for k in ks:
                d = k - 4 * c
                n0 = 128 * max(d, 0)
                WT(t, ("a_e", h, c, k))
                if d >= 0:
                    WT(t, ("a_mk", h, c, k))
                if k == 0:
                    W(t, ("a_onG", gc - 2))
                ins = t.matmul(bank(cb, n0, 512),
                               u.vg[g % 2][:, k, :],
                               u.expS[cb][:, k, n0:512],
                               start=(k == 0), stop=(k == ks[-1]))
            M[("a_pv", h, c)] = s_pe.tick(ins)
            ins = None
            for k in ks:
                d = k - 4 * c
                n0 = 128 * max(d, 0)
                if k == 0:
                    W(t, ("a_rcpG", gc - 2))
                ins = t.matmul(bank(2 + cb, n0, 512, p0=0, p1=1),
                               onesc[:], u.expS[cb][:, k, n0:512],
                               start=(k == 0), stop=(k == ks[-1]))
            M[("a_dnG", gc)] = s_pe.tick(ins)

        def em_bc(t, u, h, c):
            cb = c % 2
            gc = 4 * h + c
            WT(t, ("a_rcpG", gc))
            W(t, ("a_onG", gc - 2))
            M[("a_bcG", gc)] = s_pe.tick(t.matmul(
                bank(6 + cb), onesr32[:], u.rcp[cb][0:1, :], start=True, stop=True))

        def a_tens(t, u):
            for h in range(NH):
                g = h // 4
                WT(t, ("a_ql", h))
                if h % 4 == 0:
                    WT(t, ("a_kvl", g))
                if h > 0:
                    em_bc(t, u, h - 1, 2)
                em_scores(t, u, h, 0)
                em_scores(t, u, h, 1)
                if h > 0:
                    em_bc(t, u, h - 1, 3)
                em_pv(t, u, h, 0)
                em_scores(t, u, h, 2)
                em_bc(t, u, h, 0)
                em_pv(t, u, h, 1)
                em_scores(t, u, h, 3)
                em_bc(t, u, h, 1)
                em_pv(t, u, h, 2)
                em_pv(t, u, h, 3)
                M[("a_hdone", h)] = s_pe.cur()
                if h % 4 == 3:
                    M[("a_gdone", g)] = s_pe.cur()
            em_bc(t, u, NH - 1, 2)
            em_bc(t, u, NH - 1, 3)

        def a_act(sc, u):
            for h in range(NH):
                for c in range(4):
                    cb = c % 2
                    gc = 4 * h + c
                    W(sc, ("a_dnG", gc - 2))
                    for k in tiles_of(c):
                        d = k - 4 * c
                        n0 = 128 * max(d, 0)
                        gi = M[("a_gi_of", h, c, k)]
                        sbank = 4 + ((gi or 0) % 2)
                        WT(sc, ("a_s", h, c, k))
                        if d >= 0:
                            if gc > 0:
                                ph, pc = divmod(gc - 1, 4)
                                W(sc, ("a_mk", ph, pc, 4 * pc + d))
                            M[("a_em", h, c, k)] = s_act.tick(sc.activation(
                                u.mtmp[d][:], bank(sbank, n0, n0 + 128),
                                AF.Exp, bias=cst(C_SHIFT), scale=cst(C_SMS)))
                            if n0 + 128 < 512:
                                M[("a_e", h, c, k)] = s_act.tick(sc.activation(
                                    u.expS[cb][:, k, n0 + 128:512],
                                    bank(sbank, n0 + 128, 512),
                                    AF.Exp, bias=cst(C_SHIFT), scale=cst(C_SMS)))
                            else:
                                M[("a_e", h, c, k)] = s_act.cur()
                        else:
                            M[("a_e", h, c, k)] = s_act.tick(sc.activation(
                                u.expS[cb][:, k, :], bank(sbank),
                                AF.Exp, bias=cst(C_SHIFT), scale=cst(C_SMS)))
                        M[("a_edone_gi", M[("a_gi_of", h, c, k)])] = s_act.cur()

        def a_vec(v, u):
            def masks(h, c):
                cb = c % 2
                for k in tiles_of(c):
                    d = k - 4 * c
                    if d >= 0:
                        n0 = 128 * d
                        WT(v, ("a_em", h, c, k))
                        M[("a_mk", h, c, k)] = s_dve.tick(v.tensor_mul(
                            u.expS[cb][:, k, n0:n0 + 128], u.mtmp[d][:], maskb[:]))

            def norm(h, c):
                cb = c % 2
                gc = 4 * h + c
                WT(v, ("a_dnG", gc))
                W(v, ("a_bcG", gc - 2))
                M[("a_rcpG", gc)] = s_dve.tick(
                    v.reciprocal(u.rcp[cb][0:1, :], bank(2 + cb, p0=0, p1=1)))
                WT(v, ("a_bcG", gc))
                W(v, ("a_onG", gc - 2))
                mcp = s_dve.tick(v.tensor_copy(u.btmp[cb][:], bank(6 + cb)))
                WV(v, mcp)
                M[("a_onG", gc)] = s_dve.tick(v.tensor_mul(
                    bigT[:, h, c * 512:(c + 1) * 512], bank(cb), u.btmp[cb][:]))

            for h in range(NH):
                masks(h, 0)
                if h > 0:
                    norm(h - 1, 3)
                masks(h, 1)
                norm(h, 0)
                masks(h, 2)
                norm(h, 1)
                masks(h, 3)
                norm(h, 2)
            norm(NH - 1, 3)

        phase(a_alloc, [("sync", a_sync), ("gpsimd", a_gp_ld),
                        ("tensor", a_tens), ("scalar", a_act), ("vector", a_vec)])
        if upto == "p2":
            emit_tap()
            emit_drain()
            return

        NDS = 16

        def wo_alloc(stL):
            m = mk(stL)
            return SimpleNamespace(
                wos=[m(f"wos{i}", [128, ND, 256], BF16) for i in range(2)],
                xst=[m(f"xst{i}", [128, 256], F32) for i in range(4)],
                x1st=[m(f"x1st{i}", [128, 256], F32) for i in range(2)],
            )

        def wo_gp(g, u):
            for ds in range(NDS):
                W(g, ("wo_pe", ds - 2))
                M[("wo_w", ds)] = [
                    sm_wos[(ds % 2) * 4 + jg].tick(g.dma_start(
                        u.wos[ds % 2][:, 8 * jg:8 * jg + 8, :],
                        wo3[:, 8 * jg:8 * jg + 8, 256 * ds:256 * (ds + 1)]), 16)
                    for jg in range(4)]

        def wo_sync(sy, u):
            for ds in range(NDS):
                for tt in range(NT):
                    i = ds * NT + tt
                    W(sy, ("wo_add", i - 4))
                    M[("wo_x", i)] = sm_xst[i % 4].tick(sy.dma_start(
                        u.xst[i % 4][:],
                        x_d[tt * 128:(tt + 1) * 128, 256 * ds:256 * (ds + 1)]), 16)

        def wo_tens(t, u):
            for ds in range(NDS):
                WT(t, ("wo_w", ds))
                for tt in range(NT):
                    i = ds * NT + tt
                    bk = tt % 2
                    W(t, ("wo_add", i - 2))
                    ins = None
                    for hh in range(NH):
                        ins = t.matmul(bank(bk, 0, 256),
                                       bigT[:, hh, tt * 128:(tt + 1) * 128],
                                       u.wos[ds % 2][:, hh, :],
                                       start=(hh == 0), stop=(hh == NH - 1))
                    M[("wo_mm", i)] = s_pe.tick(ins)
                M[("wo_pe", ds)] = s_pe.cur()

        def wo_vec(v, u):
            for ds in range(NDS):
                for tt in range(NT):
                    i = ds * NT + tt
                    bk = tt % 2
                    WT(v, ("wo_mm", i))
                    WT(v, ("wo_x", i))
                    W(v, ("wo_sp", i - 2))
                    M[("wo_add", i)] = s_dve.tick(v.tensor_add(
                        u.x1st[tt % 2][:], bank(bk, 0, 256), u.xst[i % 4][:]))

        def wo_scal(sc, u):
            for ds in range(NDS):
                for tt in range(NT):
                    i = ds * NT + tt
                    WT(sc, ("wo_add", i))
                    M[("wo_sp", i)] = sm_x1w[tt % 2].tick(sc.dma_start(
                        x1sp_d[tt * 128:(tt + 1) * 128, 256 * ds:256 * (ds + 1)],
                        u.x1st[tt % 2][:]), 16)
            M["x1_done"] = [s.cur() for s in sm_x1w]
            WV(sc, M["x1_done"])

        phase(wo_alloc, [("gpsimd", wo_gp), ("sync", wo_sync), ("tensor", wo_tens),
                         ("vector", wo_vec), ("scalar", wo_scal)])
        if upto == "p3a":
            emit_drain()
            return

        def p3b_sy(sy):
            WT(sy, "x1_done")

        phase(ln_alloc("ln2"), ln_bodies("ln2", x1sp_d, pre_sy=p3b_sy))
        if upto == "p3b":
            emit_tap()
            emit_drain()
            return

        NSC = (NIB + 3) // 4

        def p4_alloc(stL):
            m = mk(stL)
            return SimpleNamespace(
                qbg=[m(f"qbg{i}", [128, D], BF16) for i in range(3)],
                qbu=[m(f"qbu{i}", [128, D], BF16) for i in range(3)],
                wgT=m("wgT", [128, ND, 128], BF16),
                wuT=m("wuT", [128, ND, 128], BF16),
                sgb=[m(f"sgb{i}", [128, 4, 32], F32) for i in range(2)],
                zgb=[m(f"zgb{i}", [128, 4, 32], F32) for i in range(2)],
                sub=[m(f"sub{i}", [128, 4, 32], F32) for i in range(2)],
                zub=[m(f"zub{i}", [128, 4, 32], F32) for i in range(2)],
                nzg=[m(f"nzg{i}", [128, 4, 32], F32) for i in range(2)],
                nzu=[m(f"nzu{i}", [128, 4, 32], F32) for i in range(2)],
                silu=[m(f"silu{i}", [128, 512], BF16) for i in range(2)],
                hac=[m(f"hac{i}", [128, 1024], BF16) for i in range(2)],
            )

        def p4_sync(sy, u):
            def scl_load(sy, ac):
                if ac >= 2:
                    W(sy, ("p4dqa", 4 * (ac - 1) - 1))
                a0 = 4 * ac
                an = min(4, NIB - a0)
                M[("p4scl", ac)] = [
                    sm_scl[ai * 2 + ac % 2].tick(sy.dma_start(
                        dst[ac % 2][:, 0:an, :], src[:, a0:a0 + an, :]), 16)
                    for ai, (src, dst) in enumerate(
                        ((sg3, u.sgb), (zg3, u.zgb), (su3, u.sub), (zu3, u.zub)))]

            for a in range(NIB):
                if a == 0:
                    scl_load(sy, 0)
                    scl_load(sy, 1)
                elif (a + 2) % 4 == 0 and 2 <= (a + 2) // 4 < (NIB + 3) // 4:
                    scl_load(sy, (a + 2) // 4)
                for hf in range(2):
                    WT(sy, ("p4hh", a, hf))
                    M[("p4hspH", a, hf)] = sm_hw[(a % 2) * 2 + hf].tick(
                        sy.dma_start(hsp_d[a][:, hf * 1024:(hf + 1) * 1024],
                                     u.hac[a % 2][:]), 16)
            M["p4_spills"] = [s.cur() for s in sm_hw]
            WV(sy, M["p4_spills"])

        def p4_gp(g, u):
            for a in range(NIB):
                W(g, ("p4T", a - 3))
                M[("p4w", a)] = [
                    sm_qgu[0 + a % 3].tick(g.dma_start(
                        u.qbg[a % 3][:], qg_d[a * 128:(a + 1) * 128, :]), 16),
                    sm_qgu[3 + a % 3].tick(g.dma_start(
                        u.qbu[a % 3][:], qu_d[a * 128:(a + 1) * 128, :]), 16)]

        def p4_nz(v, u, a):
            ac = a // 4
            if a % 4 == 0:
                WT(v, ("p4scl", ac))
                if ac >= 2:
                    W(v, ("p4dqa", 4 * (ac - 1) - 1))
                s_dve.tick(v.scalar_tensor_tensor(
                    u.nzg[ac % 2][:], u.zgb[ac % 2][:], -1.0, u.sgb[ac % 2][:],
                    ALU.mult, ALU.mult))
                mz = s_dve.tick(v.scalar_tensor_tensor(
                    u.nzu[ac % 2][:], u.zub[ac % 2][:], -1.0, u.sub[ac % 2][:],
                    ALU.mult, ALU.mult))
                WV(v, mz)
                M[("p4nz", ac)] = mz

        def p4_dqa(sc, u, a, part):
            ac = a // 4
            js = list(range(ND))
            quarter = js[part * len(js) // 4:(part + 1) * len(js) // 4]
            if part == 0:
                WT(sc, ("p4w", a))
                WT(sc, ("p4nz", ac))
                W(sc, ("p4dqa", a - 1))
            ins = None
            for qb, sb, nzb in ((u.qbg, u.sgb, u.nzg), (u.qbu, u.sub, u.nzu)):
                for j in quarter:
                    ins = sc.activation(
                        qb[a % 3][:, j * 128:(j + 1) * 128],
                        qb[a % 3][:, j * 128:(j + 1) * 128],
                        AF.Identity, bias=nzb[ac % 2][:, a % 4, j:j + 1],
                        scale=sb[ac % 2][:, a % 4, j:j + 1])
            if part == 3:
                M[("p4dqa", a)] = s_act.tick(ins)
            else:
                s_act.tick(ins)

        GBK = (0, 4, 1, 5)
        UBK = (2, 6, 3, 7)

        def p4_vec(v, u):
            p4_nz(v, u, 0)
            for a in range(NIB):
                for tv in (0, 2, 1, 3, 4, 6, 5, 7):
                    WT(v, ("p4tp", a, 2 * tv))
                    WT(v, ("p4tp", a, 2 * tv + 1))
                    dstT = u.wgT if tv < 4 else u.wuT
                    j0 = (tv % 4) * 8
                    pb = 4 + 2 * (tv % 2)
                    src = psA[:, pb * BPF32:(pb + 2) * BPF32]
                    M[("p4tev2", a, tv)] = s_dve.tick(v.tensor_copy(
                        dstT[:, j0:j0 + 8, :], src.rearrange("p (q c) -> p q c", q=8)))
                M[("p4tevall", a)] = s_dve.cur()
                if a + 1 < NIB:
                    p4_nz(v, u, a + 1)
                for c in range(4):
                    WT(v, ("p4silu", a, c))
                    WT(v, ("p4u", a, c))
                    if c == 0:
                        W(v, ("p4hspH", a - 2, 1))
                    elif c == 2:
                        WT(v, ("p4hspH", a, 0))
                    M[("p4mul", a, c)] = s_dve.tick(v.tensor_mul(
                        u.hac[a % 2][:, (c % 2) * 512:(c % 2 + 1) * 512],
                        bank(UBK[c]), u.silu[c % 2][:]))
                    if c % 2 == 1:
                        M[("p4hh", a, c // 2)] = s_dve.cur()

        def p4_act(sc, u):
            for part in range(4):
                p4_dqa(sc, u, 0, part)
            for a in range(NIB):
                for c in range(4):
                    if a + 1 < NIB:
                        p4_dqa(sc, u, a + 1, c)
                    WT(sc, ("p4g", a, c))
                    if c >= 2:
                        W(sc, ("p4mul", a, c - 2))
                    elif a > 0:
                        W(sc, ("p4mul", a - 1, c + 2))
                    M[("p4silu", a, c)] = s_act.tick(sc.activation(
                        u.silu[c % 2][:], bank(GBK[c]), AF.Silu,
                        bias=cst(C_ZERO), scale=cst(C_ONE)))

        def p4_tens(t, u):
            P_LIST = (0, 1, 4, 5, 2, 3, 6, 7, 8, 9, 12, 13, 10, 11, 14, 15)
            for a in range(NIB):
                WT(t, ("p4dqa", a))
                if a > 0:
                    for c in range(4):
                        W(t, ("p4silu", a - 1, c))
                for k, gb in enumerate(P_LIST):
                    if k == 4 and a > 0:
                        for c in range(4):
                            W(t, ("p4mul", a - 1, c))
                    if gb >= 4:
                        W(t, ("p4tev2", a, (gb - 4) // 2))
                    else:
                        W(t, ("p4tev2", a - 1, (gb + 12) // 2))
                    src = u.qbg if gb < 8 else u.qbu
                    j0 = (gb % 8) * 4
                    pb = 4 + 2 * ((gb // 2) % 2)
                    ins = None
                    for q4 in range(4):
                        j = j0 + q4
                        ins = t.matmul(bank(pb + gb % 2, q4 * 128, (q4 + 1) * 128),
                                       src[a % 3][:, j * 128:(j + 1) * 128],
                                       ident[:], start=True, stop=True)
                    M[("p4tp", a, gb)] = s_pe.tick(ins)
                M[("p4T", a)] = s_pe.cur()
                for c in range(4):
                    if c == 0:
                        W(t, ("p4tev2", a, 3))
                    elif c == 1:
                        W(t, ("p4tev2", a, 6))
                    ins = None
                    for j in range(ND):
                        ins = t.matmul(bank(GBK[c]), u.wgT[:, j, :],
                                       bigT[:, j, c * 512:(c + 1) * 512],
                                       start=(j == 0), stop=(j == ND - 1))
                    M[("p4g", a, c)] = s_pe.tick(ins)
                    if c == 0:
                        WT(t, ("p4tevall", a))
                    ins = None
                    for j in range(ND):
                        ins = t.matmul(bank(UBK[c]), u.wuT[:, j, :],
                                       bigT[:, j, c * 512:(c + 1) * 512],
                                       start=(j == 0), stop=(j == ND - 1))
                    M[("p4u", a, c)] = s_pe.tick(ins)

        phase(p4_alloc, [("sync", p4_sync), ("gpsimd", p4_gp), ("vector", p4_vec),
                         ("scalar", p4_act), ("tensor", p4_tens)])
        if upto == "p4":
            emit_drain()
            return

        ctx["close_big"]()

        NS = len(SUPS)
        SL = [(q4, s) for q4 in range(4) for s in range(NS)]
        NGB = [(ss + 3) // 4 for ss in SUPS]

        def p5_li(idx, hh):
            return idx * 8 + hh

        def p5_gidx(idx, hh, gb):
            q4, s = SL[idx]
            per = [8 * NGB[SL[i][1]] for i in range(len(SL))]
            return sum(per[:idx]) + hh * NGB[s] + gb

        _gcum = [0]
        for _i in range(len(SL)):
            _gcum.append(_gcum[-1] + 8 * NGB[SL[_i][1]])

        def p5_gidx(idx, hh, gb):
            return _gcum[idx] + hh * NGB[SL[idx][1]] + gb

        def p5_lin(idx, tt, ch):
            return (idx * NT + tt) * 2 + ch

        def p5_alloc(stL):
            m = mk(stL)
            return SimpleNamespace(
                y2=m("y2", [128, NT * 1024], F32),
                wdT=[m(f"wdT{i}", [128, 8, 1024], BF16) for i in range(2)],
                hbuf=[m(f"hbuf{i}", [128, 8, 512], BF16) for i in range(2)],
                qdb=[m(f"qdb{i}", [128, 1024], BF16) for i in range(8)],
                sdb=m("sdb", [128, 32, NIB], F32),
                zdb=m("zdb", [128, 32, NIB], F32),
                nzd=[m(f"nzd{i}", [128, 8, 8], F32) for i in range(2)],
                outst=[m(f"outst{i}", [128, 1024], F32) for i in range(4)],
                x1b=[m(f"x1b{i}", [128, 1024], F32) for i in range(8)],
            )

        def p5_sync(sy, u):
            WT(sy, "x1_done")
            WT(sy, "p4_spills")
            M["p5sdz"] = (
                [sm_sdz[sl].tick(sy.dma_start(
                    u.sdb[:, 8 * sl:8 * sl + 8, :], sd3[:, 8 * sl:8 * sl + 8, :]), 16)
                 for sl in range(4)] +
                [sm_sdz[4 + sl].tick(sy.dma_start(
                    u.zdb[:, 8 * sl:8 * sl + 8, :], zd3[:, 8 * sl:8 * sl + 8, :]), 16)
                 for sl in range(4)])
            for idx, (q4, s) in enumerate(SL):
                ss = SUPS[s]
                for tg in range(4):
                    li4 = idx * 4 + tg
                    W(sy, ("p5mmG", li4 - 2))
                    nh = (ss + 1) // 2
                    M[("p5h", idx, tg)] = [
                        sm_hb[(tg % 2) * 2 + 0].tick(sy.dma_start(
                            u.hbuf[tg % 2][:, 0:nh, :],
                            hsp_d[8 * s:8 * s + nh, :, tg * 512:(tg + 1) * 512]
                            .rearrange("a p c -> p a c")), 16),
                        sm_hb[(tg % 2) * 2 + 1].tick(sy.dma_start(
                            u.hbuf[tg % 2][:, nh:ss, :],
                            hsp_d[8 * s + nh:8 * s + ss, :, tg * 512:(tg + 1) * 512]
                            .rearrange("a p c -> p a c")), 16)]
                    if idx == 0 and tg < 2:
                        for tt in (4 * tg, 4 * tg + 1, 4 * tg + 2, 4 * tg + 3):
                            M[("p5x1", 0, tt)] = sm_x1b[tt % 8].tick(sy.dma_start(
                                u.x1b[tt % 8][:],
                                x1sp_d[tt * 128:(tt + 1) * 128, 0:1024]), 16)

        def p5_gp(g, u):
            for idx, (q4, s) in enumerate(SL):
                ss = SUPS[s]
                cw = ss * 128
                for hh in range(8):
                    li = p5_li(idx, hh)
                    W(g, ("p5tpA", li - 8))
                    r0 = q4 * 1024 + hh * 128
                    M[("p5w", li)] = sm_qdb[hh].tick(g.dma_start(
                        u.qdb[hh][:, 0:cw],
                        qd_d[r0:r0 + 128, 1024 * s:1024 * s + cw]), 16)

        def p5_dq(v, u, idx):
            q4, s = SL[idx]
            ss = SUPS[s]
            if idx >= 2:
                W(v, ("p5dqaA", idx - 2))
            mz = s_dve.tick(v.scalar_tensor_tensor(
                u.nzd[idx % 2][:, :, 0:ss],
                u.zdb[:, q4 * 8:q4 * 8 + 8, 8 * s:8 * s + ss], -1.0,
                u.sdb[:, q4 * 8:q4 * 8 + 8, 8 * s:8 * s + ss],
                ALU.mult, ALU.mult))
            WV(v, mz)
            M[("p5nz", idx)] = mz

        def p5_dqa(sc, u, idx):
            q4, s = SL[idx]
            ss = SUPS[s]
            WT(sc, ("p5nz", idx))
            for hh in range(8):
                li = p5_li(idx, hh)
                WT(sc, ("p5w", li))
                ai = q4 * 8 + hh
                ins = None
                for gg in range(ss):
                    ins = sc.activation(
                        u.qdb[hh][:, gg * 128:(gg + 1) * 128],
                        u.qdb[hh][:, gg * 128:(gg + 1) * 128],
                        AF.Identity,
                        bias=u.nzd[idx % 2][:, hh, gg:gg + 1],
                        scale=u.sdb[:, ai, 8 * s + gg:8 * s + gg + 1])
                M[("p5dqa", li)] = s_act.tick(ins)
            M[("p5dqaA", idx)] = s_act.cur()

        def p5_tev(v, u, idx):
            q4, s = SL[idx]
            ss = SUPS[s]
            for hh in range(8):
                for gb in range(NGB[s]):
                    WT(v, ("p5tp", idx, hh, gb))
                pb = 4 + 2 * (hh % 2)
                M[("p5tevHL", idx * 8 + hh)] = s_dve.tick(v.tensor_copy(
                    u.wdT[idx % 2][:, 0:ss, hh * 128:(hh + 1) * 128],
                    psA[:, pb * BPF32:pb * BPF32 + ss * 128]
                    .rearrange("p (q c) -> p q c", q=ss)))
            M[("p5tev", idx)] = s_dve.cur()

        def p5_vec(v, u):
            WT(v, "p5sdz")
            p5_dq(v, u, 0)
            p5_tev(v, u, 0)
            p5_dq(v, u, 1)
            for idx, (q4, s) in enumerate(SL):
                for tg in range(4):
                    for ti in range(4):
                        tt = 4 * tg + ti
                        pb = 2 * (tt % 2)
                        WT(v, ("p5mm", idx, tt, 0))
                        WT(v, ("p5mm", idx, tt, 1))
                        if s > 0:
                            W(v, ("p5y2T", idx - 1, tt))
                        elif q4 > 0:
                            W(v, ("p5fin", q4 - 1, tt))
                        dst = u.y2[:, tt * 1024:(tt + 1) * 1024]
                        src_ = psA[:, pb * BPF32:(pb + 2) * BPF32]
                        if s == 0:
                            ins = v.tensor_copy(dst, src_)
                        else:
                            ins = v.tensor_add(dst, src_, dst)
                        my = s_dve.tick(ins)
                        M[("p5y2T", idx, tt)] = my
                        M[("p5y2L", idx * NT + tt)] = my
                        if s == NS - 1:
                            WV(v, my)
                            WT(v, ("p5x1", q4, tt))
                            W(v, ("p5out", q4, tt - 4))
                            if tt < 4 and q4 > 0:
                                W(v, ("p5out", q4 - 1, tt + 12))
                            M[("p5fin", q4, tt)] = s_dve.tick(v.tensor_add(
                                u.outst[tt % 4][:], dst, u.x1b[tt % 8][:]))
                if idx + 1 < len(SL):
                    p5_tev(v, u, idx + 1)
                if idx + 2 < len(SL):
                    p5_dq(v, u, idx + 2)

        def p5_act(sc, u):
            p5_dqa(sc, u, 0)
            p5_dqa(sc, u, 1)
            for idx, (q4, s) in enumerate(SL):
                if s == NS - 1:
                    for tt in range(NT):
                        WT(sc, ("p5fin", q4, tt))
                        M[("p5out", q4, tt)] = sm_out[tt % 4].tick(sc.dma_start(
                            y_d[tt * 128:(tt + 1) * 128, q4 * 1024:(q4 + 1) * 1024],
                            u.outst[tt % 4][:]), 16)
                        nq, ntt = (q4, tt + 8) if tt < 8 else (q4 + 1, tt - 8)
                        if nq < 4:
                            M[("p5x1", nq, ntt)] = sm_x1b[ntt % 8].tick(
                                sc.dma_start(
                                    u.x1b[ntt % 8][:],
                                    x1sp_d[ntt * 128:(ntt + 1) * 128,
                                           nq * 1024:(nq + 1) * 1024]), 16)
                if idx + 2 < len(SL):
                    p5_dqa(sc, u, idx + 2)
            WV(sc, [s.cur() for s in sm_out])

        def p5_T(t, u, idx):
            q4, s = SL[idx]
            ss = SUPS[s]
            for hh in range(8):
                li = p5_li(idx, hh)
                WT(t, ("p5dqa", li))
                pb = 4 + 2 * (hh % 2)
                W(t, ("p5tevHL", idx * 8 + hh - 2))
                for gb in range(NGB[s]):
                    it0 = 4 * gb
                    itn = min(4, ss - it0)
                    ins = None
                    for q in range(itn):
                        it = it0 + q
                        ins = t.matmul(
                            bank(pb + gb, q * 128, (q + 1) * 128),
                            u.qdb[hh][:, it * 128:(it + 1) * 128],
                            ident[:], start=True, stop=True)
                    M[("p5tp", idx, hh, gb)] = s_pe.tick(ins)
                M[("p5tpA", li)] = s_pe.cur()

        def p5_tens(t, u):
            p5_T(t, u, 0)
            for idx, (q4, s) in enumerate(SL):
                ss = SUPS[s]
                WT(t, ("p5tev", idx))
                for tg in range(4):
                    li4 = idx * 4 + tg
                    WT(t, ("p5h", idx, tg))
                    for ti in range(4):
                        tt = 4 * tg + ti
                        for ch in range(2):
                            bk = 2 * (tt % 2) + ch
                            if ch == 0:
                                W(t, ("p5y2L", idx * NT + tt - 2))
                            ins = None
                            for it in range(ss):
                                ins = t.matmul(
                                    bank(bk),
                                    u.hbuf[tg % 2][:, it, ti * 128:(ti + 1) * 128],
                                    u.wdT[idx % 2][:, it, ch * 512:(ch + 1) * 512],
                                    start=(it == 0), stop=(it == ss - 1))
                            M[("p5mm", idx, tt, ch)] = s_pe.tick(ins)
                    M[("p5mmG", li4)] = s_pe.cur()
                if idx + 1 < len(SL):
                    p5_T(t, u, idx + 1)

        phase(p5_alloc, [("sync", p5_sync), ("gpsimd", p5_gp), ("vector", p5_vec),
                         ("scalar", p5_act), ("tensor", p5_tens)])
        emit_drain()

    ctx = {"bigT": _Dummy(), "close_big": lambda: None}
    M.lenient = True
    phases(True, ctx)
    M.lenient = False

    stBig = ExitStack()
    bigT_real = stBig.enter_context(nc.sbuf_tensor("bigT", [128, ND, S], BF16))
    ctx = {"bigT": bigT_real, "close_big": stBig.close}
    phases(False, ctx)
    try:
        stBig.close()
    except Exception:
        pass
    st0.close()
    return nc


def _tables():
    inv = 10000.0 ** (-(np.arange(64, dtype=np.float64) * 2.0 / 128.0))
    ang = np.arange(S, dtype=np.float64)[None, :] * inv[:, None]
    cosp = np.repeat(np.cos(ang), 2, axis=0)
    sinp2 = np.empty((128, S), dtype=np.float64)
    sinp2[0::2] = -np.sin(ang)
    sinp2[1::2] = np.sin(ang)
    pswap = np.zeros((128, 128), dtype=np.float32)
    for k in range(64):
        pswap[2 * k, 2 * k + 1] = 1.0
        pswap[2 * k + 1, 2 * k] = 1.0
    ident = np.eye(128, dtype=np.float32)
    mask = np.triu(np.ones((128, 128), dtype=np.float32))
    bf = ml_dtypes.bfloat16
    return (cosp.astype(bf), sinp2.astype(bf), pswap.astype(bf),
            ident.astype(bf), mask.astype(bf))


def make_inmaps(inputs):
    x = np.asarray(inputs["L0_attn_x"])
    cosp, sinp, pswap, ident, maskb = _tables()
    shared = {
        "wqkv": np.ascontiguousarray(np.asarray(inputs["L0_attn_w_qkv"])),
        "wo": np.ascontiguousarray(np.asarray(inputs["L0_attn_w_o"])),
        "qg": np.ascontiguousarray(np.asarray(inputs["L0_ffn_qw_gate"])),
        "qu": np.ascontiguousarray(np.asarray(inputs["L0_ffn_qw_up"])),
        "qd": np.ascontiguousarray(np.asarray(inputs["L0_ffn_qw_down"])),
        "sg": np.ascontiguousarray(np.asarray(inputs["L0_ffn_s_gate"])),
        "zg": np.ascontiguousarray(np.asarray(inputs["L0_ffn_z_gate"])),
        "su": np.ascontiguousarray(np.asarray(inputs["L0_ffn_s_up"])),
        "zu": np.ascontiguousarray(np.asarray(inputs["L0_ffn_z_up"])),
        "sd": np.ascontiguousarray(np.asarray(inputs["L0_ffn_s_down"])),
        "zd": np.ascontiguousarray(np.asarray(inputs["L0_ffn_z_down"])),
        "cosp": cosp, "sinp": sinp, "pswap": pswap,
        "ident": ident, "maskb": maskb,
    }
    return [dict(shared, x=np.ascontiguousarray(x[c])) for c in range(8)]


def kernel(**inputs):
    nc = build_nc("full")
    in_maps = make_inmaps(inputs)
    res = run_bass_kernel_spmd(nc, in_maps, core_ids=list(range(8)))
    return np.stack([np.asarray(res.results[c]["y"]) for c in range(8)], axis=0)


# version 14
# speedup vs baseline: 1.0657x; 1.0278x over previous
import math
from contextlib import ExitStack
from types import SimpleNamespace

import numpy as np
import ml_dtypes

import concourse.bass as bass
import concourse.mybir as mybir
from concourse.bass_utils import run_bass_kernel_spmd

F32 = mybir.dt.float32
BF16 = mybir.dt.bfloat16
I32 = mybir.dt.int32
AF = mybir.ActivationFunctionType
ALU = mybir.AluOpType
AX = mybir.AxisListType

S = 2048
D = 4096
NH, NKV, HD = 32, 8, 128
FQKV = 6144
NT = S // 128
ND = D // 128
IFF = 11008
NIB = IFF // 128
SUPS = [8] * 10 + [6]
EXP_SHIFT = -4.0
SM_SCALE = 1.0 / math.sqrt(HD)
BPF32 = 512
C_ONE, C_ZERO, C_EPS, C_NHALF, C_SHIFT, C_SMS = range(6)


class _DryInst:
    def then_inc(self, sem, k):
        return self


class _DryEng:
    def wait_ge(self, *a, **k):
        return None

    def __getattr__(self, name):
        return lambda *a, **k: _DryInst()


class _Dummy:
    def __getitem__(self, *_):
        return self

    def rearrange(self, *a, **k):
        return self


class Marks(dict):
    lenient = True

    def __setitem__(self, k, v):
        if not self.lenient and k in self and self[k] != v:
            raise AssertionError(f"mark mismatch {k}: {self[k]} vs {v}")
        dict.__setitem__(self, k, v)

    def __missing__(self, k):
        if self.lenient:
            return None
        raise KeyError(k)


def build_nc(upto="full"):
    nc = bass.Bass("TRN2", target_bir_lowering=False)

    x_d = nc.dram_tensor("x", [S, D], F32, kind="ExternalInput").ap()
    wqkv_d = nc.dram_tensor("wqkv", [D, FQKV], F32, kind="ExternalInput").ap()
    wo_d = nc.dram_tensor("wo", [D, D], F32, kind="ExternalInput").ap()
    qg_d = nc.dram_tensor("qg", [IFF, D], I32, kind="ExternalInput").ap()
    qu_d = nc.dram_tensor("qu", [IFF, D], I32, kind="ExternalInput").ap()
    qd_d = nc.dram_tensor("qd", [D, IFF], I32, kind="ExternalInput").ap()
    sg_d = nc.dram_tensor("sg", [IFF, 32], F32, kind="ExternalInput").ap()
    zg_d = nc.dram_tensor("zg", [IFF, 32], F32, kind="ExternalInput").ap()
    su_d = nc.dram_tensor("su", [IFF, 32], F32, kind="ExternalInput").ap()
    zu_d = nc.dram_tensor("zu", [IFF, 32], F32, kind="ExternalInput").ap()
    sd_d = nc.dram_tensor("sd", [D, NIB], F32, kind="ExternalInput").ap()
    zd_d = nc.dram_tensor("zd", [D, NIB], F32, kind="ExternalInput").ap()
    cosp_d = nc.dram_tensor("cosp", [128, S], BF16, kind="ExternalInput").ap()
    sinp_d = nc.dram_tensor("sinp", [128, S], BF16, kind="ExternalInput").ap()
    pswap_d = nc.dram_tensor("pswap", [128, 128], BF16, kind="ExternalInput").ap()
    ident_d = nc.dram_tensor("ident", [128, 128], BF16, kind="ExternalInput").ap()
    maskb_d = nc.dram_tensor("maskb", [128, 128], BF16, kind="ExternalInput").ap()
    y_d = nc.dram_tensor("y", [S, D], F32, kind="ExternalOutput").ap()
    qsp_d = nc.dram_tensor("qsp", [NH, 128, S], BF16).ap()
    ksp_d = nc.dram_tensor("ksp", [NKV, 128, S], BF16).ap()
    vsp_d = nc.dram_tensor("vsp", [NT, 128, NKV * 128], BF16).ap()
    x1sp_d = nc.dram_tensor("x1sp", [S, D], F32).ap()
    hsp_d = nc.dram_tensor("hsp", [NIB, 128, S], BF16).ap()
    tap_d = None
    if upto in ("p0", "p2", "p3b"):
        tap_d = nc.dram_tensor("tap", [128, ND * S], BF16, kind="ExternalOutput").ap()

    wq3 = wqkv_d.rearrange("(j p) c -> p j c", p=128)
    wo3 = wo_d.rearrange("(j p) c -> p j c", p=128)
    sg3 = sg_d.rearrange("(a p) g -> p a g", p=128)
    zg3 = zg_d.rearrange("(a p) g -> p a g", p=128)
    su3 = su_d.rearrange("(a p) g -> p a g", p=128)
    zu3 = zu_d.rearrange("(a p) g -> p a g", p=128)
    sd3 = sd_d.rearrange("(a p) g -> p a g", p=128)
    zd3 = zd_d.rearrange("(a p) g -> p a g", p=128)

    st0 = ExitStack()
    E0 = st0.enter_context
    psA = E0(nc.psum_tensor("psA", [128, 8 * BPF32], F32))
    consts = E0(nc.sbuf_tensor("consts", [128, 8], F32))
    onesc = E0(nc.sbuf_tensor("onesc", [128, 1], BF16))
    onesr = E0(nc.sbuf_tensor("onesr", [1, 128], BF16))
    onesr32 = E0(nc.sbuf_tensor("onesr32", [1, 128], F32))
    pswap = E0(nc.sbuf_tensor("pswap_s", [128, 128], BF16))
    ident = E0(nc.sbuf_tensor("ident_s", [128, 128], BF16))
    maskb = E0(nc.sbuf_tensor("maskb_s", [128, 128], BF16))

    REG = []

    class Sem:
        def __init__(self, name):
            self.h = st0.enter_context(nc.semaphore(name))
            self.n = 0
            self.idx = len(REG)
            REG.append(self)

        def tick(self, inst, k=1):
            if not isinstance(inst, _DryInst):
                inst.then_inc(self.h, k)
            self.n += k
            return (self.idx, self.n)

        def cur(self):
            return (self.idx, self.n)

    def bankS(name, n):
        return [Sem(f"{name}{i}") for i in range(n)]

    s_pe = Sem("s_pe")
    s_dve = Sem("s_dve")
    s_act = Sem("s_act")
    sm_tbl = bankS("m_tbl", 3)
    sm_x = bankS("m_x", 2)
    sm_cs = bankS("m_cs", 2)
    sm_wp = bankS("m_wp", 8)
    sm_rout = bankS("m_rout", 2)
    sm_vst = bankS("m_vst", 4)
    sm_qh = bankS("m_qh", 2)
    sm_kg = bankS("m_kg", 2)
    sm_vg = bankS("m_vg", 4)
    sm_wos = bankS("m_wos", 8)
    sm_xst = bankS("m_xst", 4)
    sm_x1w = bankS("m_x1w", 2)
    sm_qgu = bankS("m_qgu", 6)
    sm_scl = bankS("m_scl", 8)
    sm_hw = bankS("m_hw", 4)
    sm_sdz = bankS("m_sdz", 8)
    sm_hb = bankS("m_hb", 4)
    sm_qdb = bankS("m_qdb", 8)
    sm_x1b = bankS("m_x1b", 8)
    sm_out = bankS("m_out", 4)
    sm_tap = bankS("m_tap", 2)

    M = Marks()

    def bank(b, lo=0, hi=BPF32, p0=0, p1=128):
        assert 0 <= b < 8 and 0 <= lo < hi <= BPF32
        return psA[p0:p1, b * BPF32 + lo: b * BPF32 + hi]

    def cst(i):
        return consts[:, i:i + 1]

    def _aslist(mv):
        return mv if isinstance(mv, list) else [mv]

    def WV(eng, mv):
        if mv is None:
            return
        for si, v in _aslist(mv):
            eng.wait_ge(REG[si].h, v)

    def W(eng, key):
        WV(eng, dict.get(M, key))

    def WT(eng, key):
        mv = dict.get(M, key)
        if mv is None:
            assert M.lenient, f"missing required mark {key}"
            return
        WV(eng, mv)

    def phases(dry, ctx):
        for s in REG:
            s.n = 0

        def phase(allocfn, bodies):
            if dry:
                bufs = allocfn(None)
                for _, fn in bodies:
                    fn(_DryEng(), bufs)
            else:
                with ExitStack() as stL:
                    bufs = allocfn(stL)
                    with nc.Block() as blk:
                        for name, fn in bodies:
                            getattr(blk, name)(
                                lambda e, fn=fn, bufs=bufs: fn(e, bufs))

        def mk(stL):
            if stL is None:
                return lambda n, sh, dt: _Dummy()
            return lambda n, sh, dt: stL.enter_context(nc.sbuf_tensor(n, sh, dt))

        bigT = ctx["bigT"]

        def ln_alloc(tag):
            def f(stL):
                m = mk(stL)
                return SimpleNamespace(
                    xrow=[m(f"{tag}_xr{i}", [128, D], F32) for i in range(2)],
                    xn=[m(f"{tag}_xn{i}", [128, D], BF16) for i in range(2)],
                    trash=m(f"{tag}_tr", [128, D], F32),
                    cols=m(f"{tag}_cols", [128, 20], F32),
                )
            return f

        def ln_bodies(tag, src_d, first=False, pre_sy=None, pre_gp=None):
            def b_sync(sy, u):
                if pre_sy:
                    pre_sy(sy)
                for tt in range(NT):
                    W(sy, (tag, "xn", tt - 2))
                    M[(tag, "xld", tt)] = sm_x[tt % 2].tick(
                        sy.dma_start(u.xrow[tt % 2][:],
                                     src_d[tt * 128:(tt + 1) * 128, :]), 16)

            def b_gp(g, u):
                if pre_gp:
                    pre_gp(g)

            def b_vec(v, u):
                cols = u.cols
                if first:
                    for i, val in ((C_ONE, 1.0), (C_ZERO, 0.0), (C_EPS, 1e-5),
                                   (C_NHALF, -0.5), (C_SHIFT, EXP_SHIFT),
                                   (C_SMS, SM_SCALE)):
                        v.memset(consts[:, i:i + 1], val)
                    v.memset(onesc[:], 1.0)
                    v.memset(onesr[:], 1.0)
                    M["consts"] = s_dve.tick(v.memset(onesr32[:], 1.0))
                for tt in range(NT):
                    b = tt % 2
                    o = 10 * b
                    WT(v, (tag, "xld", tt))
                    W(v, (tag, "xn", tt - 2))
                    m1 = s_dve.tick(v.reduce_sum(cols[:, o:o + 1], u.xrow[b][:], axis=AX.X))
                    WV(v, m1)
                    s_dve.tick(v.tensor_scalar_mul(cols[:, o + 2:o + 3], cols[:, o:o + 1], 1.0 / D))
                    WT(v, (tag, "sq", tt))
                    m4 = s_dve.tick(v.tensor_scalar_mul(cols[:, o + 3:o + 4], cols[:, o + 1:o + 2], 1.0 / D))
                    WV(v, m4)
                    m5 = s_dve.tick(v.tensor_mul(cols[:, o + 4:o + 5], cols[:, o + 2:o + 3], cols[:, o + 2:o + 3]))
                    WV(v, m5)
                    M[(tag, "var", tt)] = s_dve.tick(
                        v.tensor_sub(cols[:, o + 5:o + 6], cols[:, o + 3:o + 4], cols[:, o + 4:o + 5]))
                    WT(v, (tag, "rstd", tt))
                    M[(tag, "negmr", tt)] = s_dve.tick(v.scalar_tensor_tensor(
                        cols[:, o + 8:o + 9], cols[:, o + 2:o + 3], -1.0,
                        cols[:, o + 7:o + 8], ALU.mult, ALU.mult))
                    for jb in range(8):
                        WT(v, (tag, "tp", tt, jb))
                        src = psA[:, (6 + jb % 2) * BPF32:(7 + jb % 2) * BPF32]
                        M[(tag, "ev", tt, jb)] = s_dve.tick(v.tensor_copy(
                            bigT[:, 4 * jb:4 * jb + 4, tt * 128:(tt + 1) * 128],
                            src.rearrange("p (a c) -> p a c", a=4)))

            def b_scal(sc, u):
                cols = u.cols
                if first:
                    WT(sc, "consts")
                for tt in range(NT):
                    b = tt % 2
                    o = 10 * b
                    WT(sc, (tag, "xld", tt))
                    W(sc, (tag, "xn", tt - 2))
                    M[(tag, "sq", tt)] = s_act.tick(sc.activation(
                        u.trash[:], u.xrow[b][:], AF.Square,
                        bias=cst(C_ZERO), scale=cst(C_ONE),
                        accum_out=cols[:, o + 1:o + 2]))
                    WT(sc, (tag, "var", tt))
                    a1 = s_act.tick(sc.activation(cols[:, o + 6:o + 7], cols[:, o + 5:o + 6],
                                                  AF.Ln, bias=cst(C_EPS), scale=cst(C_ONE)))
                    WV(sc, a1)
                    M[(tag, "rstd", tt)] = s_act.tick(sc.activation(
                        cols[:, o + 7:o + 8], cols[:, o + 6:o + 7],
                        AF.Exp, bias=cst(C_ZERO), scale=cst(C_NHALF)))
                    WT(sc, (tag, "negmr", tt))
                    for q in range(4):
                        M[(tag, "xnq", tt, q)] = s_act.tick(sc.activation(
                            u.xn[b][:, q * 1024:(q + 1) * 1024],
                            u.xrow[b][:, q * 1024:(q + 1) * 1024], AF.Identity,
                            bias=cols[:, o + 8:o + 9], scale=cols[:, o + 7:o + 8]))
                    M[(tag, "xn", tt)] = s_act.cur()

            def b_tens(t, u):
                if first:
                    WT(t, "consts")
                    WT(t, "tbl")
                for tt in range(NT):
                    b = tt % 2
                    for jb in range(8):
                        WT(t, (tag, "xnq", tt, jb // 2))
                        bk = 6 + jb % 2
                        if jb >= 2:
                            W(t, (tag, "ev", tt, jb - 2))
                        else:
                            W(t, (tag, "ev", tt - 1, jb + 6))
                        ins = None
                        for u4 in range(4):
                            j = 4 * jb + u4
                            ins = t.matmul(bank(bk, u4 * 128, (u4 + 1) * 128),
                                           u.xn[b][:, j * 128:(j + 1) * 128],
                                           ident[:], start=True, stop=True)
                        M[(tag, "tp", tt, jb)] = s_pe.tick(ins)

            return [("sync", b_sync), ("gpsimd", b_gp), ("vector", b_vec),
                    ("scalar", b_scal), ("tensor", b_tens)]

        def p0_gp(g):
            M["tbl"] = [
                sm_tbl[0].tick(g.dma_start(pswap[:], pswap_d), 16),
                sm_tbl[1].tick(g.dma_start(ident[:], ident_d), 16),
                sm_tbl[2].tick(g.dma_start(maskb[:], maskb_d), 16),
            ]

        def emit_tap():
            def tap_body(sy, u):
                fins = []
                for j in range(ND):
                    if j >= 2:
                        WV(sy, fins[j - 2])
                    fins.append(sm_tap[j % 2].tick(
                        sy.dma_start(tap_d[:, j * S:(j + 1) * S], bigT[:, j, :]), 16))
                WV(sy, fins[-2])
                WV(sy, fins[-1])

            phase(lambda stL: None, [("sync", tap_body)])

        def emit_drain():
            def drain_body(sy, u):
                WV(sy, [s.cur() for s in REG])

            phase(lambda stL: None, [("sync", drain_body)])

        phase(ln_alloc("ln1"), ln_bodies("ln1", x_d, first=True, pre_gp=p0_gp))
        if upto == "p0":
            emit_tap()
            emit_drain()
            return

        NPAIR = FQKV // 256

        def p1_alloc(stL):
            m = mk(stL)
            return SimpleNamespace(
                wpair=[m(f"wp{i}", [128, ND, 256], BF16) for i in range(2)],
                qpre=[m(f"qpre{i}", [128, S], BF16) for i in range(2)],
                rtA=[m(f"rtA{i}", [128, S], BF16) for i in range(2)],
                rtB=[m(f"rtB{i}", [128, S], BF16) for i in range(2)],
                rout=[m(f"rout{i}", [128, S], BF16) for i in range(2)],
                vst=[m(f"vst{i}", [128, 256], BF16) for i in range(4)],
                cosp=m("cosp_s", [128, S], BF16),
                sinp=m("sinp_s", [128, S], BF16),
            )

        def p1_sync(sy, u):
            M["cos"] = [
                sm_cs[0].tick(sy.dma_start(u.cosp[:], cosp_d), 16),
                sm_cs[1].tick(sy.dma_start(u.sinp[:], sinp_d), 16),
            ]

        def p1_gp(g, u):
            for fp in range(NPAIR):
                W(g, ("p1pe", fp - 2))
                M[("wp", fp)] = [
                    sm_wp[(fp % 2) * 4 + jg].tick(g.dma_start(
                        u.wpair[fp % 2][:, 8 * jg:8 * jg + 8, :],
                        wq3[:, 8 * jg:8 * jg + 8, 256 * fp:256 * (fp + 1)]), 16)
                    for jg in range(4)]

        def p1_tens(t, u):
            WT(t, "cos")
            for fp in range(NPAIR):
                WT(t, ("wp", fp))
                if fp < 20:
                    for sub in range(2):
                        ft = 2 * fp + sub
                        grp = ft % 2
                        W(t, ("p1ropeA", ft - 2))
                        ins = None
                        for j in range(ND):
                            for ch in range(4):
                                ins = t.matmul(
                                    bank(grp * 4 + ch),
                                    u.wpair[fp % 2][:, j, sub * 128:(sub + 1) * 128],
                                    bigT[:, j, ch * 512:(ch + 1) * 512],
                                    start=(j == 0), stop=(j == ND - 1))
                        M[("p1qpre", ft)] = s_pe.tick(ins)
                        ins = None
                        for ch in range(4):
                            WT(t, ("p1evacC", ft, ch))
                            ins = t.matmul(bank(grp * 4 + ch), pswap[:],
                                           u.qpre[ft % 2][:, ch * 512:(ch + 1) * 512],
                                           start=True, stop=True)
                        M[("p1swap", ft)] = s_pe.tick(ins)
                    M[("p1pe", fp)] = s_pe.cur()
                else:
                    if fp == 20:
                        WT(t, ("p1ropeA", 38))
                    for tt in range(NT):
                        slot = tt % 4
                        if tt >= 4:
                            W(t, ("p1vev", fp, tt - 4))
                        elif fp > 20:
                            W(t, ("p1vev", fp - 1, tt + 12))
                        ins = None
                        for j in range(ND):
                            ins = t.matmul(bank(slot, 0, 256),
                                           bigT[:, j, tt * 128:(tt + 1) * 128],
                                           u.wpair[fp % 2][:, j, :],
                                           start=(j == 0), stop=(j == ND - 1))
                        M[("p1v", fp, tt)] = s_pe.tick(ins)
                    M[("p1pe", fp)] = s_pe.cur()

        def p1_vec(v, u):
            for fp in range(NPAIR):
                if fp < 20:
                    for sub in range(2):
                        ft = 2 * fp + sub
                        grp = ft % 2
                        reg = psA[:, grp * 4 * BPF32:(grp * 4 + 4) * BPF32]
                        WT(v, ("p1qpre", ft))
                        W(v, ("p1ropeB", ft - 2))
                        for ch in range(4):
                            M[("p1evacC", ft, ch)] = s_dve.tick(v.tensor_copy(
                                u.qpre[ft % 2][:, ch * 512:(ch + 1) * 512],
                                psA[:, (grp * 4 + ch) * BPF32:(grp * 4 + ch + 1) * BPF32]))
                        M[("p1evac", ft)] = s_dve.cur()
                        WT(v, ("p1swap", ft))
                        W(v, ("p1spill", ft - 2))
                        M[("p1ropeA", ft)] = s_dve.tick(
                            v.tensor_mul(u.rtA[ft % 2][:], reg, u.sinp[:]))
                        mB = s_dve.tick(v.tensor_mul(u.rtB[ft % 2][:], u.qpre[ft % 2][:], u.cosp[:]))
                        M[("p1ropeB", ft)] = mB
                        WV(v, mB)
                        M[("p1rope", ft)] = s_dve.tick(
                            v.tensor_add(u.rout[ft % 2][:], u.rtA[ft % 2][:], u.rtB[ft % 2][:]))
                else:
                    for tt in range(NT):
                        slot = tt % 4
                        WT(v, ("p1v", fp, tt))
                        if tt >= 4:
                            W(v, ("p1vsp", fp, tt - 4))
                        elif fp > 20:
                            W(v, ("p1vsp", fp - 1, tt + 12))
                        M[("p1vev", fp, tt)] = s_dve.tick(
                            v.tensor_copy(u.vst[slot][:], bank(slot, 0, 256)))

        def p1_scal(sc, u):
            for fp in range(NPAIR):
                if fp < 20:
                    for sub in range(2):
                        ft = 2 * fp + sub
                        WT(sc, ("p1rope", ft))
                        dst = qsp_d[ft] if ft < NH else ksp_d[ft - NH]
                        M[("p1spill", ft)] = sm_rout[ft % 2].tick(
                            sc.dma_start(dst, u.rout[ft % 2][:]), 16)
                else:
                    for tt in range(NT):
                        WT(sc, ("p1vev", fp, tt))
                        M[("p1vsp", fp, tt)] = sm_vst[tt % 4].tick(sc.dma_start(
                            vsp_d[tt][:, (fp - 20) * 256:(fp - 19) * 256],
                            u.vst[tt % 4][:]), 16)
            M["p1_spills"] = ([s.cur() for s in sm_rout] + [s.cur() for s in sm_vst])
            WV(sc, M["p1_spills"])

        phase(p1_alloc, [("sync", p1_sync), ("gpsimd", p1_gp), ("tensor", p1_tens),
                         ("vector", p1_vec), ("scalar", p1_scal)])
        if upto == "p1":
            emit_drain()
            return

        def a_alloc(stL):
            m = mk(stL)
            return SimpleNamespace(
                qh=[m(f"qh{i}", [128, S], BF16) for i in range(2)],
                kg=[m(f"kg{i}", [128, S], BF16) for i in range(2)],
                vg=[m(f"vg{i}", [128, NT, 128], BF16) for i in range(2)],
                expS=[m(f"expS{i}", [128, NT, 512], BF16) for i in range(2)],
                btmp=[m(f"btmp{i}", [128, 512], F32) for i in range(2)],
                rcp=[m(f"rcp{i}", [1, 512], F32) for i in range(2)],
                mtmp=[m(f"mtmp{i}", [128, 128], BF16) for i in range(4)],
            )

        def tiles_of(c):
            return list(range(4 * c + 4))

        def a_sync(sy, u):
            for h in range(NH):
                WT(sy, ("p1spill", h))
                W(sy, ("a_hdone", h - 2))
                M[("a_ql", h)] = sm_qh[h % 2].tick(
                    sy.dma_start(u.qh[h % 2][:], qsp_d[h]), 16)

        def a_gp_ld(g_, u):
            for g in range(NKV):
                WT(g_, ("p1spill", NH + g))
                fp = 20 + g // 2
                for tt in range(NT):
                    WT(g_, ("p1vsp", fp, tt))
                W(g_, ("a_gdone", g - 2))
                mv = [sm_kg[g % 2].tick(g_.dma_start(u.kg[g % 2][:], ksp_d[g]), 16)]
                for half in range(2):
                    mv.append(sm_vg[(g % 2) * 2 + half].tick(g_.dma_start(
                        u.vg[g % 2][:, 8 * half:8 * half + 8, :],
                        vsp_d[8 * half:8 * half + 8, :, g * 128:(g + 1) * 128]
                        .rearrange("t p c -> p t c")), 16))
                M[("a_kvl", g)] = mv

        gsidx = [0]

        def em_scores(t, u, h, c):
            g = h // 4
            for k in tiles_of(c):
                d = k - 4 * c
                n0 = 128 * max(d, 0)
                gi = gsidx[0]
                sbank = 4 + (gi % 2)
                W(t, ("a_edone_gi", gi - 2))
                ins = t.matmul(bank(sbank, n0, 512),
                               u.kg[g % 2][:, k * 128:(k + 1) * 128],
                               u.qh[h % 2][:, c * 512 + n0:(c + 1) * 512],
                               start=True, stop=True)
                M[("a_s", h, c, k)] = s_pe.tick(ins)
                M[("a_gi_of", h, c, k)] = gi
                gsidx[0] += 1

        def em_pv(t, u, h, c):
            g = h // 4
            cb = c % 2
            ks = tiles_of(c)
            gc = 4 * h + c
            ins = None
            for k in ks:
                d = k - 4 * c
                n0 = 128 * max(d, 0)
                WT(t, ("a_e", h, c, k))
                if d >= 0:
                    WT(t, ("a_mk", h, c, k))
                if k == 0:
                    W(t, ("a_onG", gc - 2))
                ins = t.matmul(bank(cb, n0, 512),
                               u.vg[g % 2][:, k, :],
                               u.expS[cb][:, k, n0:512],
                               start=(k == 0), stop=(k == ks[-1]))
            M[("a_pv", h, c)] = s_pe.tick(ins)
            ins = None
            for k in ks:
                d = k - 4 * c
                n0 = 128 * max(d, 0)
                if k == 0:
                    W(t, ("a_rcpG", gc - 2))
                ins = t.matmul(bank(2 + cb, n0, 512, p0=0, p1=1),
                               onesc[:], u.expS[cb][:, k, n0:512],
                               start=(k == 0), stop=(k == ks[-1]))
            M[("a_dnG", gc)] = s_pe.tick(ins)

        def em_bc(t, u, h, c):
            cb = c % 2
            gc = 4 * h + c
            WT(t, ("a_rcpG", gc))
            W(t, ("a_onG", gc - 2))
            M[("a_bcG", gc)] = s_pe.tick(t.matmul(
                bank(6 + cb), onesr32[:], u.rcp[cb][0:1, :], start=True, stop=True))

        def a_tens(t, u):
            for h in range(NH):
                g = h // 4
                WT(t, ("a_ql", h))
                if h % 4 == 0:
                    WT(t, ("a_kvl", g))
                if h > 0:
                    em_bc(t, u, h - 1, 2)
                em_scores(t, u, h, 0)
                em_scores(t, u, h, 1)
                if h > 0:
                    em_bc(t, u, h - 1, 3)
                em_pv(t, u, h, 0)
                em_scores(t, u, h, 2)
                em_bc(t, u, h, 0)
                em_pv(t, u, h, 1)
                em_scores(t, u, h, 3)
                em_bc(t, u, h, 1)
                em_pv(t, u, h, 2)
                em_pv(t, u, h, 3)
                M[("a_hdone", h)] = s_pe.cur()
                if h % 4 == 3:
                    M[("a_gdone", g)] = s_pe.cur()
            em_bc(t, u, NH - 1, 2)
            em_bc(t, u, NH - 1, 3)

        def a_act(sc, u):
            for h in range(NH):
                for c in range(4):
                    cb = c % 2
                    gc = 4 * h + c
                    W(sc, ("a_dnG", gc - 2))
                    for k in tiles_of(c):
                        d = k - 4 * c
                        n0 = 128 * max(d, 0)
                        gi = M[("a_gi_of", h, c, k)]
                        sbank = 4 + ((gi or 0) % 2)
                        WT(sc, ("a_s", h, c, k))
                        if d >= 0:
                            if gc > 0:
                                ph, pc = divmod(gc - 1, 4)
                                W(sc, ("a_mk", ph, pc, 4 * pc + d))
                            M[("a_em", h, c, k)] = s_act.tick(sc.activation(
                                u.mtmp[d][:], bank(sbank, n0, n0 + 128),
                                AF.Exp, bias=cst(C_SHIFT), scale=cst(C_SMS)))
                            if n0 + 128 < 512:
                                M[("a_e", h, c, k)] = s_act.tick(sc.activation(
                                    u.expS[cb][:, k, n0 + 128:512],
                                    bank(sbank, n0 + 128, 512),
                                    AF.Exp, bias=cst(C_SHIFT), scale=cst(C_SMS)))
                            else:
                                M[("a_e", h, c, k)] = s_act.cur()
                        else:
                            M[("a_e", h, c, k)] = s_act.tick(sc.activation(
                                u.expS[cb][:, k, :], bank(sbank),
                                AF.Exp, bias=cst(C_SHIFT), scale=cst(C_SMS)))
                        M[("a_edone_gi", M[("a_gi_of", h, c, k)])] = s_act.cur()

        def a_vec(v, u):
            def masks(h, c):
                cb = c % 2
                for k in tiles_of(c):
                    d = k - 4 * c
                    if d >= 0:
                        n0 = 128 * d
                        WT(v, ("a_em", h, c, k))
                        M[("a_mk", h, c, k)] = s_dve.tick(v.tensor_mul(
                            u.expS[cb][:, k, n0:n0 + 128], u.mtmp[d][:], maskb[:]))

            def norm(h, c):
                cb = c % 2
                gc = 4 * h + c
                WT(v, ("a_dnG", gc))
                W(v, ("a_bcG", gc - 2))
                M[("a_rcpG", gc)] = s_dve.tick(
                    v.reciprocal(u.rcp[cb][0:1, :], bank(2 + cb, p0=0, p1=1)))
                WT(v, ("a_bcG", gc))
                W(v, ("a_onG", gc - 2))
                mcp = s_dve.tick(v.tensor_copy(u.btmp[cb][:], bank(6 + cb)))
                WV(v, mcp)
                M[("a_onG", gc)] = s_dve.tick(v.tensor_mul(
                    bigT[:, h, c * 512:(c + 1) * 512], bank(cb), u.btmp[cb][:]))

            for h in range(NH):
                masks(h, 0)
                if h > 0:
                    norm(h - 1, 3)
                masks(h, 1)
                norm(h, 0)
                masks(h, 2)
                norm(h, 1)
                masks(h, 3)
                norm(h, 2)
            norm(NH - 1, 3)

        phase(a_alloc, [("sync", a_sync), ("gpsimd", a_gp_ld),
                        ("tensor", a_tens), ("scalar", a_act), ("vector", a_vec)])
        if upto == "p2":
            emit_tap()
            emit_drain()
            return

        NDS = 16

        def wo_alloc(stL):
            m = mk(stL)
            return SimpleNamespace(
                wos=[m(f"wos{i}", [128, ND, 256], BF16) for i in range(2)],
                xst=[m(f"xst{i}", [128, 256], F32) for i in range(4)],
                x1st=[m(f"x1st{i}", [128, 256], F32) for i in range(2)],
            )

        def wo_gp(g, u):
            for ds in range(NDS):
                W(g, ("wo_pe", ds - 2))
                M[("wo_w", ds)] = [
                    sm_wos[(ds % 2) * 4 + jg].tick(g.dma_start(
                        u.wos[ds % 2][:, 8 * jg:8 * jg + 8, :],
                        wo3[:, 8 * jg:8 * jg + 8, 256 * ds:256 * (ds + 1)]), 16)
                    for jg in range(4)]

        def wo_sync(sy, u):
            for ds in range(NDS):
                for tt in range(NT):
                    i = ds * NT + tt
                    W(sy, ("wo_add", i - 4))
                    M[("wo_x", i)] = sm_xst[i % 4].tick(sy.dma_start(
                        u.xst[i % 4][:],
                        x_d[tt * 128:(tt + 1) * 128, 256 * ds:256 * (ds + 1)]), 16)

        def wo_tens(t, u):
            for ds in range(NDS):
                WT(t, ("wo_w", ds))
                for tt in range(NT):
                    i = ds * NT + tt
                    bk = tt % 2
                    W(t, ("wo_add", i - 2))
                    ins = None
                    for hh in range(NH):
                        ins = t.matmul(bank(bk, 0, 256),
                                       bigT[:, hh, tt * 128:(tt + 1) * 128],
                                       u.wos[ds % 2][:, hh, :],
                                       start=(hh == 0), stop=(hh == NH - 1))
                    M[("wo_mm", i)] = s_pe.tick(ins)
                M[("wo_pe", ds)] = s_pe.cur()

        def wo_vec(v, u):
            for ds in range(NDS):
                for tt in range(NT):
                    i = ds * NT + tt
                    bk = tt % 2
                    WT(v, ("wo_mm", i))
                    WT(v, ("wo_x", i))
                    W(v, ("wo_sp", i - 2))
                    M[("wo_add", i)] = s_dve.tick(v.tensor_add(
                        u.x1st[tt % 2][:], bank(bk, 0, 256), u.xst[i % 4][:]))

        def wo_scal(sc, u):
            for ds in range(NDS):
                for tt in range(NT):
                    i = ds * NT + tt
                    WT(sc, ("wo_add", i))
                    M[("wo_sp", i)] = sm_x1w[tt % 2].tick(sc.dma_start(
                        x1sp_d[tt * 128:(tt + 1) * 128, 256 * ds:256 * (ds + 1)],
                        u.x1st[tt % 2][:]), 16)
            M["x1_done"] = [s.cur() for s in sm_x1w]
            WV(sc, M["x1_done"])

        phase(wo_alloc, [("gpsimd", wo_gp), ("sync", wo_sync), ("tensor", wo_tens),
                         ("vector", wo_vec), ("scalar", wo_scal)])
        if upto == "p3a":
            emit_drain()
            return

        def p3b_sy(sy):
            WT(sy, "x1_done")

        phase(ln_alloc("ln2"), ln_bodies("ln2", x1sp_d, pre_sy=p3b_sy))
        if upto == "p3b":
            emit_tap()
            emit_drain()
            return

        NSC = (NIB + 3) // 4

        def p4_alloc(stL):
            m = mk(stL)
            return SimpleNamespace(
                qbg=[m(f"qbg{i}", [128, D], BF16) for i in range(3)],
                qbu=[m(f"qbu{i}", [128, D], BF16) for i in range(3)],
                wgT=m("wgT", [128, ND, 128], BF16),
                wuT=m("wuT", [128, ND, 128], BF16),
                sgb=[m(f"sgb{i}", [128, 4, 32], F32) for i in range(2)],
                zgb=[m(f"zgb{i}", [128, 4, 32], F32) for i in range(2)],
                sub=[m(f"sub{i}", [128, 4, 32], F32) for i in range(2)],
                zub=[m(f"zub{i}", [128, 4, 32], F32) for i in range(2)],
                nzg=[m(f"nzg{i}", [128, 4, 32], F32) for i in range(2)],
                nzu=[m(f"nzu{i}", [128, 4, 32], F32) for i in range(2)],
                silu=[m(f"silu{i}", [128, 512], BF16) for i in range(2)],
                hac=[m(f"hac{i}", [128, 1024], BF16) for i in range(2)],
            )

        def p4_sync(sy, u):
            def scl_load(sy, ac):
                if ac >= 2:
                    W(sy, ("p4dqa", 4 * (ac - 1) - 1))
                a0 = 4 * ac
                an = min(4, NIB - a0)
                M[("p4scl", ac)] = [
                    sm_scl[ai * 2 + ac % 2].tick(sy.dma_start(
                        dst[ac % 2][:, 0:an, :], src[:, a0:a0 + an, :]), 16)
                    for ai, (src, dst) in enumerate(
                        ((sg3, u.sgb), (zg3, u.zgb), (su3, u.sub), (zu3, u.zub)))]

            for a in range(NIB):
                if a == 0:
                    scl_load(sy, 0)
                    scl_load(sy, 1)
                elif (a + 2) % 4 == 0 and 2 <= (a + 2) // 4 < (NIB + 3) // 4:
                    scl_load(sy, (a + 2) // 4)
                for hf in range(2):
                    WT(sy, ("p4hh", a, hf))
                    M[("p4hspH", a, hf)] = sm_hw[(a % 2) * 2 + hf].tick(
                        sy.dma_start(hsp_d[a][:, hf * 1024:(hf + 1) * 1024],
                                     u.hac[a % 2][:]), 16)
            M["p4_spills"] = [s.cur() for s in sm_hw]
            WV(sy, M["p4_spills"])

        def p4_gp(g, u):
            for a in range(NIB):
                W(g, ("p4T", a - 3))
                M[("p4w", a)] = [
                    sm_qgu[0 + a % 3].tick(g.dma_start(
                        u.qbg[a % 3][:], qg_d[a * 128:(a + 1) * 128, :]), 16),
                    sm_qgu[3 + a % 3].tick(g.dma_start(
                        u.qbu[a % 3][:], qu_d[a * 128:(a + 1) * 128, :]), 16)]

        def p4_nz(v, u, a):
            ac = a // 4
            if a % 4 == 0:
                WT(v, ("p4scl", ac))
                if ac >= 2:
                    W(v, ("p4dqa", 4 * (ac - 1) - 1))
                s_dve.tick(v.scalar_tensor_tensor(
                    u.nzg[ac % 2][:], u.zgb[ac % 2][:], -1.0, u.sgb[ac % 2][:],
                    ALU.mult, ALU.mult))
                mz = s_dve.tick(v.scalar_tensor_tensor(
                    u.nzu[ac % 2][:], u.zub[ac % 2][:], -1.0, u.sub[ac % 2][:],
                    ALU.mult, ALU.mult))
                WV(v, mz)
                M[("p4nz", ac)] = mz

        def p4_dqa(sc, u, a, part):
            ac = a // 4
            js = list(range(ND))
            quarter = js[part * len(js) // 4:(part + 1) * len(js) // 4]
            if part == 0:
                WT(sc, ("p4w", a))
                WT(sc, ("p4nz", ac))
                W(sc, ("p4dqa", a - 1))
            ins = None
            for qb, sb, nzb in ((u.qbg, u.sgb, u.nzg), (u.qbu, u.sub, u.nzu)):
                for j in quarter:
                    ins = sc.activation(
                        qb[a % 3][:, j * 128:(j + 1) * 128],
                        qb[a % 3][:, j * 128:(j + 1) * 128],
                        AF.Identity, bias=nzb[ac % 2][:, a % 4, j:j + 1],
                        scale=sb[ac % 2][:, a % 4, j:j + 1])
            if part == 3:
                M[("p4dqa", a)] = s_act.tick(ins)
            else:
                s_act.tick(ins)

        GBK = (0, 4, 1, 5)
        UBK = (2, 6, 3, 7)

        def p4_vec(v, u):
            p4_nz(v, u, 0)
            for a in range(NIB):
                for tv in (0, 2, 1, 3, 4, 6, 5, 7):
                    WT(v, ("p4tp", a, 2 * tv))
                    WT(v, ("p4tp", a, 2 * tv + 1))
                    dstT = u.wgT if tv < 4 else u.wuT
                    j0 = (tv % 4) * 8
                    pb = 4 + 2 * (tv % 2)
                    src = psA[:, pb * BPF32:(pb + 2) * BPF32]
                    M[("p4tev2", a, tv)] = s_dve.tick(v.tensor_copy(
                        dstT[:, j0:j0 + 8, :], src.rearrange("p (q c) -> p q c", q=8)))
                M[("p4tevall", a)] = s_dve.cur()
                if a + 1 < NIB:
                    p4_nz(v, u, a + 1)
                for c in range(4):
                    WT(v, ("p4silu", a, c))
                    WT(v, ("p4u", a, c))
                    if c == 0:
                        W(v, ("p4hspH", a - 2, 1))
                    elif c == 2:
                        WT(v, ("p4hspH", a, 0))
                    M[("p4mul", a, c)] = s_dve.tick(v.tensor_mul(
                        u.hac[a % 2][:, (c % 2) * 512:(c % 2 + 1) * 512],
                        bank(UBK[c]), u.silu[c % 2][:]))
                    if c % 2 == 1:
                        M[("p4hh", a, c // 2)] = s_dve.cur()

        def p4_act(sc, u):
            for part in range(4):
                p4_dqa(sc, u, 0, part)
            for a in range(NIB):
                for c in range(4):
                    WT(sc, ("p4g", a, c))
                    if c >= 2:
                        W(sc, ("p4mul", a, c - 2))
                    elif a > 0:
                        W(sc, ("p4mul", a - 1, c + 2))
                    M[("p4silu", a, c)] = s_act.tick(sc.activation(
                        u.silu[c % 2][:], bank(GBK[c]), AF.Silu,
                        bias=cst(C_ZERO), scale=cst(C_ONE)))
                    if a + 1 < NIB:
                        p4_dqa(sc, u, a + 1, c)

        def p4_tens(t, u):
            P_LIST = (0, 1, 4, 5, 2, 3, 6, 7, 8, 9, 12, 13, 10, 11, 14, 15)
            for a in range(NIB):
                WT(t, ("p4dqa", a))
                if a > 0:
                    W(t, ("p4silu", a - 1, 1))
                    W(t, ("p4silu", a - 1, 3))
                for k, gb in enumerate(P_LIST):
                    if k == 4 and a > 0:
                        W(t, ("p4mul", a - 1, 1))
                        W(t, ("p4mul", a - 1, 3))
                    if gb >= 4:
                        W(t, ("p4tev2", a, (gb - 4) // 2))
                    else:
                        W(t, ("p4tev2", a - 1, (gb + 12) // 2))
                    src = u.qbg if gb < 8 else u.qbu
                    j0 = (gb % 8) * 4
                    pb = 4 + 2 * ((gb // 2) % 2)
                    ins = None
                    for q4 in range(4):
                        j = j0 + q4
                        ins = t.matmul(bank(pb + gb % 2, q4 * 128, (q4 + 1) * 128),
                                       src[a % 3][:, j * 128:(j + 1) * 128],
                                       ident[:], start=True, stop=True)
                    M[("p4tp", a, gb)] = s_pe.tick(ins)
                M[("p4T", a)] = s_pe.cur()
                for c in range(4):
                    if c == 0:
                        W(t, ("p4tev2", a, 3))
                    elif c == 1:
                        W(t, ("p4tev2", a, 6))
                    if a > 0 and c % 2 == 0:
                        W(t, ("p4silu", a - 1, c))
                    ins = None
                    for j in range(ND):
                        ins = t.matmul(bank(GBK[c]), u.wgT[:, j, :],
                                       bigT[:, j, c * 512:(c + 1) * 512],
                                       start=(j == 0), stop=(j == ND - 1))
                    M[("p4g", a, c)] = s_pe.tick(ins)
                    if c == 0:
                        WT(t, ("p4tevall", a))
                    if a > 0 and c % 2 == 0:
                        W(t, ("p4mul", a - 1, c))
                    ins = None
                    for j in range(ND):
                        ins = t.matmul(bank(UBK[c]), u.wuT[:, j, :],
                                       bigT[:, j, c * 512:(c + 1) * 512],
                                       start=(j == 0), stop=(j == ND - 1))
                    M[("p4u", a, c)] = s_pe.tick(ins)

        phase(p4_alloc, [("sync", p4_sync), ("gpsimd", p4_gp), ("vector", p4_vec),
                         ("scalar", p4_act), ("tensor", p4_tens)])
        if upto == "p4":
            emit_drain()
            return

        ctx["close_big"]()

        NS = len(SUPS)
        SL = [(q4, s) for q4 in range(4) for s in range(NS)]
        NGB = [(ss + 3) // 4 for ss in SUPS]

        def p5_li(idx, hh):
            return idx * 8 + hh

        def p5_gidx(idx, hh, gb):
            q4, s = SL[idx]
            per = [8 * NGB[SL[i][1]] for i in range(len(SL))]
            return sum(per[:idx]) + hh * NGB[s] + gb

        _gcum = [0]
        for _i in range(len(SL)):
            _gcum.append(_gcum[-1] + 8 * NGB[SL[_i][1]])

        def p5_gidx(idx, hh, gb):
            return _gcum[idx] + hh * NGB[SL[idx][1]] + gb

        def p5_lin(idx, tt, ch):
            return (idx * NT + tt) * 2 + ch

        def p5_alloc(stL):
            m = mk(stL)
            return SimpleNamespace(
                y2=m("y2", [128, NT * 1024], F32),
                wdT=[m(f"wdT{i}", [128, 8, 1024], BF16) for i in range(2)],
                hbuf=[m(f"hbuf{i}", [128, 8, 512], BF16) for i in range(2)],
                qdb=[m(f"qdb{i}", [128, 1024], BF16) for i in range(8)],
                sdb=m("sdb", [128, 32, NIB], F32),
                zdb=m("zdb", [128, 32, NIB], F32),
                nzd=[m(f"nzd{i}", [128, 8, 8], F32) for i in range(2)],
                outst=[m(f"outst{i}", [128, 1024], F32) for i in range(4)],
                x1b=[m(f"x1b{i}", [128, 1024], F32) for i in range(8)],
            )

        def p5_sync(sy, u):
            WT(sy, "x1_done")
            WT(sy, "p4_spills")
            M["p5sdz"] = (
                [sm_sdz[sl].tick(sy.dma_start(
                    u.sdb[:, 8 * sl:8 * sl + 8, :], sd3[:, 8 * sl:8 * sl + 8, :]), 16)
                 for sl in range(4)] +
                [sm_sdz[4 + sl].tick(sy.dma_start(
                    u.zdb[:, 8 * sl:8 * sl + 8, :], zd3[:, 8 * sl:8 * sl + 8, :]), 16)
                 for sl in range(4)])
            for idx, (q4, s) in enumerate(SL):
                ss = SUPS[s]
                for tg in range(4):
                    li4 = idx * 4 + tg
                    W(sy, ("p5mmG", li4 - 2))
                    nh = (ss + 1) // 2
                    M[("p5h", idx, tg)] = [
                        sm_hb[(tg % 2) * 2 + 0].tick(sy.dma_start(
                            u.hbuf[tg % 2][:, 0:nh, :],
                            hsp_d[8 * s:8 * s + nh, :, tg * 512:(tg + 1) * 512]
                            .rearrange("a p c -> p a c")), 16),
                        sm_hb[(tg % 2) * 2 + 1].tick(sy.dma_start(
                            u.hbuf[tg % 2][:, nh:ss, :],
                            hsp_d[8 * s + nh:8 * s + ss, :, tg * 512:(tg + 1) * 512]
                            .rearrange("a p c -> p a c")), 16)]
                    if idx == 0 and tg < 2:
                        for tt in (4 * tg, 4 * tg + 1, 4 * tg + 2, 4 * tg + 3):
                            M[("p5x1", 0, tt)] = sm_x1b[tt % 8].tick(sy.dma_start(
                                u.x1b[tt % 8][:],
                                x1sp_d[tt * 128:(tt + 1) * 128, 0:1024]), 16)

        def p5_gp(g, u):
            for idx, (q4, s) in enumerate(SL):
                ss = SUPS[s]
                cw = ss * 128
                for hh in range(8):
                    li = p5_li(idx, hh)
                    W(g, ("p5tpA", li - 8))
                    r0 = q4 * 1024 + hh * 128
                    M[("p5w", li)] = sm_qdb[hh].tick(g.dma_start(
                        u.qdb[hh][:, 0:cw],
                        qd_d[r0:r0 + 128, 1024 * s:1024 * s + cw]), 16)

        def p5_dq(v, u, idx):
            q4, s = SL[idx]
            ss = SUPS[s]
            if idx >= 2:
                W(v, ("p5dqaA", idx - 2))
            mz = s_dve.tick(v.scalar_tensor_tensor(
                u.nzd[idx % 2][:, :, 0:ss],
                u.zdb[:, q4 * 8:q4 * 8 + 8, 8 * s:8 * s + ss], -1.0,
                u.sdb[:, q4 * 8:q4 * 8 + 8, 8 * s:8 * s + ss],
                ALU.mult, ALU.mult))
            WV(v, mz)
            M[("p5nz", idx)] = mz

        def p5_dqa(sc, u, idx):
            q4, s = SL[idx]
            ss = SUPS[s]
            WT(sc, ("p5nz", idx))
            for hh in range(8):
                li = p5_li(idx, hh)
                WT(sc, ("p5w", li))
                ai = q4 * 8 + hh
                ins = None
                for gg in range(ss):
                    ins = sc.activation(
                        u.qdb[hh][:, gg * 128:(gg + 1) * 128],
                        u.qdb[hh][:, gg * 128:(gg + 1) * 128],
                        AF.Identity,
                        bias=u.nzd[idx % 2][:, hh, gg:gg + 1],
                        scale=u.sdb[:, ai, 8 * s + gg:8 * s + gg + 1])
                M[("p5dqa", li)] = s_act.tick(ins)
            M[("p5dqaA", idx)] = s_act.cur()

        def p5_tev(v, u, idx):
            q4, s = SL[idx]
            ss = SUPS[s]
            for hh in range(8):
                for gb in range(NGB[s]):
                    WT(v, ("p5tp", idx, hh, gb))
                pb = 4 + 2 * (hh % 2)
                M[("p5tevHL", idx * 8 + hh)] = s_dve.tick(v.tensor_copy(
                    u.wdT[idx % 2][:, 0:ss, hh * 128:(hh + 1) * 128],
                    psA[:, pb * BPF32:pb * BPF32 + ss * 128]
                    .rearrange("p (q c) -> p q c", q=ss)))
            M[("p5tev", idx)] = s_dve.cur()

        def p5_vec(v, u):
            WT(v, "p5sdz")
            p5_dq(v, u, 0)
            p5_tev(v, u, 0)
            p5_dq(v, u, 1)
            for idx, (q4, s) in enumerate(SL):
                for tg in range(4):
                    for ti in range(4):
                        tt = 4 * tg + ti
                        pb = 2 * (tt % 2)
                        WT(v, ("p5mm", idx, tt, 0))
                        WT(v, ("p5mm", idx, tt, 1))
                        if s > 0:
                            W(v, ("p5y2T", idx - 1, tt))
                        elif q4 > 0:
                            W(v, ("p5fin", q4 - 1, tt))
                        dst = u.y2[:, tt * 1024:(tt + 1) * 1024]
                        src_ = psA[:, pb * BPF32:(pb + 2) * BPF32]
                        if s == 0:
                            ins = v.tensor_copy(dst, src_)
                        else:
                            ins = v.tensor_add(dst, src_, dst)
                        my = s_dve.tick(ins)
                        M[("p5y2T", idx, tt)] = my
                        M[("p5y2L", idx * NT + tt)] = my
                        if s == NS - 1:
                            WV(v, my)
                            WT(v, ("p5x1", q4, tt))
                            W(v, ("p5out", q4, tt - 4))
                            if tt < 4 and q4 > 0:
                                W(v, ("p5out", q4 - 1, tt + 12))
                            M[("p5fin", q4, tt)] = s_dve.tick(v.tensor_add(
                                u.outst[tt % 4][:], dst, u.x1b[tt % 8][:]))
                if idx + 1 < len(SL):
                    p5_tev(v, u, idx + 1)
                if idx + 2 < len(SL):
                    p5_dq(v, u, idx + 2)

        def p5_act(sc, u):
            p5_dqa(sc, u, 0)
            p5_dqa(sc, u, 1)
            for idx, (q4, s) in enumerate(SL):
                if s == NS - 1:
                    for tt in range(NT):
                        WT(sc, ("p5fin", q4, tt))
                        M[("p5out", q4, tt)] = sm_out[tt % 4].tick(sc.dma_start(
                            y_d[tt * 128:(tt + 1) * 128, q4 * 1024:(q4 + 1) * 1024],
                            u.outst[tt % 4][:]), 16)
                        nq, ntt = (q4, tt + 8) if tt < 8 else (q4 + 1, tt - 8)
                        if nq < 4:
                            M[("p5x1", nq, ntt)] = sm_x1b[ntt % 8].tick(
                                sc.dma_start(
                                    u.x1b[ntt % 8][:],
                                    x1sp_d[ntt * 128:(ntt + 1) * 128,
                                           nq * 1024:(nq + 1) * 1024]), 16)
                if idx + 2 < len(SL):
                    p5_dqa(sc, u, idx + 2)
            WV(sc, [s.cur() for s in sm_out])

        def p5_T(t, u, idx):
            q4, s = SL[idx]
            ss = SUPS[s]
            for hh in range(8):
                li = p5_li(idx, hh)
                WT(t, ("p5dqa", li))
                pb = 4 + 2 * (hh % 2)
                W(t, ("p5tevHL", idx * 8 + hh - 2))
                for gb in range(NGB[s]):
                    it0 = 4 * gb
                    itn = min(4, ss - it0)
                    ins = None
                    for q in range(itn):
                        it = it0 + q
                        ins = t.matmul(
                            bank(pb + gb, q * 128, (q + 1) * 128),
                            u.qdb[hh][:, it * 128:(it + 1) * 128],
                            ident[:], start=True, stop=True)
                    M[("p5tp", idx, hh, gb)] = s_pe.tick(ins)
                M[("p5tpA", li)] = s_pe.cur()

        def p5_tens(t, u):
            p5_T(t, u, 0)
            for idx, (q4, s) in enumerate(SL):
                ss = SUPS[s]
                WT(t, ("p5tev", idx))
                for tg in range(4):
                    li4 = idx * 4 + tg
                    WT(t, ("p5h", idx, tg))
                    for ti in range(4):
                        tt = 4 * tg + ti
                        for ch in range(2):
                            bk = 2 * (tt % 2) + ch
                            if ch == 0:
                                W(t, ("p5y2L", idx * NT + tt - 2))
                            ins = None
                            for it in range(ss):
                                ins = t.matmul(
                                    bank(bk),
                                    u.hbuf[tg % 2][:, it, ti * 128:(ti + 1) * 128],
                                    u.wdT[idx % 2][:, it, ch * 512:(ch + 1) * 512],
                                    start=(it == 0), stop=(it == ss - 1))
                            M[("p5mm", idx, tt, ch)] = s_pe.tick(ins)
                    M[("p5mmG", li4)] = s_pe.cur()
                if idx + 1 < len(SL):
                    p5_T(t, u, idx + 1)

        phase(p5_alloc, [("sync", p5_sync), ("gpsimd", p5_gp), ("vector", p5_vec),
                         ("scalar", p5_act), ("tensor", p5_tens)])
        emit_drain()

    ctx = {"bigT": _Dummy(), "close_big": lambda: None}
    M.lenient = True
    phases(True, ctx)
    M.lenient = False

    stBig = ExitStack()
    bigT_real = stBig.enter_context(nc.sbuf_tensor("bigT", [128, ND, S], BF16))
    ctx = {"bigT": bigT_real, "close_big": stBig.close}
    phases(False, ctx)
    try:
        stBig.close()
    except Exception:
        pass
    st0.close()
    return nc


def _tables():
    inv = 10000.0 ** (-(np.arange(64, dtype=np.float64) * 2.0 / 128.0))
    ang = np.arange(S, dtype=np.float64)[None, :] * inv[:, None]
    cosp = np.repeat(np.cos(ang), 2, axis=0)
    sinp2 = np.empty((128, S), dtype=np.float64)
    sinp2[0::2] = -np.sin(ang)
    sinp2[1::2] = np.sin(ang)
    pswap = np.zeros((128, 128), dtype=np.float32)
    for k in range(64):
        pswap[2 * k, 2 * k + 1] = 1.0
        pswap[2 * k + 1, 2 * k] = 1.0
    ident = np.eye(128, dtype=np.float32)
    mask = np.triu(np.ones((128, 128), dtype=np.float32))
    bf = ml_dtypes.bfloat16
    return (cosp.astype(bf), sinp2.astype(bf), pswap.astype(bf),
            ident.astype(bf), mask.astype(bf))


def make_inmaps(inputs):
    x = np.asarray(inputs["L0_attn_x"])
    cosp, sinp, pswap, ident, maskb = _tables()
    shared = {
        "wqkv": np.ascontiguousarray(np.asarray(inputs["L0_attn_w_qkv"])),
        "wo": np.ascontiguousarray(np.asarray(inputs["L0_attn_w_o"])),
        "qg": np.ascontiguousarray(np.asarray(inputs["L0_ffn_qw_gate"])),
        "qu": np.ascontiguousarray(np.asarray(inputs["L0_ffn_qw_up"])),
        "qd": np.ascontiguousarray(np.asarray(inputs["L0_ffn_qw_down"])),
        "sg": np.ascontiguousarray(np.asarray(inputs["L0_ffn_s_gate"])),
        "zg": np.ascontiguousarray(np.asarray(inputs["L0_ffn_z_gate"])),
        "su": np.ascontiguousarray(np.asarray(inputs["L0_ffn_s_up"])),
        "zu": np.ascontiguousarray(np.asarray(inputs["L0_ffn_z_up"])),
        "sd": np.ascontiguousarray(np.asarray(inputs["L0_ffn_s_down"])),
        "zd": np.ascontiguousarray(np.asarray(inputs["L0_ffn_z_down"])),
        "cosp": cosp, "sinp": sinp, "pswap": pswap,
        "ident": ident, "maskb": maskb,
    }
    return [dict(shared, x=np.ascontiguousarray(x[c])) for c in range(8)]


def kernel(**inputs):
    nc = build_nc("full")
    in_maps = make_inmaps(inputs)
    res = run_bass_kernel_spmd(nc, in_maps, core_ids=list(range(8)))
    return np.stack([np.asarray(res.results[c]["y"]) for c in range(8)], axis=0)


# version 15 (final)
# speedup vs baseline: 1.0966x; 1.0290x over previous
import math
from contextlib import ExitStack
from types import SimpleNamespace

import numpy as np
import ml_dtypes

import concourse.bass as bass
import concourse.mybir as mybir
from concourse.bass_utils import run_bass_kernel_spmd

F32 = mybir.dt.float32
BF16 = mybir.dt.bfloat16
I32 = mybir.dt.int32
AF = mybir.ActivationFunctionType
ALU = mybir.AluOpType
AX = mybir.AxisListType

S = 2048
D = 4096
NH, NKV, HD = 32, 8, 128
FQKV = 6144
NT = S // 128
ND = D // 128
IFF = 11008
NIB = IFF // 128
SUPS = [8] * 10 + [6]
EXP_SHIFT = -4.0
SM_SCALE = 1.0 / math.sqrt(HD)
BPF32 = 512
C_ONE, C_ZERO, C_EPS, C_NHALF, C_SHIFT, C_SMS = range(6)


class _DryInst:
    def then_inc(self, sem, k):
        return self


class _DryEng:
    def wait_ge(self, *a, **k):
        return None

    def __getattr__(self, name):
        return lambda *a, **k: _DryInst()


class _Dummy:
    def __getitem__(self, *_):
        return self

    def rearrange(self, *a, **k):
        return self


class Marks(dict):
    lenient = True

    def __setitem__(self, k, v):
        if not self.lenient and k in self and self[k] != v:
            raise AssertionError(f"mark mismatch {k}: {self[k]} vs {v}")
        dict.__setitem__(self, k, v)

    def __missing__(self, k):
        if self.lenient:
            return None
        raise KeyError(k)


def build_nc(upto="full"):
    nc = bass.Bass("TRN2", target_bir_lowering=False)

    x_d = nc.dram_tensor("x", [S, D], F32, kind="ExternalInput").ap()
    wqkv_d = nc.dram_tensor("wqkv", [D, FQKV], F32, kind="ExternalInput").ap()
    wo_d = nc.dram_tensor("wo", [D, D], F32, kind="ExternalInput").ap()
    qg_d = nc.dram_tensor("qg", [IFF, D], I32, kind="ExternalInput").ap()
    qu_d = nc.dram_tensor("qu", [IFF, D], I32, kind="ExternalInput").ap()
    qd_d = nc.dram_tensor("qd", [D, IFF], I32, kind="ExternalInput").ap()
    sg_d = nc.dram_tensor("sg", [IFF, 32], F32, kind="ExternalInput").ap()
    zg_d = nc.dram_tensor("zg", [IFF, 32], F32, kind="ExternalInput").ap()
    su_d = nc.dram_tensor("su", [IFF, 32], F32, kind="ExternalInput").ap()
    zu_d = nc.dram_tensor("zu", [IFF, 32], F32, kind="ExternalInput").ap()
    sd_d = nc.dram_tensor("sd", [D, NIB], F32, kind="ExternalInput").ap()
    zd_d = nc.dram_tensor("zd", [D, NIB], F32, kind="ExternalInput").ap()
    cosp_d = nc.dram_tensor("cosp", [128, S], BF16, kind="ExternalInput").ap()
    sinp_d = nc.dram_tensor("sinp", [128, S], BF16, kind="ExternalInput").ap()
    pswap_d = nc.dram_tensor("pswap", [128, 128], BF16, kind="ExternalInput").ap()
    ident_d = nc.dram_tensor("ident", [128, 128], BF16, kind="ExternalInput").ap()
    maskb_d = nc.dram_tensor("maskb", [128, 128], BF16, kind="ExternalInput").ap()
    y_d = nc.dram_tensor("y", [S, D], F32, kind="ExternalOutput").ap()
    qsp_d = nc.dram_tensor("qsp", [NH, 128, S], BF16).ap()
    ksp_d = nc.dram_tensor("ksp", [NKV, 128, S], BF16).ap()
    vsp_d = nc.dram_tensor("vsp", [NT, 128, NKV * 128], BF16).ap()
    x1sp_d = nc.dram_tensor("x1sp", [S, D], F32).ap()
    hsp_d = nc.dram_tensor("hsp", [NIB, 128, S], BF16).ap()
    tap_d = None
    if upto in ("p0", "p2", "p3b"):
        tap_d = nc.dram_tensor("tap", [128, ND * S], BF16, kind="ExternalOutput").ap()

    wq3 = wqkv_d.rearrange("(j p) c -> p j c", p=128)
    wo3 = wo_d.rearrange("(j p) c -> p j c", p=128)
    sg3 = sg_d.rearrange("(a p) g -> p a g", p=128)
    zg3 = zg_d.rearrange("(a p) g -> p a g", p=128)
    su3 = su_d.rearrange("(a p) g -> p a g", p=128)
    zu3 = zu_d.rearrange("(a p) g -> p a g", p=128)
    sd3 = sd_d.rearrange("(a p) g -> p a g", p=128)
    zd3 = zd_d.rearrange("(a p) g -> p a g", p=128)

    st0 = ExitStack()
    E0 = st0.enter_context
    psA = E0(nc.psum_tensor("psA", [128, 8 * BPF32], F32))
    consts = E0(nc.sbuf_tensor("consts", [128, 8], F32))
    onesc = E0(nc.sbuf_tensor("onesc", [128, 1], BF16))
    onesr = E0(nc.sbuf_tensor("onesr", [1, 128], BF16))
    onesr32 = E0(nc.sbuf_tensor("onesr32", [1, 128], F32))
    pswap = E0(nc.sbuf_tensor("pswap_s", [128, 128], BF16))
    ident = E0(nc.sbuf_tensor("ident_s", [128, 128], BF16))
    maskb = E0(nc.sbuf_tensor("maskb_s", [128, 128], BF16))

    REG = []

    class Sem:
        def __init__(self, name):
            self.h = st0.enter_context(nc.semaphore(name))
            self.n = 0
            self.idx = len(REG)
            REG.append(self)

        def tick(self, inst, k=1):
            if not isinstance(inst, _DryInst):
                inst.then_inc(self.h, k)
            self.n += k
            return (self.idx, self.n)

        def cur(self):
            return (self.idx, self.n)

    def bankS(name, n):
        return [Sem(f"{name}{i}") for i in range(n)]

    s_pe = Sem("s_pe")
    s_dve = Sem("s_dve")
    s_act = Sem("s_act")
    sm_tbl = bankS("m_tbl", 3)
    sm_x = bankS("m_x", 2)
    sm_cs = bankS("m_cs", 2)
    sm_wp = bankS("m_wp", 8)
    sm_rout = bankS("m_rout", 2)
    sm_vst = bankS("m_vst", 4)
    sm_qh = bankS("m_qh", 2)
    sm_kg = bankS("m_kg", 2)
    sm_vg = bankS("m_vg", 4)
    sm_wos = bankS("m_wos", 8)
    sm_xst = bankS("m_xst", 4)
    sm_x1w = bankS("m_x1w", 2)
    sm_qgu = bankS("m_qgu", 6)
    sm_scl = bankS("m_scl", 8)
    sm_hw = bankS("m_hw", 4)
    sm_sdz = bankS("m_sdz", 8)
    sm_hb = bankS("m_hb", 4)
    sm_qdb = bankS("m_qdb", 8)
    sm_x1b = bankS("m_x1b", 8)
    sm_out = bankS("m_out", 4)
    sm_tap = bankS("m_tap", 2)

    M = Marks()

    def bank(b, lo=0, hi=BPF32, p0=0, p1=128):
        assert 0 <= b < 8 and 0 <= lo < hi <= BPF32
        return psA[p0:p1, b * BPF32 + lo: b * BPF32 + hi]

    def cst(i):
        return consts[:, i:i + 1]

    def _aslist(mv):
        return mv if isinstance(mv, list) else [mv]

    def WV(eng, mv):
        if mv is None:
            return
        for si, v in _aslist(mv):
            eng.wait_ge(REG[si].h, v)

    def W(eng, key):
        WV(eng, dict.get(M, key))

    def WT(eng, key):
        mv = dict.get(M, key)
        if mv is None:
            assert M.lenient, f"missing required mark {key}"
            return
        WV(eng, mv)

    def phases(dry, ctx):
        for s in REG:
            s.n = 0

        def phase(allocfn, bodies):
            if dry:
                bufs = allocfn(None)
                for _, fn in bodies:
                    fn(_DryEng(), bufs)
            else:
                with ExitStack() as stL:
                    bufs = allocfn(stL)
                    with nc.Block() as blk:
                        for name, fn in bodies:
                            getattr(blk, name)(
                                lambda e, fn=fn, bufs=bufs: fn(e, bufs))

        def mk(stL):
            if stL is None:
                return lambda n, sh, dt: _Dummy()
            return lambda n, sh, dt: stL.enter_context(nc.sbuf_tensor(n, sh, dt))

        bigT = ctx["bigT"]

        def ln_alloc(tag):
            def f(stL):
                m = mk(stL)
                return SimpleNamespace(
                    xrow=[m(f"{tag}_xr{i}", [128, D], F32) for i in range(2)],
                    xn=[m(f"{tag}_xn{i}", [128, D], BF16) for i in range(2)],
                    trash=m(f"{tag}_tr", [128, D], F32),
                    cols=m(f"{tag}_cols", [128, 20], F32),
                )
            return f

        def ln_bodies(tag, src_d, first=False, pre_sy=None, pre_gp=None):
            def b_sync(sy, u):
                if pre_sy:
                    pre_sy(sy)
                for tt in range(NT):
                    W(sy, (tag, "xn", tt - 2))
                    M[(tag, "xld", tt)] = sm_x[tt % 2].tick(
                        sy.dma_start(u.xrow[tt % 2][:],
                                     src_d[tt * 128:(tt + 1) * 128, :]), 16)

            def b_gp(g, u):
                if pre_gp:
                    pre_gp(g)

            def b_vec(v, u):
                cols = u.cols
                if first:
                    for i, val in ((C_ONE, 1.0), (C_ZERO, 0.0), (C_EPS, 1e-5),
                                   (C_NHALF, -0.5), (C_SHIFT, EXP_SHIFT),
                                   (C_SMS, SM_SCALE)):
                        v.memset(consts[:, i:i + 1], val)
                    v.memset(onesc[:], 1.0)
                    v.memset(onesr[:], 1.0)
                    M["consts"] = s_dve.tick(v.memset(onesr32[:], 1.0))
                for tt in range(NT):
                    b = tt % 2
                    o = 10 * b
                    WT(v, (tag, "xld", tt))
                    W(v, (tag, "xn", tt - 2))
                    m1 = s_dve.tick(v.reduce_sum(cols[:, o:o + 1], u.xrow[b][:], axis=AX.X))
                    WV(v, m1)
                    s_dve.tick(v.tensor_scalar_mul(cols[:, o + 2:o + 3], cols[:, o:o + 1], 1.0 / D))
                    WT(v, (tag, "sq", tt))
                    m4 = s_dve.tick(v.tensor_scalar_mul(cols[:, o + 3:o + 4], cols[:, o + 1:o + 2], 1.0 / D))
                    WV(v, m4)
                    m5 = s_dve.tick(v.tensor_mul(cols[:, o + 4:o + 5], cols[:, o + 2:o + 3], cols[:, o + 2:o + 3]))
                    WV(v, m5)
                    M[(tag, "var", tt)] = s_dve.tick(
                        v.tensor_sub(cols[:, o + 5:o + 6], cols[:, o + 3:o + 4], cols[:, o + 4:o + 5]))
                    WT(v, (tag, "rstd", tt))
                    M[(tag, "negmr", tt)] = s_dve.tick(v.scalar_tensor_tensor(
                        cols[:, o + 8:o + 9], cols[:, o + 2:o + 3], -1.0,
                        cols[:, o + 7:o + 8], ALU.mult, ALU.mult))
                    for jb in range(8):
                        WT(v, (tag, "tp", tt, jb))
                        src = psA[:, (6 + jb % 2) * BPF32:(7 + jb % 2) * BPF32]
                        M[(tag, "ev", tt, jb)] = s_dve.tick(v.tensor_copy(
                            bigT[:, 4 * jb:4 * jb + 4, tt * 128:(tt + 1) * 128],
                            src.rearrange("p (a c) -> p a c", a=4)))

            def b_scal(sc, u):
                cols = u.cols
                if first:
                    WT(sc, "consts")
                for tt in range(NT):
                    b = tt % 2
                    o = 10 * b
                    WT(sc, (tag, "xld", tt))
                    W(sc, (tag, "xn", tt - 2))
                    M[(tag, "sq", tt)] = s_act.tick(sc.activation(
                        u.trash[:], u.xrow[b][:], AF.Square,
                        bias=cst(C_ZERO), scale=cst(C_ONE),
                        accum_out=cols[:, o + 1:o + 2]))
                    WT(sc, (tag, "var", tt))
                    a1 = s_act.tick(sc.activation(cols[:, o + 6:o + 7], cols[:, o + 5:o + 6],
                                                  AF.Ln, bias=cst(C_EPS), scale=cst(C_ONE)))
                    WV(sc, a1)
                    M[(tag, "rstd", tt)] = s_act.tick(sc.activation(
                        cols[:, o + 7:o + 8], cols[:, o + 6:o + 7],
                        AF.Exp, bias=cst(C_ZERO), scale=cst(C_NHALF)))
                    WT(sc, (tag, "negmr", tt))
                    for q in range(4):
                        M[(tag, "xnq", tt, q)] = s_act.tick(sc.activation(
                            u.xn[b][:, q * 1024:(q + 1) * 1024],
                            u.xrow[b][:, q * 1024:(q + 1) * 1024], AF.Identity,
                            bias=cols[:, o + 8:o + 9], scale=cols[:, o + 7:o + 8]))
                    M[(tag, "xn", tt)] = s_act.cur()

            def b_tens(t, u):
                if first:
                    WT(t, "consts")
                    WT(t, "tbl")
                for tt in range(NT):
                    b = tt % 2
                    for jb in range(8):
                        WT(t, (tag, "xnq", tt, jb // 2))
                        bk = 6 + jb % 2
                        if jb >= 2:
                            W(t, (tag, "ev", tt, jb - 2))
                        else:
                            W(t, (tag, "ev", tt - 1, jb + 6))
                        ins = None
                        for u4 in range(4):
                            j = 4 * jb + u4
                            ins = t.matmul(bank(bk, u4 * 128, (u4 + 1) * 128),
                                           u.xn[b][:, j * 128:(j + 1) * 128],
                                           ident[:], start=True, stop=True)
                        M[(tag, "tp", tt, jb)] = s_pe.tick(ins)

            return [("sync", b_sync), ("gpsimd", b_gp), ("vector", b_vec),
                    ("scalar", b_scal), ("tensor", b_tens)]

        def p0_gp(g):
            M["tbl"] = [
                sm_tbl[0].tick(g.dma_start(pswap[:], pswap_d), 16),
                sm_tbl[1].tick(g.dma_start(ident[:], ident_d), 16),
                sm_tbl[2].tick(g.dma_start(maskb[:], maskb_d), 16),
            ]

        def emit_tap():
            def tap_body(sy, u):
                fins = []
                for j in range(ND):
                    if j >= 2:
                        WV(sy, fins[j - 2])
                    fins.append(sm_tap[j % 2].tick(
                        sy.dma_start(tap_d[:, j * S:(j + 1) * S], bigT[:, j, :]), 16))
                WV(sy, fins[-2])
                WV(sy, fins[-1])

            phase(lambda stL: None, [("sync", tap_body)])

        def emit_drain():
            def drain_body(sy, u):
                WV(sy, [s.cur() for s in REG])

            phase(lambda stL: None, [("sync", drain_body)])

        phase(ln_alloc("ln1"), ln_bodies("ln1", x_d, first=True, pre_gp=p0_gp))
        if upto == "p0":
            emit_tap()
            emit_drain()
            return

        NPAIR = FQKV // 256

        def p1_alloc(stL):
            m = mk(stL)
            return SimpleNamespace(
                wpair=[m(f"wp{i}", [128, ND, 256], BF16) for i in range(2)],
                qpre=[m(f"qpre{i}", [128, S], BF16) for i in range(2)],
                rtA=[m(f"rtA{i}", [128, S], BF16) for i in range(2)],
                rtB=[m(f"rtB{i}", [128, S], BF16) for i in range(2)],
                rout=[m(f"rout{i}", [128, S], BF16) for i in range(2)],
                vst=[m(f"vst{i}", [128, 256], BF16) for i in range(4)],
                cosp=m("cosp_s", [128, S], BF16),
                sinp=m("sinp_s", [128, S], BF16),
            )

        def p1_sync(sy, u):
            M["cos"] = [
                sm_cs[0].tick(sy.dma_start(u.cosp[:], cosp_d), 16),
                sm_cs[1].tick(sy.dma_start(u.sinp[:], sinp_d), 16),
            ]

        def p1_gp(g, u):
            for fp in range(NPAIR):
                W(g, ("p1pe", fp - 2))
                M[("wp", fp)] = [
                    sm_wp[(fp % 2) * 4 + jg].tick(g.dma_start(
                        u.wpair[fp % 2][:, 8 * jg:8 * jg + 8, :],
                        wq3[:, 8 * jg:8 * jg + 8, 256 * fp:256 * (fp + 1)]), 16)
                    for jg in range(4)]

        def p1_tens(t, u):
            WT(t, "cos")
            for fp in range(NPAIR):
                WT(t, ("wp", fp))
                if fp < 20:
                    for sub in range(2):
                        ft = 2 * fp + sub
                        grp = ft % 2
                        W(t, ("p1ropeA", ft - 2))
                        ins = None
                        for j in range(ND):
                            for ch in range(4):
                                ins = t.matmul(
                                    bank(grp * 4 + ch),
                                    u.wpair[fp % 2][:, j, sub * 128:(sub + 1) * 128],
                                    bigT[:, j, ch * 512:(ch + 1) * 512],
                                    start=(j == 0), stop=(j == ND - 1))
                        M[("p1qpre", ft)] = s_pe.tick(ins)
                        ins = None
                        for ch in range(4):
                            WT(t, ("p1evacC", ft, ch))
                            ins = t.matmul(bank(grp * 4 + ch), pswap[:],
                                           u.qpre[ft % 2][:, ch * 512:(ch + 1) * 512],
                                           start=True, stop=True)
                        M[("p1swap", ft)] = s_pe.tick(ins)
                    M[("p1pe", fp)] = s_pe.cur()
                else:
                    if fp == 20:
                        WT(t, ("p1ropeA", 38))
                    for tt in range(NT):
                        slot = tt % 4
                        if tt >= 4:
                            W(t, ("p1vev", fp, tt - 4))
                        elif fp > 20:
                            W(t, ("p1vev", fp - 1, tt + 12))
                        ins = None
                        for j in range(ND):
                            ins = t.matmul(bank(slot, 0, 256),
                                           bigT[:, j, tt * 128:(tt + 1) * 128],
                                           u.wpair[fp % 2][:, j, :],
                                           start=(j == 0), stop=(j == ND - 1))
                        M[("p1v", fp, tt)] = s_pe.tick(ins)
                    M[("p1pe", fp)] = s_pe.cur()

        def p1_vec(v, u):
            for fp in range(NPAIR):
                if fp < 20:
                    for sub in range(2):
                        ft = 2 * fp + sub
                        grp = ft % 2
                        reg = psA[:, grp * 4 * BPF32:(grp * 4 + 4) * BPF32]
                        WT(v, ("p1qpre", ft))
                        W(v, ("p1ropeB", ft - 2))
                        for ch in range(4):
                            M[("p1evacC", ft, ch)] = s_dve.tick(v.tensor_copy(
                                u.qpre[ft % 2][:, ch * 512:(ch + 1) * 512],
                                psA[:, (grp * 4 + ch) * BPF32:(grp * 4 + ch + 1) * BPF32]))
                        M[("p1evac", ft)] = s_dve.cur()
                        WT(v, ("p1swap", ft))
                        W(v, ("p1spill", ft - 2))
                        M[("p1ropeA", ft)] = s_dve.tick(
                            v.tensor_mul(u.rtA[ft % 2][:], reg, u.sinp[:]))
                        mB = s_dve.tick(v.tensor_mul(u.rtB[ft % 2][:], u.qpre[ft % 2][:], u.cosp[:]))
                        M[("p1ropeB", ft)] = mB
                        WV(v, mB)
                        M[("p1rope", ft)] = s_dve.tick(
                            v.tensor_add(u.rout[ft % 2][:], u.rtA[ft % 2][:], u.rtB[ft % 2][:]))
                else:
                    for tt in range(NT):
                        slot = tt % 4
                        WT(v, ("p1v", fp, tt))
                        if tt >= 4:
                            W(v, ("p1vsp", fp, tt - 4))
                        elif fp > 20:
                            W(v, ("p1vsp", fp - 1, tt + 12))
                        M[("p1vev", fp, tt)] = s_dve.tick(
                            v.tensor_copy(u.vst[slot][:], bank(slot, 0, 256)))

        def p1_scal(sc, u):
            for fp in range(NPAIR):
                if fp < 20:
                    for sub in range(2):
                        ft = 2 * fp + sub
                        WT(sc, ("p1rope", ft))
                        dst = qsp_d[ft] if ft < NH else ksp_d[ft - NH]
                        M[("p1spill", ft)] = sm_rout[ft % 2].tick(
                            sc.dma_start(dst, u.rout[ft % 2][:]), 16)
                else:
                    for tt in range(NT):
                        WT(sc, ("p1vev", fp, tt))
                        M[("p1vsp", fp, tt)] = sm_vst[tt % 4].tick(sc.dma_start(
                            vsp_d[tt][:, (fp - 20) * 256:(fp - 19) * 256],
                            u.vst[tt % 4][:]), 16)
            M["p1_spills"] = ([s.cur() for s in sm_rout] + [s.cur() for s in sm_vst])
            WV(sc, M["p1_spills"])

        phase(p1_alloc, [("sync", p1_sync), ("gpsimd", p1_gp), ("tensor", p1_tens),
                         ("vector", p1_vec), ("scalar", p1_scal)])
        if upto == "p1":
            emit_drain()
            return

        def a_alloc(stL):
            m = mk(stL)
            return SimpleNamespace(
                qh=[m(f"qh{i}", [128, S], BF16) for i in range(2)],
                kg=[m(f"kg{i}", [128, S], BF16) for i in range(2)],
                vg=[m(f"vg{i}", [128, NT, 128], BF16) for i in range(2)],
                expS=[m(f"expS{i}", [128, NT, 512], BF16) for i in range(2)],
                btmp=[m(f"btmp{i}", [128, 512], F32) for i in range(2)],
                rcp=[m(f"rcp{i}", [1, 512], F32) for i in range(2)],
                mtmp=[m(f"mtmp{i}", [128, 128], BF16) for i in range(4)],
            )

        def tiles_of(c):
            return list(range(4 * c + 4))

        def a_sync(sy, u):
            for h in range(NH):
                WT(sy, ("p1spill", h))
                W(sy, ("a_hdone", h - 2))
                M[("a_ql", h)] = sm_qh[h % 2].tick(
                    sy.dma_start(u.qh[h % 2][:], qsp_d[h]), 16)

        def a_gp_ld(g_, u):
            for g in range(NKV):
                WT(g_, ("p1spill", NH + g))
                fp = 20 + g // 2
                for tt in range(NT):
                    WT(g_, ("p1vsp", fp, tt))
                W(g_, ("a_gdone", g - 2))
                mv = [sm_kg[g % 2].tick(g_.dma_start(u.kg[g % 2][:], ksp_d[g]), 16)]
                for half in range(2):
                    mv.append(sm_vg[(g % 2) * 2 + half].tick(g_.dma_start(
                        u.vg[g % 2][:, 8 * half:8 * half + 8, :],
                        vsp_d[8 * half:8 * half + 8, :, g * 128:(g + 1) * 128]
                        .rearrange("t p c -> p t c")), 16))
                M[("a_kvl", g)] = mv

        gsidx = [0]

        def em_scores(t, u, h, c):
            g = h // 4
            for k in tiles_of(c):
                d = k - 4 * c
                n0 = 128 * max(d, 0)
                gi = gsidx[0]
                sbank = 4 + (gi % 2)
                W(t, ("a_edone_gi", gi - 2))
                ins = t.matmul(bank(sbank, n0, 512),
                               u.kg[g % 2][:, k * 128:(k + 1) * 128],
                               u.qh[h % 2][:, c * 512 + n0:(c + 1) * 512],
                               start=True, stop=True)
                M[("a_s", h, c, k)] = s_pe.tick(ins)
                M[("a_gi_of", h, c, k)] = gi
                gsidx[0] += 1

        def em_pv(t, u, h, c):
            g = h // 4
            cb = c % 2
            ks = tiles_of(c)
            gc = 4 * h + c
            ins = None
            for k in ks:
                d = k - 4 * c
                n0 = 128 * max(d, 0)
                WT(t, ("a_e", h, c, k))
                if d >= 0:
                    WT(t, ("a_mk", h, c, k))
                if k == 0:
                    W(t, ("a_onG", gc - 2))
                ins = t.matmul(bank(cb, n0, 512),
                               u.vg[g % 2][:, k, :],
                               u.expS[cb][:, k, n0:512],
                               start=(k == 0), stop=(k == ks[-1]))
            M[("a_pv", h, c)] = s_pe.tick(ins)
            ins = None
            for k in ks:
                d = k - 4 * c
                n0 = 128 * max(d, 0)
                if k == 0:
                    W(t, ("a_rcpG", gc - 2))
                ins = t.matmul(bank(2 + cb, n0, 512, p0=0, p1=1),
                               onesc[:], u.expS[cb][:, k, n0:512],
                               start=(k == 0), stop=(k == ks[-1]))
            M[("a_dnG", gc)] = s_pe.tick(ins)

        def em_bc(t, u, h, c):
            cb = c % 2
            gc = 4 * h + c
            WT(t, ("a_rcpG", gc))
            W(t, ("a_onG", gc - 2))
            M[("a_bcG", gc)] = s_pe.tick(t.matmul(
                bank(6 + cb), onesr32[:], u.rcp[cb][0:1, :], start=True, stop=True))

        def a_tens(t, u):
            for h in range(NH):
                g = h // 4
                WT(t, ("a_ql", h))
                if h % 4 == 0:
                    WT(t, ("a_kvl", g))
                if h > 0:
                    em_bc(t, u, h - 1, 2)
                em_scores(t, u, h, 0)
                em_scores(t, u, h, 1)
                if h > 0:
                    em_bc(t, u, h - 1, 3)
                em_pv(t, u, h, 0)
                em_scores(t, u, h, 2)
                em_bc(t, u, h, 0)
                em_pv(t, u, h, 1)
                em_scores(t, u, h, 3)
                em_bc(t, u, h, 1)
                em_pv(t, u, h, 2)
                em_pv(t, u, h, 3)
                M[("a_hdone", h)] = s_pe.cur()
                if h % 4 == 3:
                    M[("a_gdone", g)] = s_pe.cur()
            em_bc(t, u, NH - 1, 2)
            em_bc(t, u, NH - 1, 3)

        def a_act(sc, u):
            for h in range(NH):
                for c in range(4):
                    cb = c % 2
                    gc = 4 * h + c
                    W(sc, ("a_dnG", gc - 2))
                    for k in tiles_of(c):
                        d = k - 4 * c
                        n0 = 128 * max(d, 0)
                        gi = M[("a_gi_of", h, c, k)]
                        sbank = 4 + ((gi or 0) % 2)
                        WT(sc, ("a_s", h, c, k))
                        if d >= 0:
                            if gc > 0:
                                ph, pc = divmod(gc - 1, 4)
                                W(sc, ("a_mk", ph, pc, 4 * pc + d))
                            M[("a_em", h, c, k)] = s_act.tick(sc.activation(
                                u.mtmp[d][:], bank(sbank, n0, n0 + 128),
                                AF.Exp, bias=cst(C_SHIFT), scale=cst(C_SMS)))
                            if n0 + 128 < 512:
                                M[("a_e", h, c, k)] = s_act.tick(sc.activation(
                                    u.expS[cb][:, k, n0 + 128:512],
                                    bank(sbank, n0 + 128, 512),
                                    AF.Exp, bias=cst(C_SHIFT), scale=cst(C_SMS)))
                            else:
                                M[("a_e", h, c, k)] = s_act.cur()
                        else:
                            M[("a_e", h, c, k)] = s_act.tick(sc.activation(
                                u.expS[cb][:, k, :], bank(sbank),
                                AF.Exp, bias=cst(C_SHIFT), scale=cst(C_SMS)))
                        M[("a_edone_gi", M[("a_gi_of", h, c, k)])] = s_act.cur()

        def a_vec(v, u):
            def masks(h, c):
                cb = c % 2
                for k in tiles_of(c):
                    d = k - 4 * c
                    if d >= 0:
                        n0 = 128 * d
                        WT(v, ("a_em", h, c, k))
                        M[("a_mk", h, c, k)] = s_dve.tick(v.tensor_mul(
                            u.expS[cb][:, k, n0:n0 + 128], u.mtmp[d][:], maskb[:]))

            def norm(h, c):
                cb = c % 2
                gc = 4 * h + c
                WT(v, ("a_dnG", gc))
                W(v, ("a_bcG", gc - 2))
                M[("a_rcpG", gc)] = s_dve.tick(
                    v.reciprocal(u.rcp[cb][0:1, :], bank(2 + cb, p0=0, p1=1)))
                WT(v, ("a_bcG", gc))
                W(v, ("a_onG", gc - 2))
                mcp = s_dve.tick(v.tensor_copy(u.btmp[cb][:], bank(6 + cb)))
                WV(v, mcp)
                M[("a_onG", gc)] = s_dve.tick(v.tensor_mul(
                    bigT[:, h, c * 512:(c + 1) * 512], bank(cb), u.btmp[cb][:]))

            for h in range(NH):
                masks(h, 0)
                if h > 0:
                    norm(h - 1, 3)
                masks(h, 1)
                norm(h, 0)
                masks(h, 2)
                norm(h, 1)
                masks(h, 3)
                norm(h, 2)
            norm(NH - 1, 3)

        phase(a_alloc, [("sync", a_sync), ("gpsimd", a_gp_ld),
                        ("tensor", a_tens), ("scalar", a_act), ("vector", a_vec)])
        if upto == "p2":
            emit_tap()
            emit_drain()
            return

        NDS = 16

        def wo_alloc(stL):
            m = mk(stL)
            return SimpleNamespace(
                wos=[m(f"wos{i}", [128, ND, 256], BF16) for i in range(2)],
                xst=[m(f"xst{i}", [128, 256], F32) for i in range(4)],
                x1st=[m(f"x1st{i}", [128, 256], F32) for i in range(2)],
            )

        def wo_gp(g, u):
            for ds in range(NDS):
                W(g, ("wo_pe", ds - 2))
                M[("wo_w", ds)] = [
                    sm_wos[(ds % 2) * 4 + jg].tick(g.dma_start(
                        u.wos[ds % 2][:, 8 * jg:8 * jg + 8, :],
                        wo3[:, 8 * jg:8 * jg + 8, 256 * ds:256 * (ds + 1)]), 16)
                    for jg in range(4)]

        def wo_sync(sy, u):
            for ds in range(NDS):
                for tt in range(NT):
                    i = ds * NT + tt
                    W(sy, ("wo_add", i - 4))
                    M[("wo_x", i)] = sm_xst[i % 4].tick(sy.dma_start(
                        u.xst[i % 4][:],
                        x_d[tt * 128:(tt + 1) * 128, 256 * ds:256 * (ds + 1)]), 16)

        def wo_tens(t, u):
            for ds in range(NDS):
                WT(t, ("wo_w", ds))
                for tt in range(NT):
                    i = ds * NT + tt
                    bk = tt % 2
                    W(t, ("wo_add", i - 2))
                    ins = None
                    for hh in range(NH):
                        ins = t.matmul(bank(bk, 0, 256),
                                       bigT[:, hh, tt * 128:(tt + 1) * 128],
                                       u.wos[ds % 2][:, hh, :],
                                       start=(hh == 0), stop=(hh == NH - 1))
                    M[("wo_mm", i)] = s_pe.tick(ins)
                M[("wo_pe", ds)] = s_pe.cur()

        def wo_vec(v, u):
            for ds in range(NDS):
                for tt in range(NT):
                    i = ds * NT + tt
                    bk = tt % 2
                    WT(v, ("wo_mm", i))
                    WT(v, ("wo_x", i))
                    W(v, ("wo_sp", i - 2))
                    M[("wo_add", i)] = s_dve.tick(v.tensor_add(
                        u.x1st[tt % 2][:], bank(bk, 0, 256), u.xst[i % 4][:]))

        def wo_scal(sc, u):
            for ds in range(NDS):
                for tt in range(NT):
                    i = ds * NT + tt
                    WT(sc, ("wo_add", i))
                    M[("wo_sp", i)] = sm_x1w[tt % 2].tick(sc.dma_start(
                        x1sp_d[tt * 128:(tt + 1) * 128, 256 * ds:256 * (ds + 1)],
                        u.x1st[tt % 2][:]), 16)
            M["x1_done"] = [s.cur() for s in sm_x1w]
            WV(sc, M["x1_done"])

        phase(wo_alloc, [("gpsimd", wo_gp), ("sync", wo_sync), ("tensor", wo_tens),
                         ("vector", wo_vec), ("scalar", wo_scal)])
        if upto == "p3a":
            emit_drain()
            return

        def p3b_sy(sy):
            WT(sy, "x1_done")

        phase(ln_alloc("ln2"), ln_bodies("ln2", x1sp_d, pre_sy=p3b_sy))
        if upto == "p3b":
            emit_tap()
            emit_drain()
            return

        NSC = (NIB + 3) // 4

        def p4_alloc(stL):
            m = mk(stL)
            return SimpleNamespace(
                qbg=[m(f"qbg{i}", [128, D], BF16) for i in range(3)],
                qbu=[m(f"qbu{i}", [128, D], BF16) for i in range(3)],
                wgT=m("wgT", [128, ND, 128], BF16),
                wuT=m("wuT", [128, ND, 128], BF16),
                sgb=[m(f"sgb{i}", [128, 4, 32], F32) for i in range(2)],
                zgb=[m(f"zgb{i}", [128, 4, 32], F32) for i in range(2)],
                sub=[m(f"sub{i}", [128, 4, 32], F32) for i in range(2)],
                zub=[m(f"zub{i}", [128, 4, 32], F32) for i in range(2)],
                nzg=[m(f"nzg{i}", [128, 4, 32], F32) for i in range(2)],
                nzu=[m(f"nzu{i}", [128, 4, 32], F32) for i in range(2)],
                silu=[m(f"silu{i}", [128, 512], BF16) for i in range(2)],
                hac=[m(f"hac{i}", [128, 1024], BF16) for i in range(2)],
            )

        def p4_sync(sy, u):
            def scl_load(sy, ac):
                if ac >= 2:
                    W(sy, ("p4dqa", 4 * (ac - 1) - 1))
                a0 = 4 * ac
                an = min(4, NIB - a0)
                M[("p4scl", ac)] = [
                    sm_scl[ai * 2 + ac % 2].tick(sy.dma_start(
                        dst[ac % 2][:, 0:an, :], src[:, a0:a0 + an, :]), 16)
                    for ai, (src, dst) in enumerate(
                        ((sg3, u.sgb), (zg3, u.zgb), (su3, u.sub), (zu3, u.zub)))]

            for a in range(NIB):
                if a == 0:
                    scl_load(sy, 0)
                    scl_load(sy, 1)
                elif (a + 2) % 4 == 0 and 2 <= (a + 2) // 4 < (NIB + 3) // 4:
                    scl_load(sy, (a + 2) // 4)
                for hf in range(2):
                    WT(sy, ("p4hh", a, hf))
                    M[("p4hspH", a, hf)] = sm_hw[(a % 2) * 2 + hf].tick(
                        sy.dma_start(hsp_d[a][:, hf * 1024:(hf + 1) * 1024],
                                     u.hac[a % 2][:]), 16)
            M["p4_spills"] = [s.cur() for s in sm_hw]
            WV(sy, M["p4_spills"])

        def p4_gp(g, u):
            for a in range(NIB):
                W(g, ("p4T", a - 3))
                M[("p4w", a)] = [
                    sm_qgu[0 + a % 3].tick(g.dma_start(
                        u.qbg[a % 3][:], qg_d[a * 128:(a + 1) * 128, :]), 16),
                    sm_qgu[3 + a % 3].tick(g.dma_start(
                        u.qbu[a % 3][:], qu_d[a * 128:(a + 1) * 128, :]), 16)]

        def p4_nz(v, u, a):
            ac = a // 4
            if a % 4 == 0:
                WT(v, ("p4scl", ac))
                if ac >= 2:
                    W(v, ("p4dqa", 4 * (ac - 1) - 1))
                s_dve.tick(v.scalar_tensor_tensor(
                    u.nzg[ac % 2][:], u.zgb[ac % 2][:], -1.0, u.sgb[ac % 2][:],
                    ALU.mult, ALU.mult))
                mz = s_dve.tick(v.scalar_tensor_tensor(
                    u.nzu[ac % 2][:], u.zub[ac % 2][:], -1.0, u.sub[ac % 2][:],
                    ALU.mult, ALU.mult))
                WV(v, mz)
                M[("p4nz", ac)] = mz

        def p4_dqa(sc, u, a, part):
            ac = a // 4
            js = list(range(ND))
            quarter = js[part * len(js) // 4:(part + 1) * len(js) // 4]
            if part == 0:
                WT(sc, ("p4w", a))
                WT(sc, ("p4nz", ac))
                W(sc, ("p4dqa", a - 1))
            ins = None
            for qb, sb, nzb in ((u.qbg, u.sgb, u.nzg), (u.qbu, u.sub, u.nzu)):
                for j in quarter:
                    ins = sc.activation(
                        qb[a % 3][:, j * 128:(j + 1) * 128],
                        qb[a % 3][:, j * 128:(j + 1) * 128],
                        AF.Identity, bias=nzb[ac % 2][:, a % 4, j:j + 1],
                        scale=sb[ac % 2][:, a % 4, j:j + 1])
            if part == 3:
                M[("p4dqa", a)] = s_act.tick(ins)
            else:
                s_act.tick(ins)

        def p4_vec(v, u):
            p4_nz(v, u, 0)
            for a in range(NIB):
                for tv in range(8):
                    WT(v, ("p4tp", a, 2 * tv))
                    WT(v, ("p4tp", a, 2 * tv + 1))
                    dstT = u.wgT if tv < 4 else u.wuT
                    j0 = (tv % 4) * 8
                    pb = 4 + 2 * (tv % 2)
                    src = psA[:, pb * BPF32:(pb + 2) * BPF32]
                    M[("p4tev2", a, tv)] = s_dve.tick(v.tensor_copy(
                        dstT[:, j0:j0 + 8, :], src.rearrange("p (q c) -> p q c", q=8)))
                M[("p4tevall", a)] = s_dve.cur()
                if a + 2 < NIB:
                    p4_nz(v, u, a + 2)
                for c in range(4):
                    WT(v, ("p4silu", a, c))
                    WT(v, ("p4u", a, c))
                    if c == 0:
                        W(v, ("p4hspH", a - 2, 1))
                    elif c == 2:
                        WT(v, ("p4hspH", a, 0))
                    M[("p4mul", a, c)] = s_dve.tick(v.tensor_mul(
                        u.hac[a % 2][:, (c % 2) * 512:(c % 2 + 1) * 512],
                        bank(2 + c % 2), u.silu[c % 2][:]))
                    if c % 2 == 1:
                        M[("p4hh", a, c // 2)] = s_dve.cur()

        def p4_act(sc, u):
            for part in range(4):
                p4_dqa(sc, u, 0, part)
            for part in range(4):
                p4_dqa(sc, u, 1, part)
            for a in range(NIB):
                for c in range(4):
                    if a + 2 < NIB:
                        p4_dqa(sc, u, a + 2, c)
                    WT(sc, ("p4g", a, c))
                    if c >= 2:
                        W(sc, ("p4mul", a, c - 2))
                    elif a > 0:
                        W(sc, ("p4mul", a - 1, c + 2))
                    M[("p4silu", a, c)] = s_act.tick(sc.activation(
                        u.silu[c % 2][:], bank(c % 2), AF.Silu,
                        bias=cst(C_ZERO), scale=cst(C_ONE)))

        def p4_tens(t, u):
            for a in range(NIB):
                WT(t, ("p4dqa", a))
                for gb in range(16):
                    if gb >= 4:
                        W(t, ("p4tev2", a, (gb - 4) // 2))
                    else:
                        W(t, ("p4tev2", a - 1, (gb + 12) // 2))
                    src = u.qbg if gb < 8 else u.qbu
                    j0 = (gb % 8) * 4
                    pb = 4 + 2 * ((gb // 2) % 2)
                    ins = None
                    for q4 in range(4):
                        j = j0 + q4
                        ins = t.matmul(bank(pb + gb % 2, q4 * 128, (q4 + 1) * 128),
                                       src[a % 3][:, j * 128:(j + 1) * 128],
                                       ident[:], start=True, stop=True)
                    M[("p4tp", a, gb)] = s_pe.tick(ins)
                M[("p4T", a)] = s_pe.cur()
                WT(t, ("p4tevall", a))
                for c in range(4):
                    if c >= 2:
                        W(t, ("p4silu", a, c - 2))
                    elif a > 0:
                        W(t, ("p4silu", a - 1, c + 2))
                    ins = None
                    for j in range(ND):
                        ins = t.matmul(bank(c % 2), u.wgT[:, j, :],
                                       bigT[:, j, c * 512:(c + 1) * 512],
                                       start=(j == 0), stop=(j == ND - 1))
                    M[("p4g", a, c)] = s_pe.tick(ins)
                    if c >= 2:
                        W(t, ("p4mul", a, c - 2))
                    elif a > 0:
                        W(t, ("p4mul", a - 1, c + 2))
                    ins = None
                    for j in range(ND):
                        ins = t.matmul(bank(2 + c % 2), u.wuT[:, j, :],
                                       bigT[:, j, c * 512:(c + 1) * 512],
                                       start=(j == 0), stop=(j == ND - 1))
                    M[("p4u", a, c)] = s_pe.tick(ins)

        phase(p4_alloc, [("sync", p4_sync), ("gpsimd", p4_gp), ("vector", p4_vec),
                         ("scalar", p4_act), ("tensor", p4_tens)])
        if upto == "p4":
            emit_drain()
            return

        ctx["close_big"]()

        NS = len(SUPS)
        SL = [(q4, s) for q4 in range(4) for s in range(NS)]
        NGB = [(ss + 3) // 4 for ss in SUPS]

        def p5_li(idx, hh):
            return idx * 8 + hh

        def p5_gidx(idx, hh, gb):
            q4, s = SL[idx]
            per = [8 * NGB[SL[i][1]] for i in range(len(SL))]
            return sum(per[:idx]) + hh * NGB[s] + gb

        _gcum = [0]
        for _i in range(len(SL)):
            _gcum.append(_gcum[-1] + 8 * NGB[SL[_i][1]])

        def p5_gidx(idx, hh, gb):
            return _gcum[idx] + hh * NGB[SL[idx][1]] + gb

        def p5_lin(idx, tt, ch):
            return (idx * NT + tt) * 2 + ch

        def p5_alloc(stL):
            m = mk(stL)
            return SimpleNamespace(
                y2=m("y2", [128, NT * 1024], F32),
                wdT=[m(f"wdT{i}", [128, 8, 1024], BF16) for i in range(2)],
                hbuf=[m(f"hbuf{i}", [128, 8, 512], BF16) for i in range(2)],
                qdb=[m(f"qdb{i}", [128, 1024], BF16) for i in range(8)],
                sdb=m("sdb", [128, 32, NIB], F32),
                zdb=m("zdb", [128, 32, NIB], F32),
                nzd=[m(f"nzd{i}", [128, 8, 8], F32) for i in range(2)],
                outst=[m(f"outst{i}", [128, 1024], F32) for i in range(4)],
                x1b=[m(f"x1b{i}", [128, 1024], F32) for i in range(8)],
            )

        def p5_sync(sy, u):
            WT(sy, "x1_done")
            WT(sy, "p4_spills")
            for sl in range(4):
                M[("p5sdzS", sl)] = sm_sdz[sl].tick(sy.dma_start(
                    u.sdb[:, 8 * sl:8 * sl + 8, :],
                    sd3[:, 8 * sl:8 * sl + 8, :]), 16)
                M[("p5sdzS", 4 + sl)] = sm_sdz[4 + sl].tick(sy.dma_start(
                    u.zdb[:, 8 * sl:8 * sl + 8, :],
                    zd3[:, 8 * sl:8 * sl + 8, :]), 16)
            for idx, (q4, s) in enumerate(SL):
                ss = SUPS[s]
                for tg in range(4):
                    li4 = idx * 4 + tg
                    W(sy, ("p5mmG", li4 - 2))
                    nh = (ss + 1) // 2
                    M[("p5h", idx, tg)] = [
                        sm_hb[(tg % 2) * 2 + 0].tick(sy.dma_start(
                            u.hbuf[tg % 2][:, 0:nh, :],
                            hsp_d[8 * s:8 * s + nh, :, tg * 512:(tg + 1) * 512]
                            .rearrange("a p c -> p a c")), 16),
                        sm_hb[(tg % 2) * 2 + 1].tick(sy.dma_start(
                            u.hbuf[tg % 2][:, nh:ss, :],
                            hsp_d[8 * s + nh:8 * s + ss, :, tg * 512:(tg + 1) * 512]
                            .rearrange("a p c -> p a c")), 16)]
                    if idx == 0 and tg < 2:
                        for tt in (4 * tg, 4 * tg + 1, 4 * tg + 2, 4 * tg + 3):
                            M[("p5x1", 0, tt)] = sm_x1b[tt % 8].tick(sy.dma_start(
                                u.x1b[tt % 8][:],
                                x1sp_d[tt * 128:(tt + 1) * 128, 0:1024]), 16)

        def p5_gp(g, u):
            for idx, (q4, s) in enumerate(SL):
                ss = SUPS[s]
                cw = ss * 128
                for hh in range(8):
                    li = p5_li(idx, hh)
                    W(g, ("p5tpA", li - 8))
                    r0 = q4 * 1024 + hh * 128
                    M[("p5w", li)] = sm_qdb[hh].tick(g.dma_start(
                        u.qdb[hh][:, 0:cw],
                        qd_d[r0:r0 + 128, 1024 * s:1024 * s + cw]), 16)

        def p5_dq(v, u, idx):
            q4, s = SL[idx]
            ss = SUPS[s]
            WT(v, ("p5sdzS", q4))
            WT(v, ("p5sdzS", 4 + q4))
            if idx >= 2:
                W(v, ("p5dqaA", idx - 2))
            mz = s_dve.tick(v.scalar_tensor_tensor(
                u.nzd[idx % 2][:, :, 0:ss],
                u.zdb[:, q4 * 8:q4 * 8 + 8, 8 * s:8 * s + ss], -1.0,
                u.sdb[:, q4 * 8:q4 * 8 + 8, 8 * s:8 * s + ss],
                ALU.mult, ALU.mult))
            WV(v, mz)
            M[("p5nz", idx)] = mz

        def p5_dqa(sc, u, idx):
            q4, s = SL[idx]
            ss = SUPS[s]
            WT(sc, ("p5nz", idx))
            for hh in range(8):
                li = p5_li(idx, hh)
                WT(sc, ("p5w", li))
                ai = q4 * 8 + hh
                ins = None
                for gg in range(ss):
                    ins = sc.activation(
                        u.qdb[hh][:, gg * 128:(gg + 1) * 128],
                        u.qdb[hh][:, gg * 128:(gg + 1) * 128],
                        AF.Identity,
                        bias=u.nzd[idx % 2][:, hh, gg:gg + 1],
                        scale=u.sdb[:, ai, 8 * s + gg:8 * s + gg + 1])
                M[("p5dqa", li)] = s_act.tick(ins)
            M[("p5dqaA", idx)] = s_act.cur()

        def p5_tev(v, u, idx):
            q4, s = SL[idx]
            ss = SUPS[s]
            for hh in range(8):
                for gb in range(NGB[s]):
                    WT(v, ("p5tp", idx, hh, gb))
                pb = 4 + 2 * (hh % 2)
                M[("p5tevHL", idx * 8 + hh)] = s_dve.tick(v.tensor_copy(
                    u.wdT[idx % 2][:, 0:ss, hh * 128:(hh + 1) * 128],
                    psA[:, pb * BPF32:pb * BPF32 + ss * 128]
                    .rearrange("p (q c) -> p q c", q=ss)))
            M[("p5tev", idx)] = s_dve.cur()

        def p5_vec(v, u):
            p5_dq(v, u, 0)
            p5_tev(v, u, 0)
            p5_dq(v, u, 1)
            for idx, (q4, s) in enumerate(SL):
                for tg in range(4):
                    for ti in range(4):
                        tt = 4 * tg + ti
                        pb = 2 * (tt % 2)
                        WT(v, ("p5mm", idx, tt, 0))
                        WT(v, ("p5mm", idx, tt, 1))
                        if s > 0:
                            W(v, ("p5y2T", idx - 1, tt))
                        elif q4 > 0:
                            W(v, ("p5fin", q4 - 1, tt))
                        dst = u.y2[:, tt * 1024:(tt + 1) * 1024]
                        src_ = psA[:, pb * BPF32:(pb + 2) * BPF32]
                        if s == 0:
                            ins = v.tensor_copy(dst, src_)
                        else:
                            ins = v.tensor_add(dst, src_, dst)
                        my = s_dve.tick(ins)
                        M[("p5y2T", idx, tt)] = my
                        M[("p5y2L", idx * NT + tt)] = my
                        if s == NS - 1:
                            WV(v, my)
                            WT(v, ("p5x1", q4, tt))
                            W(v, ("p5out", q4, tt - 4))
                            if tt < 4 and q4 > 0:
                                W(v, ("p5out", q4 - 1, tt + 12))
                            M[("p5fin", q4, tt)] = s_dve.tick(v.tensor_add(
                                u.outst[tt % 4][:], dst, u.x1b[tt % 8][:]))
                if idx + 1 < len(SL):
                    p5_tev(v, u, idx + 1)
                if idx + 2 < len(SL):
                    p5_dq(v, u, idx + 2)

        def p5_act(sc, u):
            p5_dqa(sc, u, 0)
            p5_dqa(sc, u, 1)
            for idx, (q4, s) in enumerate(SL):
                if s == NS - 1:
                    for tt in range(NT):
                        WT(sc, ("p5fin", q4, tt))
                        M[("p5out", q4, tt)] = sm_out[tt % 4].tick(sc.dma_start(
                            y_d[tt * 128:(tt + 1) * 128, q4 * 1024:(q4 + 1) * 1024],
                            u.outst[tt % 4][:]), 16)
                        nq, ntt = (q4, tt + 8) if tt < 8 else (q4 + 1, tt - 8)
                        if nq < 4:
                            M[("p5x1", nq, ntt)] = sm_x1b[ntt % 8].tick(
                                sc.dma_start(
                                    u.x1b[ntt % 8][:],
                                    x1sp_d[ntt * 128:(ntt + 1) * 128,
                                           nq * 1024:(nq + 1) * 1024]), 16)
                if idx + 2 < len(SL):
                    p5_dqa(sc, u, idx + 2)
            WV(sc, [s.cur() for s in sm_out])

        def p5_T(t, u, idx):
            q4, s = SL[idx]
            ss = SUPS[s]
            for hh in range(8):
                li = p5_li(idx, hh)
                WT(t, ("p5dqa", li))
                pb = 4 + 2 * (hh % 2)
                W(t, ("p5tevHL", idx * 8 + hh - 2))
                for gb in range(NGB[s]):
                    it0 = 4 * gb
                    itn = min(4, ss - it0)
                    ins = None
                    for q in range(itn):
                        it = it0 + q
                        ins = t.matmul(
                            bank(pb + gb, q * 128, (q + 1) * 128),
                            u.qdb[hh][:, it * 128:(it + 1) * 128],
                            ident[:], start=True, stop=True)
                    M[("p5tp", idx, hh, gb)] = s_pe.tick(ins)
                M[("p5tpA", li)] = s_pe.cur()

        def p5_tens(t, u):
            p5_T(t, u, 0)
            for idx, (q4, s) in enumerate(SL):
                ss = SUPS[s]
                WT(t, ("p5tev", idx))
                for tg in range(4):
                    li4 = idx * 4 + tg
                    WT(t, ("p5h", idx, tg))
                    for ti in range(4):
                        tt = 4 * tg + ti
                        for ch in range(2):
                            bk = 2 * (tt % 2) + ch
                            if ch == 0:
                                W(t, ("p5y2L", idx * NT + tt - 2))
                            ins = None
                            for it in range(ss):
                                ins = t.matmul(
                                    bank(bk),
                                    u.hbuf[tg % 2][:, it, ti * 128:(ti + 1) * 128],
                                    u.wdT[idx % 2][:, it, ch * 512:(ch + 1) * 512],
                                    start=(it == 0), stop=(it == ss - 1))
                            M[("p5mm", idx, tt, ch)] = s_pe.tick(ins)
                    M[("p5mmG", li4)] = s_pe.cur()
                if idx + 1 < len(SL):
                    p5_T(t, u, idx + 1)

        phase(p5_alloc, [("sync", p5_sync), ("gpsimd", p5_gp), ("vector", p5_vec),
                         ("scalar", p5_act), ("tensor", p5_tens)])
        emit_drain()

    ctx = {"bigT": _Dummy(), "close_big": lambda: None}
    M.lenient = True
    phases(True, ctx)
    M.lenient = False

    stBig = ExitStack()
    bigT_real = stBig.enter_context(nc.sbuf_tensor("bigT", [128, ND, S], BF16))
    ctx = {"bigT": bigT_real, "close_big": stBig.close}
    phases(False, ctx)
    try:
        stBig.close()
    except Exception:
        pass
    st0.close()
    return nc


def _tables():
    inv = 10000.0 ** (-(np.arange(64, dtype=np.float64) * 2.0 / 128.0))
    ang = np.arange(S, dtype=np.float64)[None, :] * inv[:, None]
    cosp = np.repeat(np.cos(ang), 2, axis=0)
    sinp2 = np.empty((128, S), dtype=np.float64)
    sinp2[0::2] = -np.sin(ang)
    sinp2[1::2] = np.sin(ang)
    pswap = np.zeros((128, 128), dtype=np.float32)
    for k in range(64):
        pswap[2 * k, 2 * k + 1] = 1.0
        pswap[2 * k + 1, 2 * k] = 1.0
    ident = np.eye(128, dtype=np.float32)
    mask = np.triu(np.ones((128, 128), dtype=np.float32))
    bf = ml_dtypes.bfloat16
    return (cosp.astype(bf), sinp2.astype(bf), pswap.astype(bf),
            ident.astype(bf), mask.astype(bf))


def make_inmaps(inputs):
    x = np.asarray(inputs["L0_attn_x"])
    cosp, sinp, pswap, ident, maskb = _tables()
    shared = {
        "wqkv": np.ascontiguousarray(np.asarray(inputs["L0_attn_w_qkv"])),
        "wo": np.ascontiguousarray(np.asarray(inputs["L0_attn_w_o"])),
        "qg": np.ascontiguousarray(np.asarray(inputs["L0_ffn_qw_gate"])),
        "qu": np.ascontiguousarray(np.asarray(inputs["L0_ffn_qw_up"])),
        "qd": np.ascontiguousarray(np.asarray(inputs["L0_ffn_qw_down"])),
        "sg": np.ascontiguousarray(np.asarray(inputs["L0_ffn_s_gate"])),
        "zg": np.ascontiguousarray(np.asarray(inputs["L0_ffn_z_gate"])),
        "su": np.ascontiguousarray(np.asarray(inputs["L0_ffn_s_up"])),
        "zu": np.ascontiguousarray(np.asarray(inputs["L0_ffn_z_up"])),
        "sd": np.ascontiguousarray(np.asarray(inputs["L0_ffn_s_down"])),
        "zd": np.ascontiguousarray(np.asarray(inputs["L0_ffn_z_down"])),
        "cosp": cosp, "sinp": sinp, "pswap": pswap,
        "ident": ident, "maskb": maskb,
    }
    return [dict(shared, x=np.ascontiguousarray(x[c])) for c in range(8)]


def kernel(**inputs):
    nc = build_nc("full")
    in_maps = make_inmaps(inputs)
    res = run_bass_kernel_spmd(nc, in_maps, core_ids=list(range(8)))
    return np.stack([np.asarray(res.results[c]["y"]) for c in range(8)], axis=0)
